# version 22 (final)
# baseline (speedup 1.0000x reference)
import os as _os
import numpy as np
import concourse.bass as bass
import concourse.mybir as mybir
from concourse.bass_utils import run_bass_kernel_spmd
from concourse.bass_core_rust import IndirectOffsetOnAxis

F32 = mybir.dt.float32
I32 = mybir.dt.int32
BF16 = mybir.dt.bfloat16
AF = mybir.ActivationFunctionType
ALU = mybir.AluOpType
AX = mybir.AxisListType

ENGINES = ("tensor", "vector", "scalar", "gpsimd", "sync")
NB = 8
NH = 16
NE = 8
NT = 11
BIG = 1.0e7


class Sem:
    __slots__ = ("h", "v")

    def __init__(self, nc, name):
        self.h = nc.alloc_semaphore(name)
        self.v = 0


class Bank:
    def __init__(self, nc, name, ap):
        self.ap = ap
        self.pe = Sem(nc, name + "p")
        self.rd = Sem(nc, name + "r")


class Prog:
    def __init__(self, nc):
        self.nc = nc
        self.ops = {e: [] for e in ENGINES}

    def op(self, eng, fn, *incs):
        for s, n in incs:
            s.v += n
        pairs = tuple((s.h, n) for s, n in incs)

        def g(E, fn=fn, pairs=pairs):
            ins = fn(E)
            if pairs:
                ins.then_inc(*pairs[0])
                for h, n in pairs[1:]:
                    E.nop().then_inc(h, n)
        self.ops[eng].append(g)

    def wait(self, eng, sem, val=None):
        v = sem.v if val is None else val
        self.ops[eng].append(lambda E, s=sem.h, v=v: E.wait_ge(s, v))


class Stream:

    def __init__(self, nc, p, name, queue, bufs):
        self.p = p
        self.queue = queue
        self.bufs = bufs
        self.slots = [Sem(nc, f"{name}_s{i}") for i in range(len(bufs))]
        self.free = Sem(nc, f"{name}_f")
        self.rounds = 0

    def load(self, dma_fns, gates=()):
        d = len(self.bufs)
        slot = self.rounds % d
        for g in gates:
            self.p.wait(self.queue, g[0], g[1])
        if self.rounds >= d:
            self.p.wait(self.queue, self.free, self.rounds - d + 1)
        s = self.slots[slot]
        for fn in dma_fns:
            self.p.op(self.queue, (lambda E, f=fn, b=self.bufs[slot]: f(E, b)),
                      (s, 16))
        self.rounds += 1
        return slot, s.v

    def buf(self, slot):
        return self.bufs[slot]


def build_nc(taps=()):
    nc = bass.Bass("TRN2", target_bir_lowering=False,
                   detect_race_conditions=False)
    p = Prog(nc)

    d_x = nc.dram_tensor("xs", [1024, 2048], F32, kind="ExternalInput").ap()
    d_wqkv = nc.dram_tensor("wqkv_s", [48, 128, 16, 128], F32, kind="ExternalInput").ap()
    d_bqkv = nc.dram_tensor("bqkv_c", [128, 48], F32, kind="ExternalInput").ap()
    d_wout = nc.dram_tensor("wout", [2048, 2048], F32, kind="ExternalInput").ap()
    d_boutb = nc.dram_tensor("bout_bc", [128, 2048], F32, kind="ExternalInput").ap()
    d_slops = nc.dram_tensor("slopes_bc", [128, 16], F32, kind="ExternalInput").ap()
    d_wgt = nc.dram_tensor("wgt", [2048, 8], F32, kind="ExternalInput").ap()
    d_w12 = nc.dram_tensor("w12s", [8, 11, 128, 2, 16, 128], F32,
                           kind="ExternalInput").ap()
    d_w3 = nc.dram_tensor("w3t", [8, 4, 128, 11, 512], F32, kind="ExternalInput").ap()
    d_diff = nc.dram_tensor("diffx", [128, 128], F32, kind="ExternalInput").ap()
    d_iota = nc.dram_tensor("iota_bc", [128, 128], F32, kind="ExternalInput").ap()
    d_cols = nc.dram_tensor("cols", [128, 2], F32, kind="ExternalInput").ap()
    d_tmet = nc.dram_tensor("neg_t_meta", [128, 4], F32, kind="ExternalInput").ap()
    d_ident = nc.dram_tensor("ident", [128, 128], F32, kind="ExternalInput").ap()
    d_blkbc = nc.dram_tensor("blk_bc", [128, 8], F32, kind="ExternalInput").ap()
    d_iota8 = nc.dram_tensor("iota8_bc", [128, 8], F32, kind="ExternalInput").ap()

    d_out = nc.dram_tensor("out", [1024, 2048], F32, kind="ExternalOutput").ap()

    d_astage = nc.dram_tensor("a_stage", [NH * 128, 128], F32, kind="Internal")
    d_aall1 = nc.dram_tensor("a_all1", [4 * 8 * 128, 128], F32, kind="Internal")
    d_aall2 = nc.dram_tensor("a_all2", [4 * 8 * 128, 128], F32, kind="Internal")
    d_qspill = nc.dram_tensor("q_spill", [NH, 128, 1024], BF16, kind="Internal").ap()
    d_xnsp = nc.dram_tensor("xn_sp", [1024, 2048], BF16, kind="Internal").ap()
    d_xperm = nc.dram_tensor("xn_perm", [3073, 2176], BF16, kind="Internal").ap()
    d_yperm = nc.dram_tensor("y_perm", [3073, 2048], F32, kind="Internal").ap()

    BASE = 16512
    LIMIT = 229344

    def sb(name, shape, dtype, rel):
        nbytes = int(np.prod(shape[1:])) * mybir.dt.size(dtype)
        assert BASE + rel + nbytes <= LIMIT, (name, rel, nbytes)
        return nc.alloc_sbuf_tensor_at(name, shape, dtype, offset=BASE + rel)

    cur = 0
    xn_fm = sb("xn_fm", [128, 16, 1024], BF16, cur); cur += 32768
    OUT0 = cur
    out_tm = sb("out_tm", [128, 8, 2048], F32, OUT0); cur += 65536
    sq_scr = sb("sq_scr", [128, 2048], F32, OUT0)
    xn_tm2 = [sb("xn_tm", [128, 2048], BF16, OUT0 + 8192),
              sb("xn_tmB", [128, 2048], BF16, OUT0 + 16384)]
    s_bf_all = sb("s_bf_all", [128, 8, 128], BF16, OUT0 + 12288)
    s_f32 = sb("s_f32", [128, 128], F32, OUT0 + 14336)
    OFM0 = cur
    o_fm = sb("o_fm", [128, 16, 1024], BF16, OFM0); cur += 32768
    hh = sb("hh", [128, 11, 384], BF16, OFM0)
    sil = sb("sil", [128, 384], F32, OFM0 + 8448)
    xn_ge2 = [sb("xn_ge", [128, 16, 384], BF16, OFM0 + 10496),
              sb("xn_geB", [128, 16, 384], BF16, 14848)]
    xn_g = [sb(f"xn_g{i}", [128, 2176], BF16, 4352 * i) for i in range(2)]
    ybounce = [sb(f"ybounce{i}", [128, 512], F32, 8704 + 2048 * i)
               for i in range(3)]
    yg = [sb(f"yg{i}", [128, 2048], F32, 14848 + 8192 * i) for i in range(2)]
    BOUT0 = cur
    bout_bc = sb("bout_bc", [128, 2048], F32, cur); cur += 8192
    ident_f = sb("ident_f", [128, 128], F32, cur); cur += 512
    ident_b = sb("ident_b", [128, 128], BF16, cur); cur += 256
    diffx = sb("diffx", [128, 128], F32, cur); cur += 512
    iota_bc = sb("iota_bc", [128, 128], F32, cur); cur += 512
    slop = sb("slop", [128, 16], F32, cur); cur += 64
    negs = sb("negs", [128, 16], F32, cur); cur += 64
    cols = sb("cols", [128, 2], F32, cur); cur += 32
    tmet = sb("tmet", [128, 4], F32, cur); cur += 32
    cfr = sb("cfr", [128, 4, 16], F32, cur); cur += 256
    d_all = sb("d_all", [128, 16], F32, cur); cur += 64
    w_tm = sb("w_tm", [128, 8, 8], F32, cur); cur += 256
    dpow = sb("dpow", [128, 1], F32, cur); cur += 32
    blkbc = sb("blkbc", [128, 8], F32, cur); cur += 32
    dpa = sb("dpa", [128, 8], F32, cur); cur += 32
    gt_all = sb("gt_all", [128, 64], F32, cur); cur += 256
    iota8 = sb("iota8", [128, 8], F32, cur); cur += 32
    route_ind = sb("route_ind", [128, 8, 16], F32, cur); cur += 512
    route_sc = sb("route_sc", [128, 8, 8], F32, cur); cur += 256
    w_g = [sb(f"w_g{i}", [128, 3], F32, cur + 32 * i) for i in range(2)]
    cur += 64
    sel_all = sb("sel_all", [128, 64], F32, cur); cur += 256
    idx1_i = sb("idx1_i", [128, 8], I32, cur); cur += 32
    idx2_i = sb("idx2_i", [128, 8], I32, cur); cur += 32
    s_in = sb("s_in", [128, 128], F32, cur); cur += 512
    s_in_b = sb("s_in_b", [128, 128], BF16, cur); cur += 256
    qd_b = sb("qd_b", [128, 128], BF16, cur); cur += 256
    stat = sb("stat", [128, 8], F32, cur); cur += 32
    bq_c = sb("bq_c", [128, 48], F32, cur); cur += 192
    wg_sb = sb("wg_sb", [128, 16, 8], BF16, cur); cur += 256
    smb = [sb(f"smb{i}", [128, 128], BF16, cur + 256 * i) for i in range(3)]
    cur += 768
    cur = (cur + 63) & ~63

    A0 = cur
    q_fmB = sb("q_fmB", [128, 1024], BF16, A0)
    q2_fm = sb("q2_fm", [128, 1024], BF16, A0 + 2048)
    k_fmB = sb("k_fmB", [128, 1024], BF16, A0 + 4096)
    v_fmB = sb("v_fmB", [128, 1024], BF16, A0 + 6144)
    k2_tm = sb("k2_tm", [128, 8, 128], BF16, A0 + 8192)
    v_tm = sb("v_tm", [128, 8, 128], BF16, A0 + 10240)
    wqb = [sb(f"wqb{i}", [128, 16, 128], BF16, A0 + 12288 + 4096 * i)
           for i in range(2)]
    w12b = [sb(f"w12b{i}", [128, 2, 16, 128], BF16, A0 + 8192 * i)
            for i in range(2)]
    cur += 20480
    B0 = cur
    xbuf = [sb(f"xbuf{i}", [128, 2048], F32, B0 + 8192 * i) for i in range(2)]
    woutb = [sb("woutb0", [128, 16, 512], BF16, B0)]
    cur += 16384
    C0 = cur
    woutb.append(sb("woutb1", [128, 16, 512], BF16, C0))
    w3b = [sb(f"w3b{i}", [128, 11, 512], BF16, C0 + 11264 * i) for i in range(2)]
    stage = [sb(f"stage{i}", [128, 2176], BF16, C0 + 4352 * i)
             for i in range(2)]
    cur += 22528
    D0 = cur
    g_sb = sb("g_sb", [8, 1024], F32, D0)
    qre = [sb(f"qre{i}", [128, 1024], BF16, D0 + 2048 * i) for i in range(2)]
    qre.append(sb("qre2", [128, 1024], BF16, OUT0 + 22528))
    cur += 4096
    a4 = [sb(f"a4{i}", [128, 4, 128], F32, OUT0 + 16384 + 2048 * i)
          for i in range(3)]
    assert BASE + cur <= LIMIT, cur

    pf = [Bank(nc, f"pf{i}", nc.alloc_psum_tensor(f"pf{i}", [128, 512], F32).ap())
          for i in range(6)]
    pbf = [Bank(nc, f"pbf{i}",
                nc.alloc_psum_tensor(f"pbf{i}", [128, 1024], BF16).ap())
           for i in range(2)]

    c_ld = Sem(nc, "c_ld")
    dv = Sem(nc, "dv")
    ac = Sem(nc, "ac")
    a_st = Sem(nc, "a_st")
    q_st = Sem(nc, "q_st")
    o_st = Sem(nc, "o_st")
    cc_s = Sem(nc, "cc_s")
    xpz = Sem(nc, "xpz")
    pcast = Sem(nc, "pcast")
    xnsp = Sem(nc, "xnsp")
    rsc = Sem(nc, "rsc")
    sgl = Sem(nc, "sgl")
    swc = Sem(nc, "swc")
    yz = Sem(nc, "yz")
    yst = Sem(nc, "yst")
    ygl = Sem(nc, "ygl")

    st_x = Stream(nc, p, "stx", "sync", xbuf)
    st_wq = Stream(nc, p, "stwq", "gpsimd", wqb)
    w12b.append(sb("w12b2", [128, 2, 16, 128], BF16, BOUT0))
    st_w12 = Stream(nc, p, "stw12", "gpsimd", w12b)
    st_wo = Stream(nc, p, "stwo", "gpsimd", woutb)
    st_w3 = Stream(nc, p, "stw3", "gpsimd", w3b)
    st_a4 = Stream(nc, p, "sta4", "sync", a4)
    st_qr = Stream(nc, p, "stqr", "sync", qre)

    def T(fn, *incs): p.op("tensor", fn, *incs)
    def V(fn, *incs): p.op("vector", fn, *incs)
    def A(fn, *incs): p.op("scalar", fn, *incs)
    def G(fn, *incs): p.op("gpsimd", fn, *incs)
    def SY(fn, *incs): p.op("sync", fn, *incs)
    def Tw(s, v=None): p.wait("tensor", s, v)
    def Vw(s, v=None): p.wait("vector", s, v)
    def Aw(s, v=None): p.wait("scalar", s, v)
    def Gw(s, v=None): p.wait("gpsimd", s, v)
    def SYw(s, v=None): p.wait("sync", s, v)

    for dst, src in ((ident_f, d_ident), (ident_b, d_ident), (diffx, d_diff),
                     (iota_bc, d_iota), (slop, d_slops), (cols, d_cols),
                     (tmet, d_tmet), (bout_bc, d_boutb), (bq_c, d_bqkv),
                     (blkbc, d_blkbc), (iota8, d_iota8)):
        G(lambda E, dst=dst, src=src: E.dma_start(dst[:], src), (c_ld, 16))
    G(lambda E: E.dma_start(wg_sb[:], d_wgt.rearrange("(c p) e -> p c e", p=128)),
      (c_ld, 16))
    n_const = 12
    _pc = []
    V(lambda E: E.memset(stage[0][:], 0.0), (dv, 1))
    V(lambda E: E.memset(stage[1][:], 0.0), (dv, 1))
    dv_stz = dv.v
    Aw(dv, dv_stz)
    for k in range(25):
        r0 = k * 128
        n = 128 if k < 24 else 1
        A(lambda E, r0=r0, n=n: E.dma_start(
            d_xperm[r0:r0 + n, :], stage[0][0:n, :]), (xpz, 16))
    Vw(c_ld, 16 * n_const)
    V(lambda E: E.tensor_scalar(negs[:], slop[:], -1.0, None, ALU.mult), (dv, 1))
    Aw(dv)
    for r in range(4):
        A(lambda E, r=r: E.activation(cfr[:, r, :], slop[:], AF.Exp,
                                      scale=tmet[:, r:r + 1]), (ac, 1))
    for h in range(NH):
        A(lambda E, h=h: E.activation(d_all[:, h:h + 1], cols[:, 1:2], AF.Exp,
                                      scale=negs[:, h:h + 1]), (ac, 1))

    for blk in range(NB):
        slot, wv = st_x.load([lambda E, b, blk=blk: E.dma_start(
            b[:], d_x[blk * 128:(blk + 1) * 128, :])])
        xb = st_x.buf(slot)
        xtm = xn_tm2[blk % 2]
        Vw(st_x.slots[slot], wv)
        V(lambda E, xb=xb: E.reduce_sum(stat[:, 0:1], xb[:], axis=AX.X), (dv, 1))
        Aw(st_x.slots[slot], wv)
        A(lambda E, xb=xb: E.activation(
            sq_scr[:], xb[:], AF.Square, accum_out=stat[:, 1:2]), (ac, 1))
        Vw(dv)
        V(lambda E: E.tensor_scalar(stat[:, 2:3], stat[:, 0:1], 1.0 / 2048,
                                    None, ALU.mult), (dv, 1))
        Vw(ac)
        V(lambda E: E.tensor_scalar(stat[:, 3:4], stat[:, 1:2], 1.0 / 2048,
                                    None, ALU.mult), (dv, 1))
        Vw(dv)
        V(lambda E: E.scalar_tensor_tensor(stat[:, 4:5], stat[:, 2:3],
                                           stat[:, 2:3], stat[:, 3:4],
                                           ALU.mult, ALU.subtract), (dv, 1))
        Vw(dv)
        V(lambda E: E.tensor_scalar(stat[:, 5:6], stat[:, 4:5], -1.0, 1.0e-5,
                                    ALU.mult, ALU.add), (dv, 1))
        Aw(dv)
        A(lambda E: E.activation(stat[:, 5:6], stat[:, 5:6], AF.Ln), (ac, 1))
        Aw(ac)
        A(lambda E: E.activation(stat[:, 6:7], stat[:, 5:6], AF.Exp,
                                 scale=-0.5), (ac, 1))
        Vw(ac)
        Vw(xnsp, 16 * max(0, blk - 1))
        V(lambda E, xb=xb, xtm=xtm: E.tensor_scalar(
            xtm[:], xb[:], stat[:, 2:3], stat[:, 6:7],
            ALU.subtract, ALU.mult), (dv, 1), (st_x.free, 1))
        SYw(dv)
        SY(lambda E, blk=blk, xtm=xtm: E.dma_start(
            d_xnsp[blk * 128:(blk + 1) * 128, :], xtm[:]), (xnsp, 16))
        if blk == 0:
            Tw(c_ld, 16 * n_const)
        for half in range(2):
            bank = pbf[half]
            Tw(dv)
            Tw(bank.rd)
            for c8 in range(8):
                c = half * 8 + c8
                T(lambda E, c=c, c8=c8, bank=bank, xtm=xtm: E.transpose(
                    bank.ap[:, c8 * 128:(c8 + 1) * 128],
                    xtm[:, c * 128:(c + 1) * 128], ident_b[:]),
                  *(((bank.pe, 1),) if c8 == 7 else ()))
            Vw(bank.pe)
            V(lambda E, half=half, blk=blk, bank=bank: E.tensor_copy(
                xn_fm[:, half * 8:half * 8 + 8, blk * 128:(blk + 1) * 128],
                bank.ap.rearrange("p (c m) -> p c m", c=8)),
              (bank.rd, 1), (dv, 1))
        if blk == 3:
            dv_ln_half = dv.v
    dv_ln_done = dv.v

    for ic in range(2):
        bank = pf[2 + ic]
        Tw(dv, dv_ln_half if ic == 0 else dv_ln_done)
        Tw(bank.rd)
        for dd in range(16):
            T(lambda E, dd=dd, ic=ic, bank=bank: E.matmul(
                bank.ap[0:8, :], wg_sb[:, dd, :],
                xn_fm[:, dd, ic * 512:(ic + 1) * 512],
                start=(dd == 0), stop=(dd == 15)),
              *(((bank.pe, 1),) if dd == 15 else ()))
        Vw(bank.pe)
        V(lambda E, ic=ic, bank=bank: E.tensor_copy(
            g_sb[:, ic * 512:(ic + 1) * 512], bank.ap[0:8, :]),
          (bank.rd, 1), (dv, 1))
    bank = pf[4]
    Tw(dv)
    Tw(bank.rd)
    for blk in range(NB):
        T(lambda E, blk=blk, bank=bank: E.transpose(
            bank.ap[:, blk * 8:(blk + 1) * 8],
            g_sb[:, blk * 128:(blk + 1) * 128],
            ident_f[0:8, 0:8]), *(((bank.pe, 1),) if blk == 7 else ()))
    Vw(bank.pe)
    V(lambda E, bank=bank: E.tensor_copy(gt_all[:], bank.ap[:, 0:64]),
      (bank.rd, 1), (dv, 1))
    for blk in range(NB):
        gt = gt_all[:, blk * 8:(blk + 1) * 8]
        i1 = route_ind[:, blk, 0:8]
        i2 = route_ind[:, blk, 8:16]
        Vw(dv)
        V(lambda E, gt=gt: E.reduce_max(dpow[:], gt, axis=AX.X), (dv, 1))
        Vw(dv)
        V(lambda E, gt=gt, i1=i1: E.tensor_scalar(i1, gt, dpow[:],
                                                  None, ALU.is_ge), (dv, 1))
        Vw(dv)
        V(lambda E, gt=gt, i1=i1: E.scalar_tensor_tensor(
            s_in[:, 8:16], i1, -1.0e30, gt, ALU.mult, ALU.add), (dv, 1))
        Vw(dv)
        V(lambda E: E.reduce_max(s_in[:, 16:17], s_in[:, 8:16], axis=AX.X),
          (dv, 1))
        Vw(dv)
        V(lambda E, i2=i2: E.tensor_scalar(i2, s_in[:, 8:16],
                                           s_in[:, 16:17], None, ALU.is_ge),
          (dv, 1))
        V(lambda E: E.tensor_tensor(s_in[:, 17:18], dpow[:], s_in[:, 16:17],
                                    ALU.subtract), (dv, 1))
        Aw(dv)
        A(lambda E, blk=blk: E.activation(route_sc[:, blk, 2:3],
                                          s_in[:, 17:18], AF.Sigmoid),
          (ac, 1))
        Vw(ac)
        V(lambda E, blk=blk: E.tensor_scalar(
            route_sc[:, blk, 3:4], route_sc[:, blk, 2:3], -1.0, 1.0,
            ALU.mult, ALU.add), (dv, 1))
        V(lambda E, blk=blk, i1=i1: E.tensor_scalar(
            s_in[:, 32:40], i1, route_sc[:, blk, 2:3], None, ALU.mult),
          (dv, 1))
        Vw(dv)
        V(lambda E, blk=blk, i2=i2: E.scalar_tensor_tensor(
            w_tm[:, blk, :], i2, route_sc[:, blk, 3:4], s_in[:, 32:40],
            ALU.mult, ALU.add), (dv, 1))
        V(lambda E, i1=i1: E.tensor_tensor(s_in[:, 40:48], i1, iota8[:],
                                           ALU.mult), (dv, 1))
        Vw(dv)
        V(lambda E, blk=blk: E.reduce_sum(route_sc[:, blk, 0:1],
                                          s_in[:, 40:48], axis=AX.X), (dv, 1))
        V(lambda E, i2=i2: E.tensor_tensor(s_in[:, 40:48], i2, iota8[:],
                                           ALU.mult), (dv, 1))
        Vw(dv)
        V(lambda E, blk=blk: E.reduce_sum(route_sc[:, blk, 1:2],
                                          s_in[:, 40:48], axis=AX.X), (dv, 1))
        V(lambda E, blk=blk, i1=i1, i2=i2: E.tensor_tensor(
            sel_all[:, blk * 8:(blk + 1) * 8], i1, i2, ALU.add), (dv, 1))


    for grp in range(2):
        bank = pf[2 + grp]
        Tw(dv)
        Tw(bank.rd)
        for b4 in range(4):
            blk = grp * 4 + b4
            T(lambda E, blk=blk, b4=b4, bank=bank: E.transpose(
                bank.ap[0:8, b4 * 128:(b4 + 1) * 128],
                sel_all[:, blk * 8:(blk + 1) * 8], ident_f[0:128, 0:128]),
              *(((bank.pe, 1),) if b4 == 3 else ()))
        Vw(bank.pe)
        V(lambda E, grp=grp, bank=bank: E.tensor_copy(
            g_sb[:, grp * 512:(grp + 1) * 512], bank.ap[0:8, 0:512]),
          (bank.rd, 1), (dv, 1))
    zv = sq_scr.ap()[0:8, 0:1024]
    pv = sq_scr.ap()[0:8, 1024:2048]
    V(lambda E, zv=zv: E.memset(zv, 0.0), (dv, 1))
    Vw(dv)
    V(lambda E, zv=zv, pv=pv: E.tensor_tensor_scan(
        pv, g_sb[:], zv, 0.0, ALU.add, ALU.add), (dv, 1))
    bank = pf[4]
    Tw(dv)
    Tw(bank.rd)
    for blk in range(NB):
        T(lambda E, blk=blk, bank=bank, pv=pv: E.transpose(
            bank.ap[:, blk * 8:(blk + 1) * 8],
            pv[0:8, blk * 128:(blk + 1) * 128], ident_f[0:8, 0:8]),
          *(((bank.pe, 1),) if blk == 7 else ()))
    Vw(bank.pe)
    V(lambda E, bank=bank: E.tensor_copy(gt_all[:], bank.ap[:, 0:64]),
      (bank.rd, 1), (dv, 1))
    for blk in range(NB):
        i1 = route_ind[:, blk, 0:8]
        i2 = route_ind[:, blk, 8:16]
        pos = gt_all[:, blk * 8:(blk + 1) * 8]
        Vw(dv)
        V(lambda E, i1=i1, pos=pos: E.tensor_tensor(s_in[:, 40:48], i1, pos,
                                                    ALU.mult), (dv, 1))
        Vw(dv)
        V(lambda E, blk=blk: E.reduce_sum(route_sc[:, blk, 4:5],
                                          s_in[:, 40:48], axis=AX.X), (dv, 1))
        V(lambda E, i2=i2, pos=pos: E.tensor_tensor(s_in[:, 40:48], i2, pos,
                                                    ALU.mult), (dv, 1))
        Vw(dv)
        V(lambda E, blk=blk: E.reduce_sum(route_sc[:, blk, 5:6],
                                          s_in[:, 40:48], axis=AX.X), (dv, 1))
        for k, (pcol, icol, itgt) in enumerate(
                ((4, 0, idx1_i), (5, 1, idx2_i))):
            Vw(dv)
            V(lambda E, blk=blk, icol=icol: E.tensor_scalar(
                s_in[:, 49:50], route_sc[:, blk, icol:icol + 1], 384.0, -1.0,
                ALU.mult, ALU.add), (dv, 1))
            Vw(dv)
            V(lambda E, blk=blk, pcol=pcol: E.tensor_tensor(
                s_in[:, 50:51], s_in[:, 49:50],
                route_sc[:, blk, pcol:pcol + 1], ALU.add), (dv, 1))
            V(lambda E, blk=blk, pcol=pcol: E.tensor_scalar(
                s_in[:, 51:52], route_sc[:, blk, pcol:pcol + 1], 384.5, None,
                ALU.is_ge), (dv, 1))
            Vw(dv)
            V(lambda E: E.scalar_tensor_tensor(
                s_in[:, 50:51], s_in[:, 51:52], 1.0e9, s_in[:, 50:51],
                ALU.mult, ALU.add), (dv, 1))
            Vw(dv)
            V(lambda E: E.tensor_scalar(s_in[:, 50:51], s_in[:, 50:51],
                                        3072.0, None, ALU.min), (dv, 1))
            Vw(dv)
            V(lambda E, blk=blk, itgt=itgt: E.tensor_copy(
                itgt[:, blk:blk + 1], s_in[:, 50:51]), (dv, 1))
    dv_route = dv.v
    Vw(dv)
    V(lambda E: E.memset(sq_scr[:], 0.0), (dv, 1))
    dv_zero = dv.v
    SYw(dv, dv_zero)
    SY(lambda E: E.dma_start(d_yperm[3072:3073, :], sq_scr[0:1, :]), (yz, 16))
    _groups = []
    _og, _ov, _os = p.ops["gpsimd"], p.ops["vector"], p.ops["sync"]
    k = 0
    for blk in range(NB):
        for itgt, pcol in ((idx1_i, 2), (idx2_i, 3)):
            sl = k % 2
            _ds, _dv, _dg = [], [], []
            p.ops["sync"] = _ds
            if k == 0:
                SYw(xpz, 16 * 25)
                SYw(xnsp, 16 * 8)
            if k >= 2:
                SYw(rsc, 16 * (k - 1))
            SY(lambda E, blk=blk, sl=sl: E.dma_start(
                stage[sl][:, 0:2048],
                d_xnsp[blk * 128:(blk + 1) * 128, :]), (sgl, 16))
            p.ops["sync"] = _os
            p.ops["vector"] = _dv
            p.wait("vector", sgl, 16 * (k + 1))
            p.op("vector", (lambda E, blk=blk, sl=sl, pcol=pcol:
                            E.tensor_copy(stage[sl][:, 2048:2049],
                                          route_sc[:, blk, pcol:pcol + 1])),
                 (swc, 1))
            p.ops["vector"] = _ov
            p.ops["gpsimd"] = _dg
            Gw(swc, k + 1)
            if k == 0:
                Gw(dv, dv_route)
                Gw(xpz, 16 * 25)
            G(lambda E, blk=blk, sl=sl, itgt=itgt: E.indirect_dma_start(
                d_xperm, IndirectOffsetOnAxis(itgt[:, blk:blk + 1], 0),
                stage[sl][:], None,
                bounds_check=3072, oob_is_err=False), (rsc, 16))
            p.ops["gpsimd"] = _og
            _groups.append((_ds, _dv, _dg))
            k += 1
    rsc_done = rsc.v

    _glag = []
    for h in range(NH):
        if h == 11:
            Gw(a_st, 16 * 8)
            G(lambda E: E.collective_compute(
                "AllGather", ALU.bypass,
                replica_groups=[[0, 1, 2, 3], [4, 5, 6, 7]],
                ins=[d_astage.ap()[0:8 * 128, :].opt()],
                outs=[d_aall1.ap().opt()]), (cc_s, 1))
        if h >= 1:
            while _glag:
                p.ops["gpsimd"].append(_glag.pop(0))
            for _ in range(2):
                if _groups:
                    gs, gv, gg = _groups.pop(0)
                    p.ops["sync"].extend(gs)
                    p.ops["vector"].extend(gv)
                    _glag.extend(gg)
        Aw(dv)
        A(lambda E, h=h: E.activation(qd_b[:], iota_bc[:], AF.Exp,
                                      scale=negs[:, h:h + 1]), (ac, 1))
        A(lambda E, h=h: E.activation(stat[:, 7:8], cols[:, 0:1], AF.Exp,
                                      scale=negs[:, h:h + 1]), (ac, 1))
        A(lambda E, h=h: E.activation(s_in[:], diffx[:], AF.Exp,
                                      scale=negs[:, h:h + 1]), (ac, 1))
        ac_tab = ac.v

        for j, slabi in ((0, 16 + h), (1, 32 + h)):
            slot, wv = st_wq.load([lambda E, b, s=slabi: E.dma_start(
                b[:], d_wqkv[s])])
            wslab = st_wq.buf(slot)
            dst = k_fmB if j == 0 else v_fmB
            for ic in range(2):
                bank = pf[ic]
                Tw(st_wq.slots[slot], wv)
                if h == 0 and j == 0:
                    Tw(dv, dv_ln_half if ic == 0 else dv_ln_done)
                Tw(bank.rd)
                for dd in range(16):
                    T(lambda E, dd=dd, ic=ic, bank=bank, wslab=wslab: E.matmul(
                        bank.ap[:], wslab[:, dd, :],
                        xn_fm[:, dd, ic * 512:(ic + 1) * 512],
                        start=(dd == 0), stop=(dd == 15)),
                      *(((bank.pe, 1),) if dd == 15 else ()))
                Vw(bank.pe)
                incs = [(bank.rd, 1), (dv, 1)]
                if ic == 1:
                    incs.append((st_wq.free, 1))
                V(lambda E, dst=dst, ic=ic, bank=bank, si=slabi: E.tensor_scalar(
                    dst[:, ic * 512:(ic + 1) * 512], bank.ap[:],
                    bq_c[:, si:si + 1], None, ALU.add), *incs)

        Tw(dv)
        for j in range(2):
            bank = pbf[j]
            src = k_fmB if j == 0 else v_fmB
            Tw(bank.rd)
            for blk in range(NB):
                T(lambda E, blk=blk, src=src, bank=bank: E.transpose(
                    bank.ap[:, blk * 128:(blk + 1) * 128],
                    src[:, blk * 128:(blk + 1) * 128], ident_b[:]),
                  *(((bank.pe, 1),) if blk == 7 else ()))
        Vw(ac, ac_tab)
        Vw(pbf[0].pe)
        V(lambda E: E.tensor_scalar(
            k2_tm.ap().rearrange("p b m -> p (b m)"), pbf[0].ap[:],
            stat[:, 7:8], None, ALU.mult), (pbf[0].rd, 1), (dv, 1))
        Vw(pbf[1].pe)
        V(lambda E: E.tensor_copy(v_tm.ap().rearrange("p b m -> p (b m)"),
                                  pbf[1].ap[:]), (pbf[1].rd, 1), (dv, 1))
        dv_kv = dv.v

        def emit_scan(h=h, dv_kv=dv_kv):
            Vw(a_st, a_st.v)
            V(lambda E: E.memset(s_f32[:], 0.0), (dv, 1))
            V(lambda E: E.memset(s_bf_all.ap().rearrange("p b m -> p (b m)"),
                                 0.0), (dv, 1))
            Tw(dv)
            for i in range(NB):
                bank = pf[4 + (i % 2)]
                Tw(bank.rd)
                T(lambda E, i=i, bank=bank: E.matmul(
                    bank.ap[:, 0:128], k2_tm[:, i, :], v_tm[:, i, :],
                    start=True, stop=True), (bank.pe, 1))
                if i > 0:
                    V(lambda E, i=i: E.tensor_copy(s_bf_all[:, i, :],
                                                   s_f32[:]), (dv, 1))
                Vw(bank.pe)
                V(lambda E, h=h, bank=bank: E.scalar_tensor_tensor(
                    s_f32[:], s_f32[:], d_all[:, h:h + 1], bank.ap[:, 0:128],
                    ALU.mult, ALU.add), (bank.rd, 1), (dv, 1))
            SYw(dv)
            SY(lambda E, h=h: E.dma_start(
                d_astage.ap()[h * 128:(h + 1) * 128, :], s_f32[:]),
               (a_st, 16))

        def emit_qproj(h=h):
            slot, wv = st_wq.load([lambda E, b, s=h: E.dma_start(
                b[:], d_wqkv[s])])
            wslab = st_wq.buf(slot)
            for ic in range(2):
                bank = pf[ic]
                Tw(st_wq.slots[slot], wv)
                Tw(bank.rd)
                for dd in range(16):
                    T(lambda E, dd=dd, ic=ic, bank=bank, wslab=wslab:
                      E.matmul(
                          bank.ap[:], wslab[:, dd, :],
                          xn_fm[:, dd, ic * 512:(ic + 1) * 512],
                          start=(dd == 0), stop=(dd == 15)),
                      *(((bank.pe, 1),) if dd == 15 else ()))
                Vw(bank.pe)
                incs = [(bank.rd, 1), (dv, 1)]
                if ic == 1:
                    incs.append((st_wq.free, 1))
                V(lambda E, ic=ic, bank=bank, h=h: E.tensor_scalar(
                    q_fmB[:, ic * 512:(ic + 1) * 512], bank.ap[:],
                    bq_c[:, h:h + 1], None, ALU.add), *incs)
            return dv.v

        def emit_q2(h=h):
            Vw(q_st, q_st.v)
            Vw(ac, ac_tab)
            Vw(dv)
            for blk in range(NB):
                V(lambda E, blk=blk: E.tensor_tensor(
                    q2_fm[:, blk * 128:(blk + 1) * 128],
                    q_fmB[:, blk * 128:(blk + 1) * 128], qd_b[:], ALU.mult),
                  (dv, 1))
            SYw(dv)
            SY(lambda E, h=h: E.dma_start(d_qspill[h], q2_fm[:]), (q_st, 16))

        if h == NH - 1:
            emit_scan()
            dv_qevac = emit_qproj()
            emit_q2()
        else:
            dv_qevac = emit_qproj()
            emit_scan()
            emit_q2()

        def emit_scoresT(i):
            sc_bank = pf[2 + (i % 2)]
            Tw(sc_bank.rd)
            T(lambda E, i=i, bank=sc_bank: E.matmul(
                bank.ap[:, 0:128], k_fmB[:, i * 128:(i + 1) * 128],
                q_fmB[:, i * 128:(i + 1) * 128], start=True, stop=True),
              (sc_bank.pe, 1))

        Tw(dv, dv_qevac)
        emit_scoresT(0)
        emit_scoresT(1)
        for i in range(NB):
            sc_bank = pf[2 + (i % 2)]
            o_bank = pf[4 + (i % 2)]
            Vw(sc_bank.pe)
            V(lambda E, i=i, bank=sc_bank: E.tensor_tensor(
                smb[i % 3][:], bank.ap[:, 0:128], s_in[:], ALU.mult),
              (sc_bank.rd, 1), (dv, 1))
            Tw(dv)
            Tw(o_bank.rd)
            T(lambda E, i=i, bank=o_bank: E.matmul(
                bank.ap[:, 0:128], s_bf_all[:, i, :],
                q2_fm[:, i * 128:(i + 1) * 128], start=True, stop=False))
            T(lambda E, i=i, bank=o_bank: E.matmul(
                bank.ap[:, 0:128], v_tm[:, i, :], smb[i % 3][:],
                start=False, stop=True), (o_bank.pe, 1))
            if i + 2 < NB:
                emit_scoresT(i + 2)
            Vw(o_bank.pe)
            V(lambda E, i=i, h=h, bank=o_bank: E.tensor_copy(
                o_fm[:, h, i * 128:(i + 1) * 128], bank.ap[:, 0:128]),
              (o_bank.rd, 1), (dv, 1))
    while _groups:
        gs, gv, gg = _groups.pop(0)
        p.ops["sync"].extend(gs)
        p.ops["vector"].extend(gv)
        _glag.extend(gg)
    while _glag:
        p.ops["gpsimd"].append(_glag.pop(0))
    dv_pass1_done = dv.v

    if "o1" in taps:
        d_do1 = nc.dram_tensor("dbg_o1", [128, 16 * 1024], BF16,
                               kind="ExternalOutput").ap()
        tsem = Sem(nc, "tap_o1")
        SYw(dv, dv_pass1_done)
        SY(lambda E: E.dma_start(
            d_do1, o_fm.ap().rearrange("p c m -> p (c m)")), (tsem, 16))
        p.wait("vector", tsem, 16)

    Gw(a_st, 16 * 16)
    G(lambda E: E.collective_compute(
        "AllGather", ALU.bypass,
        replica_groups=[[0, 1, 2, 3], [4, 5, 6, 7]],
        ins=[d_astage.ap()[8 * 128:16 * 128, :].opt()],
        outs=[d_aall2.ap().opt()]), (cc_s, 1))

    aall_v1 = d_aall1.ap().rearrange("(r h p) m -> r h p m", r=4, h=8)
    aall_v2 = d_aall2.ap().rearrange("(r h p) m -> r h p m", r=4, h=8)
    for h in range(NH):
        av = aall_v1 if h < 8 else aall_v2
        hh_i = h % 8
        gates = ()
        if h == 0:
            gates = ((cc_s, 1),)
        elif h == 8:
            gates = ((cc_s, 2),)
        slot, wv = st_a4.load(
            [lambda E, b, av=av, hh_i=hh_i: E.dma_start(
                b[:], av[:, hh_i].rearrange("r p m -> p r m"))], gates=gates)
        slotq, wvq = st_qr.load(
            [lambda E, b, h=h: E.dma_start(b[:], d_qspill[h])],
            gates=((q_st, q_st.v),) if h == 0 else ())
        a4b = st_a4.buf(slot)
        Vw(st_a4.slots[slot], wv)
        V(lambda E, a4b=a4b, h=h: E.tensor_scalar(
            s_in[:], a4b[:, 0, :], cfr[:, 0, h:h + 1], None, ALU.mult), (dv, 1))
        for r in range(1, 4):
            incs = [(dv, 1)]
            if r == 3:
                incs.append((st_a4.free, 1))
            V(lambda E, a4b=a4b, h=h, r=r: E.scalar_tensor_tensor(
                s_in[:], a4b[:, r, :], cfr[:, r, h:h + 1], s_in[:],
                ALU.mult, ALU.add), *incs)
        Aw(dv)
        A(lambda E: E.copy(s_in_b[:], s_in[:]), (ac, 1))
        A(lambda E, h=h: E.activation(dpa[:], blkbc[:], AF.Exp,
                                      scale=negs[:, h:h + 1]), (ac, 1))
        qreb = st_qr.buf(slotq)
        for ic in range(2):
            bank = pf[ic]
            Tw(ac)
            Tw(st_qr.slots[slotq], wvq)
            Tw(bank.rd)
            T(lambda E, ic=ic, bank=bank, qreb=qreb: E.matmul(
                bank.ap[:], s_in_b[:], qreb[:, ic * 512:(ic + 1) * 512],
                start=True, stop=True),
              (bank.pe, 1), *(((st_qr.free, 1),) if ic == 1 else ()))
            Vw(bank.pe)
            Vw(ac)
            for bi in range(4):
                i = ic * 4 + bi
                incs = [(dv, 1)]
                if bi == 3:
                    incs.append((bank.rd, 1))
                V(lambda E, i=i, bi=bi, h=h, bank=bank: E.scalar_tensor_tensor(
                    o_fm[:, h, i * 128:(i + 1) * 128],
                    bank.ap[:, bi * 128:(bi + 1) * 128], dpa[:, i:i + 1],
                    o_fm[:, h, i * 128:(i + 1) * 128], ALU.mult, ALU.add),
                  *incs)
    dv_pass2_done = dv.v

    if "o2" in taps:
        d_do2 = nc.dram_tensor("dbg_o2", [128, 16 * 1024], BF16,
                               kind="ExternalOutput").ap()
        tsem = Sem(nc, "tap_o2")
        SYw(dv, dv_pass2_done)
        SY(lambda E: E.dma_start(
            d_do2, o_fm.ap().rearrange("p c m -> p (c m)")), (tsem, 16))
        p.wait("vector", tsem, 16)

    for fc in range(4):
        dmas = [
            (lambda E, b, fc=fc, cg=cg: E.dma_start(
                b[:, cg * 4:(cg + 1) * 4, :],
                d_wout[cg * 512:(cg + 1) * 512, fc * 512:(fc + 1) * 512]
                .rearrange("(g p) f -> p g f", p=128)))
            for cg in range(4)]
        gates = ((dv, dv_ln_done),) if fc == 0 else ()
        slot, wv = st_wo.load(dmas, gates=gates)
        wob = st_wo.buf(slot)
        Tw(st_wo.slots[slot], wv)
        if fc == 0:
            Tw(dv, dv_pass2_done)
        for blkset in ((0, 1, 2, 3, 4, 5), (6, 7)):
            for bi in range(len(blkset)):
                Tw(pf[bi].rd)
            for c in range(16):
                for bi, blk in enumerate(blkset):
                    bank = pf[bi]
                    T(lambda E, c=c, blk=blk, bank=bank, wob=wob: E.matmul(
                        bank.ap[:], o_fm[:, c, blk * 128:(blk + 1) * 128],
                        wob[:, c, :], start=(c == 0), stop=(c == 15)),
                      *(((bank.pe, 1),) if c == 15 else ()))
            for bi, blk in enumerate(blkset):
                bank = pf[bi]
                Vw(bank.pe)
                incs = [(bank.rd, 1), (dv, 1)]
                if blk == 7:
                    incs.append((st_wo.free, 1))
                V(lambda E, blk=blk, fc=fc, bank=bank: E.tensor_tensor(
                    out_tm[:, blk, fc * 512:(fc + 1) * 512], bank.ap[:],
                    bout_bc[:, fc * 512:(fc + 1) * 512], ALU.add), *incs)
    dv_att_done = dv.v

    if "att" in taps:
        d_datt = nc.dram_tensor("dbg_att", [1024, 2048], F32,
                                kind="ExternalOutput").ap()
        tsem = Sem(nc, "tap_att")
        SYw(dv, dv_att_done)
        SY(lambda E: E.dma_start(
            d_datt.rearrange("(b p) f -> p b f", p=128),
            out_tm[:]), (tsem, 16))
        p.wait("vector", tsem, 16)

    st_xg = Stream(nc, p, "stxg", "sync", xn_g)
    ybu = [0] * 4
    dv_xg_parity = [0, 0]

    def emit_gather_loads(e):
        par = e % 2
        toks = []
        for blk in range(3):
            gates = ()
            if e == 0 and blk == 0:
                gates = ((rsc, rsc_done), (dv, dv_pass1_done))
            r0 = e * 384 + blk * 128
            sx, wvx = st_xg.load(
                [lambda E, b, r0=r0: E.dma_start(
                    b[:], d_xperm[r0:r0 + 128, :])], gates=gates)
            toks.append((sx, wvx))
        return toks

    def emit_gather_transp(e, toks):
        par = e % 2
        xg = xn_ge2[par]
        for blk in range(3):
            sx, wvx = toks[blk]
            xgb = st_xg.buf(sx)
            Vw(st_xg.slots[sx], wvx)
            V(lambda E, xgb=xgb, par=par, blk=blk: E.tensor_copy(
                w_g[par][:, blk:blk + 1], xgb[:, 2048:2049]), (dv, 1))
            for half in range(2):
                bank = pbf[half]
                Tw(st_xg.slots[sx], wvx)
                if e == 0 and blk == 0 and half == 0:
                    Tw(dv, dv_att_done)
                Tw(bank.rd)
                for c8 in range(8):
                    ch = half * 8 + c8
                    T(lambda E, ch=ch, c8=c8, bank=bank, xgb=xgb: E.transpose(
                        bank.ap[:, c8 * 128:(c8 + 1) * 128],
                        xgb[:, ch * 128:(ch + 1) * 128], ident_b[:]),
                      *(((bank.pe, 1),) if c8 == 7 else ()))
                Vw(bank.pe)
                incs = [(bank.rd, 1), (dv, 1)]
                if half == 1:
                    incs.append((st_xg.free, 1))
                V(lambda E, half=half, blk=blk, bank=bank, xg=xg: E.tensor_copy(
                    xg[:, half * 8:half * 8 + 8, blk * 128:(blk + 1) * 128],
                    bank.ap.rearrange("p (c m) -> p c m", c=8)), *incs)
        dv_xg_parity[par] = dv.v

    toks_cur = emit_gather_loads(0)
    emit_gather_transp(0, toks_cur)
    for e in range(NE):
        if e + 1 < NE:
            toks_nxt = emit_gather_loads(e + 1)
        par = e % 2
        xg = xn_ge2[par]
        dv_xg = dv_xg_parity[par]

        for t in range(NT):
            g0 = ((q_st, q_st.v),) if (e == 0 and t == 0) else ()
            if e == 0 and t == 2:
                g0 = ((dv, dv_att_done),)
            s1, wv1 = st_w12.load(
                [lambda E, b, e=e, t=t: E.dma_start(
                    b[:], d_w12[e, t])],
                gates=g0)
            b1 = pf[0] if t % 2 == 0 else pf[2]
            b2 = pf[1] if t % 2 == 0 else pf[3]
            wb = st_w12.buf(s1)
            for mi, bank in ((0, b1), (1, b2)):
                Tw(st_w12.slots[s1], wv1)
                if t == 0:
                    Tw(dv, dv_xg)
                Tw(bank.rd)
                for dd in range(16):
                    T(lambda E, dd=dd, mi=mi, bank=bank, wb=wb, xg=xg:
                      E.matmul(
                          bank.ap[:, 0:384], wb[:, mi, dd, :],
                          xg[:, dd, 0:384],
                          start=(dd == 0), stop=(dd == 15)),
                      *(((bank.pe, 1),) if dd == 15 else ()))
            Aw(b1.pe)
            A(lambda E, b1=b1: E.activation(
                sil[:], b1.ap[:, 0:384], AF.Silu), (ac, 1), (b1.rd, 1))
            Vw(ac)
            Vw(b2.pe)
            V(lambda E, t=t, b2=b2: E.tensor_tensor(
                hh[:, t, :], sil[:], b2.ap[:, 0:384], ALU.mult),
              (b2.rd, 1), (dv, 1), (st_w12.free, 1))
        dv_h_done = dv.v

        if e + 1 < NE:
            emit_gather_transp(e + 1, toks_nxt)

        for fc in range(4):
            dmas = [lambda E, b, e=e, fc=fc: E.dma_start(
                b[:], d_w3[e, fc])]
            g3 = ((dv, dv_att_done), (rsc, rsc_done)) \
                if (e == 0 and fc == 0) else ()
            sl3, wv3 = st_w3.load(dmas, gates=g3)
            w3buf = st_w3.buf(sl3)
            Tw(st_w3.slots[sl3], wv3)
            if fc == 0:
                Tw(dv, dv_h_done)
            for blk in range(3):
                Tw(pf[blk].rd)
            for t in range(NT):
                for blk in range(3):
                    bank = pf[blk]
                    T(lambda E, t=t, blk=blk, bank=bank, w3buf=w3buf:
                      E.matmul(
                          bank.ap[:], hh[:, t, blk * 128:(blk + 1) * 128],
                          w3buf[:, t, :], start=(t == 0), stop=(t == 10)),
                      *(((bank.pe, 1),) if t == 10 else ()))
            for blk in range(3):
                bank = pf[blk]
                yb = ybounce[ybu[0] % 3]
                Vw(bank.pe)
                if ybu[0] >= 3:
                    Vw(yst, 16 * (ybu[0] - 2))
                incs = [(bank.rd, 1), (dv, 1)]
                if blk == 2:
                    incs.append((st_w3.free, 1))
                V(lambda E, blk=blk, par=par, bank=bank, yb=yb:
                  E.tensor_scalar(
                      yb[:], bank.ap[:],
                      w_g[par][:, blk:blk + 1], None, ALU.mult), *incs)
                r0 = e * 384 + blk * 128
                SYw(dv)
                SY(lambda E, r0=r0, fc=fc, yb=yb: E.dma_start(
                    d_yperm[r0:r0 + 128, fc * 512:(fc + 1) * 512], yb[:]),
                   (yst, 16))
                ybu[0] += 1
    dv_ffn_done = dv.v
    yst_done = yst.v

    for blk in range(NB):
        if blk == 0:
            Gw(yst, yst_done)
            Gw(yz, 16)
        else:
            Gw(dv, dv.v)
        for j, itgt in ((0, idx1_i), (1, idx2_i)):
            G(lambda E, blk=blk, j=j, itgt=itgt: E.indirect_dma_start(
                yg[j][:], None, d_yperm,
                IndirectOffsetOnAxis(itgt[:, blk:blk + 1], 0),
                bounds_check=3072, oob_is_err=False), (ygl, 16))
        gates = ((dv, dv_ffn_done),) if blk == 0 else ()
        slot, wv = st_x.load([lambda E, b, blk=blk: E.dma_start(
            b[:], d_x[blk * 128:(blk + 1) * 128, :])], gates=gates)
        xb = st_x.buf(slot)
        Vw(ygl, ygl.v)
        V(lambda E, blk=blk: E.tensor_tensor(
            out_tm[:, blk, :], out_tm[:, blk, :], yg[0][:], ALU.add), (dv, 1))
        Vw(dv)
        V(lambda E, blk=blk: E.tensor_tensor(
            out_tm[:, blk, :], out_tm[:, blk, :], yg[1][:], ALU.add), (dv, 1))
        Vw(st_x.slots[slot], wv)
        Vw(dv)
        V(lambda E, blk=blk, xb=xb: E.tensor_tensor(
            out_tm[:, blk, :], out_tm[:, blk, :], xb[:], ALU.add),
          (st_x.free, 1), (dv, 1))
        SYw(dv)
        SY(lambda E, blk=blk: E.dma_start(
            d_out[blk * 128:(blk + 1) * 128, :], out_tm[:, blk, :]),
           (o_st, 16))
    if "route" in taps:
        d_r1 = nc.dram_tensor("dbg_idx1", [128, 8], I32, kind="ExternalOutput").ap()
        d_r2 = nc.dram_tensor("dbg_idx2", [128, 8], I32, kind="ExternalOutput").ap()
        d_rs = nc.dram_tensor("dbg_rsc", [128, 64], F32, kind="ExternalOutput").ap()
        d_xp = nc.dram_tensor("dbg_xperm", [768, 2049], BF16, kind="ExternalOutput").ap()
        d_yp = nc.dram_tensor("dbg_yperm", [768, 2048], F32, kind="ExternalOutput").ap()
        SY(lambda E: E.dma_start(d_r1, idx1_i[:]), (o_st, 16))
        SY(lambda E: E.dma_start(d_r2, idx2_i[:]), (o_st, 16))
        SY(lambda E: E.dma_start(
            d_rs, route_sc.ap().rearrange("p a b -> p (a b)")), (o_st, 16))
        SY(lambda E: E.dma_start(d_xp, d_xperm[0:768, 0:2049]), (o_st, 16))
        SY(lambda E: E.dma_start(d_yp, d_yperm[0:768, :]), (o_st, 16))
    if "xn" in taps:
        d_dxn = nc.dram_tensor("dbg_xn", [128, 16 * 1024], BF16,
                               kind="ExternalOutput").ap()
        SY(lambda E: E.dma_start(
            d_dxn, xn_fm.ap().rearrange("p c m -> p (c m)")), (o_st, 16))
    if "wtm" in taps:
        d_dwt = nc.dram_tensor("dbg_wtm", [128, 64], F32,
                               kind="ExternalOutput").ap()
        SY(lambda E: E.dma_start(
            d_dwt, w_tm.ap().rearrange("p b e -> p (b e)")), (o_st, 16))
    SYw(o_st, o_st.v)

    with nc.Block() as block:
        @block.tensor
        def _(E):
            for f in p.ops["tensor"]:
                f(E)

        @block.vector
        def _(E):
            for f in p.ops["vector"]:
                f(E)

        @block.scalar
        def _(E):
            for f in p.ops["scalar"]:
                f(E)

        @block.gpsimd
        def _(E):
            for f in p.ops["gpsimd"]:
                f(E)

        @block.sync
        def _(E):
            for f in p.ops["sync"]:
                f(E)
    return nc


def _host_inputs(ins):
    f32 = np.float32
    x = np.ascontiguousarray(np.asarray(ins["L0_attn_x"], f32).reshape(8192, 2048))
    Wqkv = np.asarray(ins["L0_attn_Wqkv"], f32)
    wqkv_s = np.ascontiguousarray(Wqkv.reshape(16, 128, 48, 128)
                                  .transpose(2, 1, 0, 3))
    bqkv_c = np.ascontiguousarray(
        np.asarray(ins["L0_attn_bqkv"], f32).reshape(48, 128).T)
    wout = np.ascontiguousarray(np.asarray(ins["L0_attn_Wout"], f32))
    bout_bc = np.ascontiguousarray(
        np.broadcast_to(np.asarray(ins["L0_attn_bout"], f32), (128, 2048)))
    slopes_bc = np.ascontiguousarray(
        np.broadcast_to(np.asarray(ins["L0_attn_slopes"], f32), (128, 16)))
    wgt = np.ascontiguousarray(np.asarray(ins["L0_ffn_Wg"], f32).T)
    W1 = np.asarray(ins["L0_ffn_W1"], f32)
    W2 = np.asarray(ins["L0_ffn_W2"], f32)
    W3 = np.asarray(ins["L0_ffn_W3"], f32)
    w1s = W1.transpose(0, 2, 1).reshape(8, 2048, 11, 128).transpose(0, 2, 1, 3)
    w2s = W2.transpose(0, 2, 1).reshape(8, 2048, 11, 128).transpose(0, 2, 1, 3)
    w12s = np.stack([w1s, w2s], axis=2)
    w12s = np.ascontiguousarray(w12s.reshape(8, 11, 2, 16, 128, 128)
                                .transpose(0, 1, 4, 2, 3, 5))
    w3t = np.ascontiguousarray(W3.transpose(0, 2, 1)
                               .reshape(8, 11, 128, 4, 512)
                               .transpose(0, 3, 2, 1, 4))

    ii = np.arange(128, dtype=f32)
    diffm = ii[None, :] - ii[:, None]
    diffx = np.where(diffm >= 0, diffm, BIG).astype(f32)
    iota_bc = np.ascontiguousarray(np.broadcast_to(ii, (128, 128)))
    cols = np.ascontiguousarray(
        np.stack([128.0 - ii, np.full(128, 128.0, f32)], axis=1).astype(f32))
    ident = np.eye(128, dtype=f32)
    blk_bc = np.ascontiguousarray(
        np.broadcast_to(128.0 * np.arange(8, dtype=f32), (128, 8)))
    iota8_bc = np.ascontiguousarray(
        np.broadcast_to(np.arange(8, dtype=f32), (128, 8)))

    shared = dict(wqkv_s=wqkv_s, bqkv_c=bqkv_c, wout=wout, bout_bc=bout_bc,
                  slopes_bc=slopes_bc, wgt=wgt, w12s=w12s, w3t=w3t,
                  diffx=diffx, iota_bc=iota_bc, cols=cols, ident=ident,
                  blk_bc=blk_bc, iota8_bc=iota8_bc)
    in_maps = []
    for core in range(8):
        q = core % 4
        tm = np.full((128, 4), -BIG, f32)
        for j in range(q):
            tm[:, j] = -1024.0 * (q - 1 - j)
        m = dict(shared)
        m["xs"] = np.ascontiguousarray(x[core * 1024:(core + 1) * 1024])
        m["neg_t_meta"] = tm
        in_maps.append(m)
    return in_maps


def kernel(**ins):
    in_maps = _host_inputs(ins)
    nc = build_nc()
    res = run_bass_kernel_spmd(nc, in_maps, core_ids=list(range(8)))
    out = np.concatenate([np.asarray(res.results[c]["out"]) for c in range(8)],
                         axis=0)
    return out.reshape(2, 4096, 2048).astype(np.float32)


# version 23
# speedup vs baseline: 1.0057x; 1.0057x over previous
import os as _os
import numpy as np
import concourse.bass as bass
import concourse.mybir as mybir
from concourse.bass_utils import run_bass_kernel_spmd
from concourse.bass_core_rust import IndirectOffsetOnAxis

F32 = mybir.dt.float32
I32 = mybir.dt.int32
BF16 = mybir.dt.bfloat16
AF = mybir.ActivationFunctionType
ALU = mybir.AluOpType
AX = mybir.AxisListType

ENGINES = ("tensor", "vector", "scalar", "gpsimd", "sync")
NB = 8
NH = 16
NE = 8
NT = 11
BIG = 1.0e7


class Sem:
    __slots__ = ("h", "v")

    def __init__(self, nc, name):
        self.h = nc.alloc_semaphore(name)
        self.v = 0


class Bank:
    def __init__(self, nc, name, ap):
        self.ap = ap
        self.pe = Sem(nc, name + "p")
        self.rd = Sem(nc, name + "r")


class Prog:
    def __init__(self, nc):
        self.nc = nc
        self.ops = {e: [] for e in ENGINES}

    def op(self, eng, fn, *incs):
        for s, n in incs:
            s.v += n
        pairs = tuple((s.h, n) for s, n in incs)

        def g(E, fn=fn, pairs=pairs):
            ins = fn(E)
            if pairs:
                ins.then_inc(*pairs[0])
                for h, n in pairs[1:]:
                    E.nop().then_inc(h, n)
        self.ops[eng].append(g)

    def wait(self, eng, sem, val=None):
        v = sem.v if val is None else val
        self.ops[eng].append(lambda E, s=sem.h, v=v: E.wait_ge(s, v))


class Stream:

    def __init__(self, nc, p, name, queue, bufs):
        self.p = p
        self.queue = queue
        self.bufs = bufs
        self.slots = [Sem(nc, f"{name}_s{i}") for i in range(len(bufs))]
        self.free = Sem(nc, f"{name}_f")
        self.rounds = 0

    def load(self, dma_fns, gates=()):
        d = len(self.bufs)
        slot = self.rounds % d
        for g in gates:
            self.p.wait(self.queue, g[0], g[1])
        if self.rounds >= d:
            self.p.wait(self.queue, self.free, self.rounds - d + 1)
        s = self.slots[slot]
        for fn in dma_fns:
            self.p.op(self.queue, (lambda E, f=fn, b=self.bufs[slot]: f(E, b)),
                      (s, 16))
        self.rounds += 1
        return slot, s.v

    def buf(self, slot):
        return self.bufs[slot]


def build_nc(taps=()):
    nc = bass.Bass("TRN2", target_bir_lowering=False,
                   detect_race_conditions=False)
    p = Prog(nc)

    d_x = nc.dram_tensor("xs", [1024, 2048], F32, kind="ExternalInput").ap()
    d_wqkv = nc.dram_tensor("wqkv_s", [48, 128, 16, 128], F32, kind="ExternalInput").ap()
    d_bqkv = nc.dram_tensor("bqkv_c", [128, 48], F32, kind="ExternalInput").ap()
    d_wout = nc.dram_tensor("wout", [2048, 2048], F32, kind="ExternalInput").ap()
    d_boutb = nc.dram_tensor("bout_bc", [128, 2048], F32, kind="ExternalInput").ap()
    d_slops = nc.dram_tensor("slopes_bc", [128, 16], F32, kind="ExternalInput").ap()
    d_wgt = nc.dram_tensor("wgt", [2048, 8], F32, kind="ExternalInput").ap()
    d_w12 = nc.dram_tensor("w12s", [8, 11, 128, 2, 16, 128], F32,
                           kind="ExternalInput").ap()
    d_w3 = nc.dram_tensor("w3t", [8, 4, 128, 11, 512], F32, kind="ExternalInput").ap()
    d_diff = nc.dram_tensor("diffx", [128, 128], F32, kind="ExternalInput").ap()
    d_iota = nc.dram_tensor("iota_bc", [128, 128], F32, kind="ExternalInput").ap()
    d_cols = nc.dram_tensor("cols", [128, 2], F32, kind="ExternalInput").ap()
    d_tmet = nc.dram_tensor("neg_t_meta", [128, 4], F32, kind="ExternalInput").ap()
    d_ident = nc.dram_tensor("ident", [128, 128], F32, kind="ExternalInput").ap()
    d_blkbc = nc.dram_tensor("blk_bc", [128, 8], F32, kind="ExternalInput").ap()
    d_iota8 = nc.dram_tensor("iota8_bc", [128, 8], F32, kind="ExternalInput").ap()

    d_out = nc.dram_tensor("out", [1024, 2048], F32, kind="ExternalOutput").ap()

    d_astage = nc.dram_tensor("a_stage", [NH * 128, 128], F32, kind="Internal")
    d_aall1 = nc.dram_tensor("a_all1", [4 * 8 * 128, 128], F32, kind="Internal")
    d_aall2 = nc.dram_tensor("a_all2", [4 * 8 * 128, 128], F32, kind="Internal")
    d_qspill = nc.dram_tensor("q_spill", [NH, 128, 1024], BF16, kind="Internal").ap()
    d_xnsp = nc.dram_tensor("xn_sp", [1024, 2048], BF16, kind="Internal").ap()
    d_xperm = nc.dram_tensor("xn_perm", [3073, 2176], BF16, kind="Internal").ap()
    d_yperm = nc.dram_tensor("y_perm", [3073, 2048], F32, kind="Internal").ap()

    BASE = 16512
    LIMIT = 229344

    def sb(name, shape, dtype, rel):
        nbytes = int(np.prod(shape[1:])) * mybir.dt.size(dtype)
        assert BASE + rel + nbytes <= LIMIT, (name, rel, nbytes)
        return nc.alloc_sbuf_tensor_at(name, shape, dtype, offset=BASE + rel)

    cur = 0
    xn_fm = sb("xn_fm", [128, 16, 1024], BF16, cur); cur += 32768
    OUT0 = cur
    out_tm = sb("out_tm", [128, 8, 2048], F32, OUT0); cur += 65536
    sq_scr = sb("sq_scr", [128, 2048], F32, OUT0)
    xn_tm2 = [sb("xn_tm", [128, 2048], BF16, OUT0 + 8192),
              sb("xn_tmB", [128, 2048], BF16, OUT0 + 16384)]
    s_bf_all = sb("s_bf_all", [128, 8, 128], BF16, OUT0 + 12288)
    s_f32 = sb("s_f32", [128, 128], F32, OUT0 + 14336)
    OFM0 = cur
    o_fm = sb("o_fm", [128, 16, 1024], BF16, OFM0); cur += 32768
    hh = sb("hh", [128, 11, 384], BF16, OFM0)
    sil = sb("sil", [128, 384], F32, OFM0 + 8448)
    xn_ge2 = [sb("xn_ge", [128, 16, 384], BF16, OFM0 + 10496),
              sb("xn_geB", [128, 16, 384], BF16, 14848)]
    xn_g = [sb(f"xn_g{i}", [128, 2176], BF16, 4352 * i) for i in range(2)]
    ybounce = [sb(f"ybounce{i}", [128, 512], F32, 8704 + 2048 * i)
               for i in range(3)]
    yg = [sb(f"yg{i}", [128, 2048], F32, 14848 + 8192 * i) for i in range(2)]
    BOUT0 = cur
    bout_bc = sb("bout_bc", [128, 2048], F32, cur); cur += 8192
    ident_f = sb("ident_f", [128, 128], F32, cur); cur += 512
    ident_b = sb("ident_b", [128, 128], BF16, cur); cur += 256
    diffx = sb("diffx", [128, 128], F32, cur); cur += 512
    iota_bc = sb("iota_bc", [128, 128], F32, cur); cur += 512
    slop = sb("slop", [128, 16], F32, cur); cur += 64
    negs = sb("negs", [128, 16], F32, cur); cur += 64
    cols = sb("cols", [128, 2], F32, cur); cur += 32
    tmet = sb("tmet", [128, 4], F32, cur); cur += 32
    cfr = sb("cfr", [128, 4, 16], F32, cur); cur += 256
    d_all = sb("d_all", [128, 16], F32, cur); cur += 64
    w_tm = sb("w_tm", [128, 8, 8], F32, cur); cur += 256
    dpow = sb("dpow", [128, 1], F32, cur); cur += 32
    blkbc = sb("blkbc", [128, 8], F32, cur); cur += 32
    dpa = sb("dpa", [128, 8], F32, cur); cur += 32
    gt_all = sb("gt_all", [128, 64], F32, cur); cur += 256
    iota8 = sb("iota8", [128, 8], F32, cur); cur += 32
    route_ind = sb("route_ind", [128, 8, 16], F32, cur); cur += 512
    route_sc = sb("route_sc", [128, 8, 8], F32, cur); cur += 256
    w_g = [sb(f"w_g{i}", [128, 3], F32, cur + 32 * i) for i in range(2)]
    cur += 64
    sel_all = sb("sel_all", [128, 64], F32, cur); cur += 256
    idx1_i = sb("idx1_i", [128, 8], I32, cur); cur += 32
    idx2_i = sb("idx2_i", [128, 8], I32, cur); cur += 32
    s_in = sb("s_in", [128, 128], F32, cur); cur += 512
    s_in_b = sb("s_in_b", [128, 128], BF16, cur); cur += 256
    qd_b = sb("qd_b", [128, 128], BF16, cur); cur += 256
    stat = sb("stat", [128, 8], F32, cur); cur += 32
    bq_c = sb("bq_c", [128, 48], F32, cur); cur += 192
    wg_sb = sb("wg_sb", [128, 16, 8], BF16, cur); cur += 256
    smb = [sb(f"smb{i}", [128, 128], BF16, cur + 256 * i) for i in range(3)]
    cur += 768
    cur = (cur + 63) & ~63

    A0 = cur
    q_fmB = sb("q_fmB", [128, 1024], BF16, A0)
    q2_fm = sb("q2_fm", [128, 1024], BF16, A0 + 2048)
    k_fmB = sb("k_fmB", [128, 1024], BF16, A0 + 4096)
    v_fmB = sb("v_fmB", [128, 1024], BF16, A0 + 6144)
    k2_tm = sb("k2_tm", [128, 8, 128], BF16, A0 + 8192)
    v_tm = sb("v_tm", [128, 8, 128], BF16, A0 + 10240)
    wqb = [sb(f"wqb{i}", [128, 16, 128], BF16, A0 + 12288 + 4096 * i)
           for i in range(2)]
    w12b = [sb(f"w12b{i}", [128, 2, 16, 128], BF16, A0 + 8192 * i)
            for i in range(2)]
    cur += 20480
    B0 = cur
    xbuf = [sb(f"xbuf{i}", [128, 2048], F32, B0 + 8192 * i) for i in range(2)]
    woutb = [sb("woutb0", [128, 16, 512], BF16, B0)]
    cur += 16384
    C0 = cur
    woutb.append(sb("woutb1", [128, 16, 512], BF16, C0))
    w3b = [sb(f"w3b{i}", [128, 11, 512], BF16, C0 + 11264 * i) for i in range(2)]
    stage = [sb(f"stage{i}", [128, 2176], BF16, C0 + 4352 * i)
             for i in range(2)]
    cur += 22528
    D0 = cur
    g_sb = sb("g_sb", [8, 1024], F32, D0)
    qre = [sb(f"qre{i}", [128, 1024], BF16, D0 + 2048 * i) for i in range(2)]
    qre.append(sb("qre2", [128, 1024], BF16, OUT0 + 22528))
    cur += 4096
    a4 = [sb(f"a4{i}", [128, 4, 128], F32, OUT0 + 16384 + 2048 * i)
          for i in range(3)]
    assert BASE + cur <= LIMIT, cur

    pf = [Bank(nc, f"pf{i}", nc.alloc_psum_tensor(f"pf{i}", [128, 512], F32).ap())
          for i in range(6)]
    pbf = [Bank(nc, f"pbf{i}",
                nc.alloc_psum_tensor(f"pbf{i}", [128, 1024], BF16).ap())
           for i in range(2)]

    c_ld = Sem(nc, "c_ld")
    dv = Sem(nc, "dv")
    ac = Sem(nc, "ac")
    a_st = Sem(nc, "a_st")
    q_st = Sem(nc, "q_st")
    o_st = Sem(nc, "o_st")
    cc_s = Sem(nc, "cc_s")
    xpz = Sem(nc, "xpz")
    pcast = Sem(nc, "pcast")
    xnsp = Sem(nc, "xnsp")
    rsc = Sem(nc, "rsc")
    sgl = Sem(nc, "sgl")
    swc = Sem(nc, "swc")
    yz = Sem(nc, "yz")
    yst = Sem(nc, "yst")
    ygl = Sem(nc, "ygl")

    st_x = Stream(nc, p, "stx", "sync", xbuf)
    st_wq = Stream(nc, p, "stwq", "gpsimd", wqb)
    w12b.append(sb("w12b2", [128, 2, 16, 128], BF16, BOUT0))
    st_w12 = Stream(nc, p, "stw12", "gpsimd", w12b)
    st_wo = Stream(nc, p, "stwo", "gpsimd", woutb)
    st_w3 = Stream(nc, p, "stw3", "gpsimd", w3b)
    st_a4 = Stream(nc, p, "sta4", "sync", a4)
    st_qr = Stream(nc, p, "stqr", "sync", qre)

    def T(fn, *incs): p.op("tensor", fn, *incs)
    def V(fn, *incs): p.op("vector", fn, *incs)
    def A(fn, *incs): p.op("scalar", fn, *incs)
    def G(fn, *incs): p.op("gpsimd", fn, *incs)
    def SY(fn, *incs): p.op("sync", fn, *incs)
    def Tw(s, v=None): p.wait("tensor", s, v)
    def Vw(s, v=None): p.wait("vector", s, v)
    def Aw(s, v=None): p.wait("scalar", s, v)
    def Gw(s, v=None): p.wait("gpsimd", s, v)
    def SYw(s, v=None): p.wait("sync", s, v)

    for dst, src in ((ident_f, d_ident), (ident_b, d_ident), (diffx, d_diff),
                     (iota_bc, d_iota), (slop, d_slops), (cols, d_cols),
                     (tmet, d_tmet), (bout_bc, d_boutb), (bq_c, d_bqkv),
                     (blkbc, d_blkbc), (iota8, d_iota8)):
        G(lambda E, dst=dst, src=src: E.dma_start(dst[:], src), (c_ld, 16))
    G(lambda E: E.dma_start(wg_sb[:], d_wgt.rearrange("(c p) e -> p c e", p=128)),
      (c_ld, 16))
    n_const = 12
    _pc = []
    V(lambda E: E.memset(stage[0][:], 0.0), (dv, 1))
    V(lambda E: E.memset(stage[1][:], 0.0), (dv, 1))
    dv_stz = dv.v
    Aw(dv, dv_stz)
    for k in range(25):
        r0 = k * 128
        n = 128 if k < 24 else 1
        A(lambda E, r0=r0, n=n: E.dma_start(
            d_xperm[r0:r0 + n, :], stage[0][0:n, :]), (xpz, 16))
    Vw(c_ld, 16 * n_const)
    V(lambda E: E.tensor_scalar(negs[:], slop[:], -1.0, None, ALU.mult), (dv, 1))
    Aw(dv)
    for r in range(4):
        A(lambda E, r=r: E.activation(cfr[:, r, :], slop[:], AF.Exp,
                                      scale=tmet[:, r:r + 1]), (ac, 1))
    for h in range(NH):
        A(lambda E, h=h: E.activation(d_all[:, h:h + 1], cols[:, 1:2], AF.Exp,
                                      scale=negs[:, h:h + 1]), (ac, 1))

    for blk in range(NB):
        slot, wv = st_x.load([lambda E, b, blk=blk: E.dma_start(
            b[:], d_x[blk * 128:(blk + 1) * 128, :])])
        xb = st_x.buf(slot)
        xtm = xn_tm2[blk % 2]
        Vw(st_x.slots[slot], wv)
        V(lambda E, xb=xb: E.reduce_sum(stat[:, 0:1], xb[:], axis=AX.X), (dv, 1))
        Aw(st_x.slots[slot], wv)
        A(lambda E, xb=xb: E.activation(
            sq_scr[:], xb[:], AF.Square, accum_out=stat[:, 1:2]), (ac, 1))
        Vw(dv)
        V(lambda E: E.tensor_scalar(stat[:, 2:3], stat[:, 0:1], 1.0 / 2048,
                                    None, ALU.mult), (dv, 1))
        Vw(ac)
        V(lambda E: E.tensor_scalar(stat[:, 3:4], stat[:, 1:2], 1.0 / 2048,
                                    None, ALU.mult), (dv, 1))
        Vw(dv)
        V(lambda E: E.scalar_tensor_tensor(stat[:, 4:5], stat[:, 2:3],
                                           stat[:, 2:3], stat[:, 3:4],
                                           ALU.mult, ALU.subtract), (dv, 1))
        Vw(dv)
        V(lambda E: E.tensor_scalar(stat[:, 5:6], stat[:, 4:5], -1.0, 1.0e-5,
                                    ALU.mult, ALU.add), (dv, 1))
        Aw(dv)
        A(lambda E: E.activation(stat[:, 5:6], stat[:, 5:6], AF.Ln), (ac, 1))
        Aw(ac)
        A(lambda E: E.activation(stat[:, 6:7], stat[:, 5:6], AF.Exp,
                                 scale=-0.5), (ac, 1))
        Vw(ac)
        Vw(xnsp, 16 * max(0, blk - 1))
        V(lambda E, xb=xb, xtm=xtm: E.tensor_scalar(
            xtm[:], xb[:], stat[:, 2:3], stat[:, 6:7],
            ALU.subtract, ALU.mult), (dv, 1), (st_x.free, 1))
        SYw(dv)
        SY(lambda E, blk=blk, xtm=xtm: E.dma_start(
            d_xnsp[blk * 128:(blk + 1) * 128, :], xtm[:]), (xnsp, 16))
        if blk == 0:
            Tw(c_ld, 16 * n_const)
        for half in range(2):
            bank = pbf[half]
            Tw(dv)
            Tw(bank.rd)
            for c8 in range(8):
                c = half * 8 + c8
                T(lambda E, c=c, c8=c8, bank=bank, xtm=xtm: E.transpose(
                    bank.ap[:, c8 * 128:(c8 + 1) * 128],
                    xtm[:, c * 128:(c + 1) * 128], ident_b[:]),
                  *(((bank.pe, 1),) if c8 == 7 else ()))
            Aw(bank.pe)
            A(lambda E, half=half, blk=blk, bank=bank: E.copy(
                xn_fm[:, half * 8:half * 8 + 8, blk * 128:(blk + 1) * 128],
                bank.ap.rearrange("p (c m) -> p c m", c=8)),
              (bank.rd, 1), (ac, 1))
        if blk == 3:
            ac_ln_half = ac.v
    ac_ln_done = ac.v
    dv_ln_done = dv.v

    for ic in range(2):
        bank = pf[2 + ic]
        Tw(ac, ac_ln_half if ic == 0 else ac_ln_done)
        Tw(bank.rd)
        for dd in range(16):
            T(lambda E, dd=dd, ic=ic, bank=bank: E.matmul(
                bank.ap[0:8, :], wg_sb[:, dd, :],
                xn_fm[:, dd, ic * 512:(ic + 1) * 512],
                start=(dd == 0), stop=(dd == 15)),
              *(((bank.pe, 1),) if dd == 15 else ()))
        Vw(bank.pe)
        V(lambda E, ic=ic, bank=bank: E.tensor_copy(
            g_sb[:, ic * 512:(ic + 1) * 512], bank.ap[0:8, :]),
          (bank.rd, 1), (dv, 1))
    bank = pf[4]
    Tw(dv)
    Tw(bank.rd)
    for blk in range(NB):
        T(lambda E, blk=blk, bank=bank: E.transpose(
            bank.ap[:, blk * 8:(blk + 1) * 8],
            g_sb[:, blk * 128:(blk + 1) * 128],
            ident_f[0:8, 0:8]), *(((bank.pe, 1),) if blk == 7 else ()))
    Vw(bank.pe)
    V(lambda E, bank=bank: E.tensor_copy(gt_all[:], bank.ap[:, 0:64]),
      (bank.rd, 1), (dv, 1))
    for blk in range(NB):
        gt = gt_all[:, blk * 8:(blk + 1) * 8]
        i1 = route_ind[:, blk, 0:8]
        i2 = route_ind[:, blk, 8:16]
        Vw(dv)
        V(lambda E, gt=gt: E.reduce_max(dpow[:], gt, axis=AX.X), (dv, 1))
        Vw(dv)
        V(lambda E, gt=gt, i1=i1: E.tensor_scalar(i1, gt, dpow[:],
                                                  None, ALU.is_ge), (dv, 1))
        Vw(dv)
        V(lambda E, gt=gt, i1=i1: E.scalar_tensor_tensor(
            s_in[:, 8:16], i1, -1.0e30, gt, ALU.mult, ALU.add), (dv, 1))
        Vw(dv)
        V(lambda E: E.reduce_max(s_in[:, 16:17], s_in[:, 8:16], axis=AX.X),
          (dv, 1))
        Vw(dv)
        V(lambda E, i2=i2: E.tensor_scalar(i2, s_in[:, 8:16],
                                           s_in[:, 16:17], None, ALU.is_ge),
          (dv, 1))
        V(lambda E: E.tensor_tensor(s_in[:, 17:18], dpow[:], s_in[:, 16:17],
                                    ALU.subtract), (dv, 1))
        Aw(dv)
        A(lambda E, blk=blk: E.activation(route_sc[:, blk, 2:3],
                                          s_in[:, 17:18], AF.Sigmoid),
          (ac, 1))
        Vw(ac)
        V(lambda E, blk=blk: E.tensor_scalar(
            route_sc[:, blk, 3:4], route_sc[:, blk, 2:3], -1.0, 1.0,
            ALU.mult, ALU.add), (dv, 1))
        V(lambda E, blk=blk, i1=i1: E.tensor_scalar(
            s_in[:, 32:40], i1, route_sc[:, blk, 2:3], None, ALU.mult),
          (dv, 1))
        Vw(dv)
        V(lambda E, blk=blk, i2=i2: E.scalar_tensor_tensor(
            w_tm[:, blk, :], i2, route_sc[:, blk, 3:4], s_in[:, 32:40],
            ALU.mult, ALU.add), (dv, 1))
        V(lambda E, i1=i1: E.tensor_tensor(s_in[:, 40:48], i1, iota8[:],
                                           ALU.mult), (dv, 1))
        Vw(dv)
        V(lambda E, blk=blk: E.reduce_sum(route_sc[:, blk, 0:1],
                                          s_in[:, 40:48], axis=AX.X), (dv, 1))
        V(lambda E, i2=i2: E.tensor_tensor(s_in[:, 40:48], i2, iota8[:],
                                           ALU.mult), (dv, 1))
        Vw(dv)
        V(lambda E, blk=blk: E.reduce_sum(route_sc[:, blk, 1:2],
                                          s_in[:, 40:48], axis=AX.X), (dv, 1))
        V(lambda E, blk=blk, i1=i1, i2=i2: E.tensor_tensor(
            sel_all[:, blk * 8:(blk + 1) * 8], i1, i2, ALU.add), (dv, 1))


    for grp in range(2):
        bank = pf[2 + grp]
        Tw(dv)
        Tw(bank.rd)
        for b4 in range(4):
            blk = grp * 4 + b4
            T(lambda E, blk=blk, b4=b4, bank=bank: E.transpose(
                bank.ap[0:8, b4 * 128:(b4 + 1) * 128],
                sel_all[:, blk * 8:(blk + 1) * 8], ident_f[0:128, 0:128]),
              *(((bank.pe, 1),) if b4 == 3 else ()))
        Vw(bank.pe)
        V(lambda E, grp=grp, bank=bank: E.tensor_copy(
            g_sb[:, grp * 512:(grp + 1) * 512], bank.ap[0:8, 0:512]),
          (bank.rd, 1), (dv, 1))
    zv = sq_scr.ap()[0:8, 0:1024]
    pv = sq_scr.ap()[0:8, 1024:2048]
    V(lambda E, zv=zv: E.memset(zv, 0.0), (dv, 1))
    Vw(dv)
    V(lambda E, zv=zv, pv=pv: E.tensor_tensor_scan(
        pv, g_sb[:], zv, 0.0, ALU.add, ALU.add), (dv, 1))
    bank = pf[4]
    Tw(dv)
    Tw(bank.rd)
    for blk in range(NB):
        T(lambda E, blk=blk, bank=bank, pv=pv: E.transpose(
            bank.ap[:, blk * 8:(blk + 1) * 8],
            pv[0:8, blk * 128:(blk + 1) * 128], ident_f[0:8, 0:8]),
          *(((bank.pe, 1),) if blk == 7 else ()))
    Vw(bank.pe)
    V(lambda E, bank=bank: E.tensor_copy(gt_all[:], bank.ap[:, 0:64]),
      (bank.rd, 1), (dv, 1))
    for blk in range(NB):
        i1 = route_ind[:, blk, 0:8]
        i2 = route_ind[:, blk, 8:16]
        pos = gt_all[:, blk * 8:(blk + 1) * 8]
        Vw(dv)
        V(lambda E, i1=i1, pos=pos: E.tensor_tensor(s_in[:, 40:48], i1, pos,
                                                    ALU.mult), (dv, 1))
        Vw(dv)
        V(lambda E, blk=blk: E.reduce_sum(route_sc[:, blk, 4:5],
                                          s_in[:, 40:48], axis=AX.X), (dv, 1))
        V(lambda E, i2=i2, pos=pos: E.tensor_tensor(s_in[:, 40:48], i2, pos,
                                                    ALU.mult), (dv, 1))
        Vw(dv)
        V(lambda E, blk=blk: E.reduce_sum(route_sc[:, blk, 5:6],
                                          s_in[:, 40:48], axis=AX.X), (dv, 1))
        for k, (pcol, icol, itgt) in enumerate(
                ((4, 0, idx1_i), (5, 1, idx2_i))):
            Vw(dv)
            V(lambda E, blk=blk, icol=icol: E.tensor_scalar(
                s_in[:, 49:50], route_sc[:, blk, icol:icol + 1], 384.0, -1.0,
                ALU.mult, ALU.add), (dv, 1))
            Vw(dv)
            V(lambda E, blk=blk, pcol=pcol: E.tensor_tensor(
                s_in[:, 50:51], s_in[:, 49:50],
                route_sc[:, blk, pcol:pcol + 1], ALU.add), (dv, 1))
            V(lambda E, blk=blk, pcol=pcol: E.tensor_scalar(
                s_in[:, 51:52], route_sc[:, blk, pcol:pcol + 1], 384.5, None,
                ALU.is_ge), (dv, 1))
            Vw(dv)
            V(lambda E: E.scalar_tensor_tensor(
                s_in[:, 50:51], s_in[:, 51:52], 1.0e9, s_in[:, 50:51],
                ALU.mult, ALU.add), (dv, 1))
            Vw(dv)
            V(lambda E: E.tensor_scalar(s_in[:, 50:51], s_in[:, 50:51],
                                        3072.0, None, ALU.min), (dv, 1))
            Vw(dv)
            V(lambda E, blk=blk, itgt=itgt: E.tensor_copy(
                itgt[:, blk:blk + 1], s_in[:, 50:51]), (dv, 1))
    dv_route = dv.v
    Vw(dv)
    V(lambda E: E.memset(sq_scr[:], 0.0), (dv, 1))
    dv_zero = dv.v
    SYw(dv, dv_zero)
    SY(lambda E: E.dma_start(d_yperm[3072:3073, :], sq_scr[0:1, :]), (yz, 16))
    _groups = []
    _og, _ov, _os = p.ops["gpsimd"], p.ops["vector"], p.ops["sync"]
    k = 0
    for blk in range(NB):
        for itgt, pcol in ((idx1_i, 2), (idx2_i, 3)):
            sl = k % 2
            _ds, _dv, _dg = [], [], []
            p.ops["sync"] = _ds
            if k == 0:
                SYw(xpz, 16 * 25)
                SYw(xnsp, 16 * 8)
            if k >= 2:
                SYw(rsc, 16 * (k - 1))
            SY(lambda E, blk=blk, sl=sl: E.dma_start(
                stage[sl][:, 0:2048],
                d_xnsp[blk * 128:(blk + 1) * 128, :]), (sgl, 16))
            p.ops["sync"] = _os
            p.ops["vector"] = _dv
            p.wait("vector", sgl, 16 * (k + 1))
            p.op("vector", (lambda E, blk=blk, sl=sl, pcol=pcol:
                            E.tensor_copy(stage[sl][:, 2048:2049],
                                          route_sc[:, blk, pcol:pcol + 1])),
                 (swc, 1))
            p.ops["vector"] = _ov
            p.ops["gpsimd"] = _dg
            Gw(swc, k + 1)
            if k == 0:
                Gw(dv, dv_route)
                Gw(xpz, 16 * 25)
            G(lambda E, blk=blk, sl=sl, itgt=itgt: E.indirect_dma_start(
                d_xperm, IndirectOffsetOnAxis(itgt[:, blk:blk + 1], 0),
                stage[sl][:], None,
                bounds_check=3072, oob_is_err=False), (rsc, 16))
            p.ops["gpsimd"] = _og
            _groups.append((_ds, _dv, _dg))
            k += 1
    rsc_done = rsc.v

    _glag = []
    for h in range(NH):
        if h == 11:
            Gw(a_st, 16 * 8)
            G(lambda E: E.collective_compute(
                "AllGather", ALU.bypass,
                replica_groups=[[0, 1, 2, 3], [4, 5, 6, 7]],
                ins=[d_astage.ap()[0:8 * 128, :].opt()],
                outs=[d_aall1.ap().opt()]), (cc_s, 1))
        if h >= 1:
            while _glag:
                p.ops["gpsimd"].append(_glag.pop(0))
            for _ in range(2):
                if _groups:
                    gs, gv, gg = _groups.pop(0)
                    p.ops["sync"].extend(gs)
                    p.ops["vector"].extend(gv)
                    _glag.extend(gg)
        Aw(dv)
        A(lambda E, h=h: E.activation(qd_b[:], iota_bc[:], AF.Exp,
                                      scale=negs[:, h:h + 1]), (ac, 1))
        A(lambda E, h=h: E.activation(stat[:, 7:8], cols[:, 0:1], AF.Exp,
                                      scale=negs[:, h:h + 1]), (ac, 1))
        A(lambda E, h=h: E.activation(s_in[:], diffx[:], AF.Exp,
                                      scale=negs[:, h:h + 1]), (ac, 1))
        ac_tab = ac.v

        for j, slabi in ((0, 16 + h), (1, 32 + h)):
            slot, wv = st_wq.load([lambda E, b, s=slabi: E.dma_start(
                b[:], d_wqkv[s])])
            wslab = st_wq.buf(slot)
            dst = k_fmB if j == 0 else v_fmB
            for ic in range(2):
                bank = pf[ic]
                Tw(st_wq.slots[slot], wv)
                if h == 0 and j == 0:
                    Tw(ac, ac_ln_half if ic == 0 else ac_ln_done)
                Tw(bank.rd)
                for dd in range(16):
                    T(lambda E, dd=dd, ic=ic, bank=bank, wslab=wslab: E.matmul(
                        bank.ap[:], wslab[:, dd, :],
                        xn_fm[:, dd, ic * 512:(ic + 1) * 512],
                        start=(dd == 0), stop=(dd == 15)),
                      *(((bank.pe, 1),) if dd == 15 else ()))
                Vw(bank.pe)
                incs = [(bank.rd, 1), (dv, 1)]
                if ic == 1:
                    incs.append((st_wq.free, 1))
                V(lambda E, dst=dst, ic=ic, bank=bank, si=slabi: E.tensor_scalar(
                    dst[:, ic * 512:(ic + 1) * 512], bank.ap[:],
                    bq_c[:, si:si + 1], None, ALU.add), *incs)

        Tw(dv)
        for j in range(2):
            bank = pbf[j]
            src = k_fmB if j == 0 else v_fmB
            Tw(bank.rd)
            for blk in range(NB):
                T(lambda E, blk=blk, src=src, bank=bank: E.transpose(
                    bank.ap[:, blk * 128:(blk + 1) * 128],
                    src[:, blk * 128:(blk + 1) * 128], ident_b[:]),
                  *(((bank.pe, 1),) if blk == 7 else ()))
        Vw(ac, ac_tab)
        Vw(pbf[0].pe)
        V(lambda E: E.tensor_scalar(
            k2_tm.ap().rearrange("p b m -> p (b m)"), pbf[0].ap[:],
            stat[:, 7:8], None, ALU.mult), (pbf[0].rd, 1), (dv, 1))
        Vw(pbf[1].pe)
        V(lambda E: E.tensor_copy(v_tm.ap().rearrange("p b m -> p (b m)"),
                                  pbf[1].ap[:]), (pbf[1].rd, 1), (dv, 1))
        dv_kv = dv.v

        def emit_scan(h=h, dv_kv=dv_kv):
            Vw(a_st, a_st.v)
            V(lambda E: E.memset(s_f32[:], 0.0), (dv, 1))
            V(lambda E: E.memset(s_bf_all.ap().rearrange("p b m -> p (b m)"),
                                 0.0), (dv, 1))
            Tw(dv)
            for i in range(NB):
                bank = pf[4 + (i % 2)]
                Tw(bank.rd)
                T(lambda E, i=i, bank=bank: E.matmul(
                    bank.ap[:, 0:128], k2_tm[:, i, :], v_tm[:, i, :],
                    start=True, stop=True), (bank.pe, 1))
                if i > 0:
                    V(lambda E, i=i: E.tensor_copy(s_bf_all[:, i, :],
                                                   s_f32[:]), (dv, 1))
                Vw(bank.pe)
                V(lambda E, h=h, bank=bank: E.scalar_tensor_tensor(
                    s_f32[:], s_f32[:], d_all[:, h:h + 1], bank.ap[:, 0:128],
                    ALU.mult, ALU.add), (bank.rd, 1), (dv, 1))
            SYw(dv)
            SY(lambda E, h=h: E.dma_start(
                d_astage.ap()[h * 128:(h + 1) * 128, :], s_f32[:]),
               (a_st, 16))

        def emit_qproj(h=h):
            slot, wv = st_wq.load([lambda E, b, s=h: E.dma_start(
                b[:], d_wqkv[s])])
            wslab = st_wq.buf(slot)
            for ic in range(2):
                bank = pf[ic]
                Tw(st_wq.slots[slot], wv)
                Tw(bank.rd)
                for dd in range(16):
                    T(lambda E, dd=dd, ic=ic, bank=bank, wslab=wslab:
                      E.matmul(
                          bank.ap[:], wslab[:, dd, :],
                          xn_fm[:, dd, ic * 512:(ic + 1) * 512],
                          start=(dd == 0), stop=(dd == 15)),
                      *(((bank.pe, 1),) if dd == 15 else ()))
                Vw(bank.pe)
                incs = [(bank.rd, 1), (dv, 1)]
                if ic == 1:
                    incs.append((st_wq.free, 1))
                V(lambda E, ic=ic, bank=bank, h=h: E.tensor_scalar(
                    q_fmB[:, ic * 512:(ic + 1) * 512], bank.ap[:],
                    bq_c[:, h:h + 1], None, ALU.add), *incs)
            return dv.v

        def emit_q2(h=h):
            Vw(q_st, q_st.v)
            Vw(ac, ac_tab)
            Vw(dv)
            for blk in range(NB):
                V(lambda E, blk=blk: E.tensor_tensor(
                    q2_fm[:, blk * 128:(blk + 1) * 128],
                    q_fmB[:, blk * 128:(blk + 1) * 128], qd_b[:], ALU.mult),
                  (dv, 1))
            SYw(dv)
            SY(lambda E, h=h: E.dma_start(d_qspill[h], q2_fm[:]), (q_st, 16))

        if h == NH - 1:
            emit_scan()
            dv_qevac = emit_qproj()
            emit_q2()
        else:
            dv_qevac = emit_qproj()
            emit_scan()
            emit_q2()

        def emit_scoresT(i):
            sc_bank = pf[2 + (i % 2)]
            Tw(sc_bank.rd)
            T(lambda E, i=i, bank=sc_bank: E.matmul(
                bank.ap[:, 0:128], k_fmB[:, i * 128:(i + 1) * 128],
                q_fmB[:, i * 128:(i + 1) * 128], start=True, stop=True),
              (sc_bank.pe, 1))

        Tw(dv, dv_qevac)
        emit_scoresT(0)
        emit_scoresT(1)
        for i in range(NB):
            sc_bank = pf[2 + (i % 2)]
            o_bank = pf[4 + (i % 2)]
            Vw(sc_bank.pe)
            V(lambda E, i=i, bank=sc_bank: E.tensor_tensor(
                smb[i % 3][:], bank.ap[:, 0:128], s_in[:], ALU.mult),
              (sc_bank.rd, 1), (dv, 1))
            Tw(dv)
            Tw(o_bank.rd)
            T(lambda E, i=i, bank=o_bank: E.matmul(
                bank.ap[:, 0:128], s_bf_all[:, i, :],
                q2_fm[:, i * 128:(i + 1) * 128], start=True, stop=False))
            T(lambda E, i=i, bank=o_bank: E.matmul(
                bank.ap[:, 0:128], v_tm[:, i, :], smb[i % 3][:],
                start=False, stop=True), (o_bank.pe, 1))
            if i + 2 < NB:
                emit_scoresT(i + 2)
            Vw(o_bank.pe)
            V(lambda E, i=i, h=h, bank=o_bank: E.tensor_copy(
                o_fm[:, h, i * 128:(i + 1) * 128], bank.ap[:, 0:128]),
              (o_bank.rd, 1), (dv, 1))
    while _groups:
        gs, gv, gg = _groups.pop(0)
        p.ops["sync"].extend(gs)
        p.ops["vector"].extend(gv)
        _glag.extend(gg)
    while _glag:
        p.ops["gpsimd"].append(_glag.pop(0))
    dv_pass1_done = dv.v

    if "o1" in taps:
        d_do1 = nc.dram_tensor("dbg_o1", [128, 16 * 1024], BF16,
                               kind="ExternalOutput").ap()
        tsem = Sem(nc, "tap_o1")
        SYw(dv, dv_pass1_done)
        SY(lambda E: E.dma_start(
            d_do1, o_fm.ap().rearrange("p c m -> p (c m)")), (tsem, 16))
        p.wait("vector", tsem, 16)

    Gw(a_st, 16 * 16)
    G(lambda E: E.collective_compute(
        "AllGather", ALU.bypass,
        replica_groups=[[0, 1, 2, 3], [4, 5, 6, 7]],
        ins=[d_astage.ap()[8 * 128:16 * 128, :].opt()],
        outs=[d_aall2.ap().opt()]), (cc_s, 1))

    aall_v1 = d_aall1.ap().rearrange("(r h p) m -> r h p m", r=4, h=8)
    aall_v2 = d_aall2.ap().rearrange("(r h p) m -> r h p m", r=4, h=8)
    for h in range(NH):
        av = aall_v1 if h < 8 else aall_v2
        hh_i = h % 8
        gates = ()
        if h == 0:
            gates = ((cc_s, 1),)
        elif h == 8:
            gates = ((cc_s, 2),)
        slot, wv = st_a4.load(
            [lambda E, b, av=av, hh_i=hh_i: E.dma_start(
                b[:], av[:, hh_i].rearrange("r p m -> p r m"))], gates=gates)
        slotq, wvq = st_qr.load(
            [lambda E, b, h=h: E.dma_start(b[:], d_qspill[h])],
            gates=((q_st, q_st.v),) if h == 0 else ())
        a4b = st_a4.buf(slot)
        Vw(st_a4.slots[slot], wv)
        V(lambda E, a4b=a4b, h=h: E.tensor_scalar(
            s_in[:], a4b[:, 0, :], cfr[:, 0, h:h + 1], None, ALU.mult), (dv, 1))
        for r in range(1, 4):
            incs = [(dv, 1)]
            if r == 3:
                incs.append((st_a4.free, 1))
            V(lambda E, a4b=a4b, h=h, r=r: E.scalar_tensor_tensor(
                s_in[:], a4b[:, r, :], cfr[:, r, h:h + 1], s_in[:],
                ALU.mult, ALU.add), *incs)
        Aw(dv)
        A(lambda E: E.copy(s_in_b[:], s_in[:]), (ac, 1))
        A(lambda E, h=h: E.activation(dpa[:], blkbc[:], AF.Exp,
                                      scale=negs[:, h:h + 1]), (ac, 1))
        qreb = st_qr.buf(slotq)
        for ic in range(2):
            bank = pf[ic]
            Tw(ac)
            Tw(st_qr.slots[slotq], wvq)
            Tw(bank.rd)
            T(lambda E, ic=ic, bank=bank, qreb=qreb: E.matmul(
                bank.ap[:], s_in_b[:], qreb[:, ic * 512:(ic + 1) * 512],
                start=True, stop=True),
              (bank.pe, 1), *(((st_qr.free, 1),) if ic == 1 else ()))
            Vw(bank.pe)
            Vw(ac)
            for bi in range(4):
                i = ic * 4 + bi
                incs = [(dv, 1)]
                if bi == 3:
                    incs.append((bank.rd, 1))
                V(lambda E, i=i, bi=bi, h=h, bank=bank: E.scalar_tensor_tensor(
                    o_fm[:, h, i * 128:(i + 1) * 128],
                    bank.ap[:, bi * 128:(bi + 1) * 128], dpa[:, i:i + 1],
                    o_fm[:, h, i * 128:(i + 1) * 128], ALU.mult, ALU.add),
                  *incs)
    dv_pass2_done = dv.v

    if "o2" in taps:
        d_do2 = nc.dram_tensor("dbg_o2", [128, 16 * 1024], BF16,
                               kind="ExternalOutput").ap()
        tsem = Sem(nc, "tap_o2")
        SYw(dv, dv_pass2_done)
        SY(lambda E: E.dma_start(
            d_do2, o_fm.ap().rearrange("p c m -> p (c m)")), (tsem, 16))
        p.wait("vector", tsem, 16)

    for fc in range(4):
        dmas = [
            (lambda E, b, fc=fc, cg=cg: E.dma_start(
                b[:, cg * 4:(cg + 1) * 4, :],
                d_wout[cg * 512:(cg + 1) * 512, fc * 512:(fc + 1) * 512]
                .rearrange("(g p) f -> p g f", p=128)))
            for cg in range(4)]
        gates = ((dv, dv_ln_done),) if fc == 0 else ()
        slot, wv = st_wo.load(dmas, gates=gates)
        wob = st_wo.buf(slot)
        Tw(st_wo.slots[slot], wv)
        if fc == 0:
            Tw(dv, dv_pass2_done)
        for blkset in ((0, 1, 2, 3, 4, 5), (6, 7)):
            for bi in range(len(blkset)):
                Tw(pf[bi].rd)
            for c in range(16):
                for bi, blk in enumerate(blkset):
                    bank = pf[bi]
                    T(lambda E, c=c, blk=blk, bank=bank, wob=wob: E.matmul(
                        bank.ap[:], o_fm[:, c, blk * 128:(blk + 1) * 128],
                        wob[:, c, :], start=(c == 0), stop=(c == 15)),
                      *(((bank.pe, 1),) if c == 15 else ()))
            for bi, blk in enumerate(blkset):
                bank = pf[bi]
                Vw(bank.pe)
                incs = [(bank.rd, 1), (dv, 1)]
                if blk == 7:
                    incs.append((st_wo.free, 1))
                V(lambda E, blk=blk, fc=fc, bank=bank: E.tensor_tensor(
                    out_tm[:, blk, fc * 512:(fc + 1) * 512], bank.ap[:],
                    bout_bc[:, fc * 512:(fc + 1) * 512], ALU.add), *incs)
    dv_att_done = dv.v

    if "att" in taps:
        d_datt = nc.dram_tensor("dbg_att", [1024, 2048], F32,
                                kind="ExternalOutput").ap()
        tsem = Sem(nc, "tap_att")
        SYw(dv, dv_att_done)
        SY(lambda E: E.dma_start(
            d_datt.rearrange("(b p) f -> p b f", p=128),
            out_tm[:]), (tsem, 16))
        p.wait("vector", tsem, 16)

    st_xg = Stream(nc, p, "stxg", "sync", xn_g)
    ybu = [0] * 4
    dv_xg_parity = [0, 0]

    def emit_gather_loads(e):
        par = e % 2
        toks = []
        for blk in range(3):
            gates = ()
            if e == 0 and blk == 0:
                gates = ((rsc, rsc_done), (dv, dv_pass1_done))
            r0 = e * 384 + blk * 128
            sx, wvx = st_xg.load(
                [lambda E, b, r0=r0: E.dma_start(
                    b[:], d_xperm[r0:r0 + 128, :])], gates=gates)
            toks.append((sx, wvx))
        return toks

    def emit_gather_transp(e, toks):
        par = e % 2
        xg = xn_ge2[par]
        for blk in range(3):
            sx, wvx = toks[blk]
            xgb = st_xg.buf(sx)
            Vw(st_xg.slots[sx], wvx)
            V(lambda E, xgb=xgb, par=par, blk=blk: E.tensor_copy(
                w_g[par][:, blk:blk + 1], xgb[:, 2048:2049]), (dv, 1))
            for half in range(2):
                bank = pbf[half]
                Tw(st_xg.slots[sx], wvx)
                if e == 0 and blk == 0 and half == 0:
                    Tw(dv, dv_att_done)
                Tw(bank.rd)
                for c8 in range(8):
                    ch = half * 8 + c8
                    T(lambda E, ch=ch, c8=c8, bank=bank, xgb=xgb: E.transpose(
                        bank.ap[:, c8 * 128:(c8 + 1) * 128],
                        xgb[:, ch * 128:(ch + 1) * 128], ident_b[:]),
                      *(((bank.pe, 1),) if c8 == 7 else ()))
                Aw(bank.pe)
                incs = [(bank.rd, 1), (ac, 1)]
                if half == 1:
                    incs.append((st_xg.free, 1))
                A(lambda E, half=half, blk=blk, bank=bank, xg=xg: E.copy(
                    xg[:, half * 8:half * 8 + 8, blk * 128:(blk + 1) * 128],
                    bank.ap.rearrange("p (c m) -> p c m", c=8)), *incs)
        dv_xg_parity[par] = ac.v

    toks_cur = emit_gather_loads(0)
    emit_gather_transp(0, toks_cur)
    for e in range(NE):
        if e + 1 < NE:
            toks_nxt = emit_gather_loads(e + 1)
        par = e % 2
        xg = xn_ge2[par]
        dv_xg = dv_xg_parity[par]

        for t in range(NT):
            g0 = ((q_st, q_st.v),) if (e == 0 and t == 0) else ()
            if e == 0 and t == 2:
                g0 = ((dv, dv_att_done),)
            s1, wv1 = st_w12.load(
                [lambda E, b, e=e, t=t: E.dma_start(
                    b[:], d_w12[e, t])],
                gates=g0)
            b1 = pf[0] if t % 2 == 0 else pf[2]
            b2 = pf[1] if t % 2 == 0 else pf[3]
            wb = st_w12.buf(s1)
            for mi, bank in ((0, b1), (1, b2)):
                Tw(st_w12.slots[s1], wv1)
                if t == 0:
                    Tw(ac, dv_xg)
                Tw(bank.rd)
                for dd in range(16):
                    T(lambda E, dd=dd, mi=mi, bank=bank, wb=wb, xg=xg:
                      E.matmul(
                          bank.ap[:, 0:384], wb[:, mi, dd, :],
                          xg[:, dd, 0:384],
                          start=(dd == 0), stop=(dd == 15)),
                      *(((bank.pe, 1),) if dd == 15 else ()))
            Aw(b1.pe)
            A(lambda E, b1=b1: E.activation(
                sil[:], b1.ap[:, 0:384], AF.Silu), (ac, 1), (b1.rd, 1))
            Vw(ac)
            Vw(b2.pe)
            V(lambda E, t=t, b2=b2: E.tensor_tensor(
                hh[:, t, :], sil[:], b2.ap[:, 0:384], ALU.mult),
              (b2.rd, 1), (dv, 1), (st_w12.free, 1))
        dv_h_done = dv.v

        if e + 1 < NE:
            emit_gather_transp(e + 1, toks_nxt)

        for fc in range(4):
            dmas = [lambda E, b, e=e, fc=fc: E.dma_start(
                b[:], d_w3[e, fc])]
            g3 = ((dv, dv_att_done), (rsc, rsc_done)) \
                if (e == 0 and fc == 0) else ()
            sl3, wv3 = st_w3.load(dmas, gates=g3)
            w3buf = st_w3.buf(sl3)
            Tw(st_w3.slots[sl3], wv3)
            if fc == 0:
                Tw(dv, dv_h_done)
            for blk in range(3):
                Tw(pf[blk].rd)
            for t in range(NT):
                for blk in range(3):
                    bank = pf[blk]
                    T(lambda E, t=t, blk=blk, bank=bank, w3buf=w3buf:
                      E.matmul(
                          bank.ap[:], hh[:, t, blk * 128:(blk + 1) * 128],
                          w3buf[:, t, :], start=(t == 0), stop=(t == 10)),
                      *(((bank.pe, 1),) if t == 10 else ()))
            for blk in range(3):
                bank = pf[blk]
                yb = ybounce[ybu[0] % 3]
                Vw(bank.pe)
                if ybu[0] >= 3:
                    Vw(yst, 16 * (ybu[0] - 2))
                incs = [(bank.rd, 1), (dv, 1)]
                if blk == 2:
                    incs.append((st_w3.free, 1))
                V(lambda E, blk=blk, par=par, bank=bank, yb=yb:
                  E.tensor_scalar(
                      yb[:], bank.ap[:],
                      w_g[par][:, blk:blk + 1], None, ALU.mult), *incs)
                r0 = e * 384 + blk * 128
                SYw(dv)
                SY(lambda E, r0=r0, fc=fc, yb=yb: E.dma_start(
                    d_yperm[r0:r0 + 128, fc * 512:(fc + 1) * 512], yb[:]),
                   (yst, 16))
                ybu[0] += 1
    dv_ffn_done = dv.v
    yst_done = yst.v

    for blk in range(NB):
        if blk == 0:
            Gw(yst, yst_done)
            Gw(yz, 16)
        else:
            Gw(dv, dv.v)
        for j, itgt in ((0, idx1_i), (1, idx2_i)):
            G(lambda E, blk=blk, j=j, itgt=itgt: E.indirect_dma_start(
                yg[j][:], None, d_yperm,
                IndirectOffsetOnAxis(itgt[:, blk:blk + 1], 0),
                bounds_check=3072, oob_is_err=False), (ygl, 16))
        gates = ((dv, dv_ffn_done),) if blk == 0 else ()
        slot, wv = st_x.load([lambda E, b, blk=blk: E.dma_start(
            b[:], d_x[blk * 128:(blk + 1) * 128, :])], gates=gates)
        xb = st_x.buf(slot)
        Vw(ygl, ygl.v)
        V(lambda E, blk=blk: E.tensor_tensor(
            out_tm[:, blk, :], out_tm[:, blk, :], yg[0][:], ALU.add), (dv, 1))
        Vw(dv)
        V(lambda E, blk=blk: E.tensor_tensor(
            out_tm[:, blk, :], out_tm[:, blk, :], yg[1][:], ALU.add), (dv, 1))
        Vw(st_x.slots[slot], wv)
        Vw(dv)
        V(lambda E, blk=blk, xb=xb: E.tensor_tensor(
            out_tm[:, blk, :], out_tm[:, blk, :], xb[:], ALU.add),
          (st_x.free, 1), (dv, 1))
        SYw(dv)
        SY(lambda E, blk=blk: E.dma_start(
            d_out[blk * 128:(blk + 1) * 128, :], out_tm[:, blk, :]),
           (o_st, 16))
    if "route" in taps:
        d_r1 = nc.dram_tensor("dbg_idx1", [128, 8], I32, kind="ExternalOutput").ap()
        d_r2 = nc.dram_tensor("dbg_idx2", [128, 8], I32, kind="ExternalOutput").ap()
        d_rs = nc.dram_tensor("dbg_rsc", [128, 64], F32, kind="ExternalOutput").ap()
        d_xp = nc.dram_tensor("dbg_xperm", [768, 2049], BF16, kind="ExternalOutput").ap()
        d_yp = nc.dram_tensor("dbg_yperm", [768, 2048], F32, kind="ExternalOutput").ap()
        SY(lambda E: E.dma_start(d_r1, idx1_i[:]), (o_st, 16))
        SY(lambda E: E.dma_start(d_r2, idx2_i[:]), (o_st, 16))
        SY(lambda E: E.dma_start(
            d_rs, route_sc.ap().rearrange("p a b -> p (a b)")), (o_st, 16))
        SY(lambda E: E.dma_start(d_xp, d_xperm[0:768, 0:2049]), (o_st, 16))
        SY(lambda E: E.dma_start(d_yp, d_yperm[0:768, :]), (o_st, 16))
    if "xn" in taps:
        d_dxn = nc.dram_tensor("dbg_xn", [128, 16 * 1024], BF16,
                               kind="ExternalOutput").ap()
        SY(lambda E: E.dma_start(
            d_dxn, xn_fm.ap().rearrange("p c m -> p (c m)")), (o_st, 16))
    if "wtm" in taps:
        d_dwt = nc.dram_tensor("dbg_wtm", [128, 64], F32,
                               kind="ExternalOutput").ap()
        SY(lambda E: E.dma_start(
            d_dwt, w_tm.ap().rearrange("p b e -> p (b e)")), (o_st, 16))
    SYw(o_st, o_st.v)

    with nc.Block() as block:
        @block.tensor
        def _(E):
            for f in p.ops["tensor"]:
                f(E)

        @block.vector
        def _(E):
            for f in p.ops["vector"]:
                f(E)

        @block.scalar
        def _(E):
            for f in p.ops["scalar"]:
                f(E)

        @block.gpsimd
        def _(E):
            for f in p.ops["gpsimd"]:
                f(E)

        @block.sync
        def _(E):
            for f in p.ops["sync"]:
                f(E)
    return nc


def _host_inputs(ins):
    f32 = np.float32
    x = np.ascontiguousarray(np.asarray(ins["L0_attn_x"], f32).reshape(8192, 2048))
    Wqkv = np.asarray(ins["L0_attn_Wqkv"], f32)
    wqkv_s = np.ascontiguousarray(Wqkv.reshape(16, 128, 48, 128)
                                  .transpose(2, 1, 0, 3))
    bqkv_c = np.ascontiguousarray(
        np.asarray(ins["L0_attn_bqkv"], f32).reshape(48, 128).T)
    wout = np.ascontiguousarray(np.asarray(ins["L0_attn_Wout"], f32))
    bout_bc = np.ascontiguousarray(
        np.broadcast_to(np.asarray(ins["L0_attn_bout"], f32), (128, 2048)))
    slopes_bc = np.ascontiguousarray(
        np.broadcast_to(np.asarray(ins["L0_attn_slopes"], f32), (128, 16)))
    wgt = np.ascontiguousarray(np.asarray(ins["L0_ffn_Wg"], f32).T)
    W1 = np.asarray(ins["L0_ffn_W1"], f32)
    W2 = np.asarray(ins["L0_ffn_W2"], f32)
    W3 = np.asarray(ins["L0_ffn_W3"], f32)
    w1s = W1.transpose(0, 2, 1).reshape(8, 2048, 11, 128).transpose(0, 2, 1, 3)
    w2s = W2.transpose(0, 2, 1).reshape(8, 2048, 11, 128).transpose(0, 2, 1, 3)
    w12s = np.stack([w1s, w2s], axis=2)
    w12s = np.ascontiguousarray(w12s.reshape(8, 11, 2, 16, 128, 128)
                                .transpose(0, 1, 4, 2, 3, 5))
    w3t = np.ascontiguousarray(W3.transpose(0, 2, 1)
                               .reshape(8, 11, 128, 4, 512)
                               .transpose(0, 3, 2, 1, 4))

    ii = np.arange(128, dtype=f32)
    diffm = ii[None, :] - ii[:, None]
    diffx = np.where(diffm >= 0, diffm, BIG).astype(f32)
    iota_bc = np.ascontiguousarray(np.broadcast_to(ii, (128, 128)))
    cols = np.ascontiguousarray(
        np.stack([128.0 - ii, np.full(128, 128.0, f32)], axis=1).astype(f32))
    ident = np.eye(128, dtype=f32)
    blk_bc = np.ascontiguousarray(
        np.broadcast_to(128.0 * np.arange(8, dtype=f32), (128, 8)))
    iota8_bc = np.ascontiguousarray(
        np.broadcast_to(np.arange(8, dtype=f32), (128, 8)))

    shared = dict(wqkv_s=wqkv_s, bqkv_c=bqkv_c, wout=wout, bout_bc=bout_bc,
                  slopes_bc=slopes_bc, wgt=wgt, w12s=w12s, w3t=w3t,
                  diffx=diffx, iota_bc=iota_bc, cols=cols, ident=ident,
                  blk_bc=blk_bc, iota8_bc=iota8_bc)
    in_maps = []
    for core in range(8):
        q = core % 4
        tm = np.full((128, 4), -BIG, f32)
        for j in range(q):
            tm[:, j] = -1024.0 * (q - 1 - j)
        m = dict(shared)
        m["xs"] = np.ascontiguousarray(x[core * 1024:(core + 1) * 1024])
        m["neg_t_meta"] = tm
        in_maps.append(m)
    return in_maps


def kernel(**ins):
    in_maps = _host_inputs(ins)
    nc = build_nc()
    res = run_bass_kernel_spmd(nc, in_maps, core_ids=list(range(8)))
    out = np.concatenate([np.asarray(res.results[c]["out"]) for c in range(8)],
                         axis=0)
    return out.reshape(2, 4096, 2048).astype(np.float32)


# version 24
# speedup vs baseline: 1.0090x; 1.0032x over previous
import os as _os
import numpy as np
import concourse.bass as bass
import concourse.mybir as mybir
from concourse.bass_utils import run_bass_kernel_spmd
from concourse.bass_core_rust import IndirectOffsetOnAxis

F32 = mybir.dt.float32
I32 = mybir.dt.int32
BF16 = mybir.dt.bfloat16
AF = mybir.ActivationFunctionType
ALU = mybir.AluOpType
AX = mybir.AxisListType

ENGINES = ("tensor", "vector", "scalar", "gpsimd", "sync")
NB = 8
NH = 16
NE = 8
NT = 11
BIG = 1.0e7


class Sem:
    __slots__ = ("h", "v")

    def __init__(self, nc, name):
        self.h = nc.alloc_semaphore(name)
        self.v = 0


class Bank:
    def __init__(self, nc, name, ap):
        self.ap = ap
        self.pe = Sem(nc, name + "p")
        self.rd = Sem(nc, name + "r")


class Prog:
    def __init__(self, nc):
        self.nc = nc
        self.ops = {e: [] for e in ENGINES}

    def op(self, eng, fn, *incs):
        for s, n in incs:
            s.v += n
        pairs = tuple((s.h, n) for s, n in incs)

        def g(E, fn=fn, pairs=pairs):
            ins = fn(E)
            if pairs:
                ins.then_inc(*pairs[0])
                for h, n in pairs[1:]:
                    E.nop().then_inc(h, n)
        self.ops[eng].append(g)

    def wait(self, eng, sem, val=None):
        v = sem.v if val is None else val
        self.ops[eng].append(lambda E, s=sem.h, v=v: E.wait_ge(s, v))


class Stream:

    def __init__(self, nc, p, name, queue, bufs):
        self.p = p
        self.queue = queue
        self.bufs = bufs
        self.slots = [Sem(nc, f"{name}_s{i}") for i in range(len(bufs))]
        self.free = Sem(nc, f"{name}_f")
        self.rounds = 0

    def load(self, dma_fns, gates=()):
        d = len(self.bufs)
        slot = self.rounds % d
        for g in gates:
            self.p.wait(self.queue, g[0], g[1])
        if self.rounds >= d:
            self.p.wait(self.queue, self.free, self.rounds - d + 1)
        s = self.slots[slot]
        for fn in dma_fns:
            self.p.op(self.queue, (lambda E, f=fn, b=self.bufs[slot]: f(E, b)),
                      (s, 16))
        self.rounds += 1
        return slot, s.v

    def buf(self, slot):
        return self.bufs[slot]


def build_nc(taps=()):
    nc = bass.Bass("TRN2", target_bir_lowering=False,
                   detect_race_conditions=False)
    p = Prog(nc)

    d_x = nc.dram_tensor("xs", [1024, 2048], F32, kind="ExternalInput").ap()
    d_wqkv = nc.dram_tensor("wqkv_s", [48, 128, 16, 128], F32, kind="ExternalInput").ap()
    d_bqkv = nc.dram_tensor("bqkv_c", [128, 48], F32, kind="ExternalInput").ap()
    d_wout = nc.dram_tensor("wout", [2048, 2048], F32, kind="ExternalInput").ap()
    d_boutb = nc.dram_tensor("bout_bc", [128, 2048], F32, kind="ExternalInput").ap()
    d_slops = nc.dram_tensor("slopes_bc", [128, 16], F32, kind="ExternalInput").ap()
    d_wgt = nc.dram_tensor("wgt", [2048, 8], F32, kind="ExternalInput").ap()
    d_w12 = nc.dram_tensor("w12s", [8, 11, 128, 2, 16, 128], F32,
                           kind="ExternalInput").ap()
    d_w3 = nc.dram_tensor("w3t", [8, 4, 128, 11, 512], F32, kind="ExternalInput").ap()
    d_diff = nc.dram_tensor("diffx", [128, 128], F32, kind="ExternalInput").ap()
    d_iota = nc.dram_tensor("iota_bc", [128, 128], F32, kind="ExternalInput").ap()
    d_cols = nc.dram_tensor("cols", [128, 2], F32, kind="ExternalInput").ap()
    d_tmet = nc.dram_tensor("neg_t_meta", [128, 4], F32, kind="ExternalInput").ap()
    d_ident = nc.dram_tensor("ident", [128, 128], F32, kind="ExternalInput").ap()
    d_blkbc = nc.dram_tensor("blk_bc", [128, 8], F32, kind="ExternalInput").ap()
    d_iota8 = nc.dram_tensor("iota8_bc", [128, 8], F32, kind="ExternalInput").ap()

    d_out = nc.dram_tensor("out", [1024, 2048], F32, kind="ExternalOutput").ap()

    d_astage = nc.dram_tensor("a_stage", [NH * 128, 128], F32, kind="Internal")
    d_aall1 = nc.dram_tensor("a_all1", [4 * 8 * 128, 128], F32, kind="Internal")
    d_aall2 = nc.dram_tensor("a_all2", [4 * 8 * 128, 128], F32, kind="Internal")
    d_qspill = nc.dram_tensor("q_spill", [NH, 128, 1024], BF16, kind="Internal").ap()
    d_xnsp = nc.dram_tensor("xn_sp", [1024, 2048], BF16, kind="Internal").ap()
    d_xperm = nc.dram_tensor("xn_perm", [3073, 2176], BF16, kind="Internal").ap()
    d_yperm = nc.dram_tensor("y_perm", [3073, 2048], F32, kind="Internal").ap()

    BASE = 16512
    LIMIT = 229344

    def sb(name, shape, dtype, rel):
        nbytes = int(np.prod(shape[1:])) * mybir.dt.size(dtype)
        assert BASE + rel + nbytes <= LIMIT, (name, rel, nbytes)
        return nc.alloc_sbuf_tensor_at(name, shape, dtype, offset=BASE + rel)

    cur = 0
    xn_fm = sb("xn_fm", [128, 16, 1024], BF16, cur); cur += 32768
    OUT0 = cur
    out_tm = sb("out_tm", [128, 8, 2048], F32, OUT0); cur += 65536
    sq_scr = sb("sq_scr", [128, 2048], F32, OUT0)
    xn_tm2 = [sb("xn_tm", [128, 2048], BF16, OUT0 + 8192),
              sb("xn_tmB", [128, 2048], BF16, OUT0 + 16384)]
    s_bf_all = sb("s_bf_all", [128, 8, 128], BF16, OUT0 + 12288)
    s_f32 = sb("s_f32", [128, 128], F32, OUT0 + 14336)
    OFM0 = cur
    o_fm = sb("o_fm", [128, 16, 1024], BF16, OFM0); cur += 32768
    hh = sb("hh", [128, 11, 384], BF16, OFM0)
    sil = sb("sil", [128, 384], F32, OFM0 + 8448)
    xn_ge2 = [sb("xn_ge", [128, 16, 384], BF16, OFM0 + 10496),
              sb("xn_geB", [128, 16, 384], BF16, 14848)]
    xn_g = [sb(f"xn_g{i}", [128, 2176], BF16, 4352 * i) for i in range(2)]
    ybounce = [sb(f"ybounce{i}", [128, 512], F32, 8704 + 2048 * i)
               for i in range(3)]
    yg = [sb(f"yg{i}", [128, 2048], F32, 14848 + 8192 * i) for i in range(2)]
    BOUT0 = cur
    bout_bc = sb("bout_bc", [128, 2048], F32, cur); cur += 8192
    ident_f = sb("ident_f", [128, 128], F32, cur); cur += 512
    ident_b = sb("ident_b", [128, 128], BF16, cur); cur += 256
    diffx = sb("diffx", [128, 128], F32, cur); cur += 512
    iota_bc = sb("iota_bc", [128, 128], F32, cur); cur += 512
    slop = sb("slop", [128, 16], F32, cur); cur += 64
    negs = sb("negs", [128, 16], F32, cur); cur += 64
    cols = sb("cols", [128, 2], F32, cur); cur += 32
    tmet = sb("tmet", [128, 4], F32, cur); cur += 32
    cfr = sb("cfr", [128, 4, 16], F32, cur); cur += 256
    d_all = sb("d_all", [128, 16], F32, cur); cur += 64
    w_tm = sb("w_tm", [128, 8, 8], F32, cur); cur += 256
    dpow = sb("dpow", [128, 1], F32, cur); cur += 32
    blkbc = sb("blkbc", [128, 8], F32, cur); cur += 32
    dpa = sb("dpa", [128, 8], F32, cur); cur += 32
    gt_all = sb("gt_all", [128, 64], F32, cur); cur += 256
    iota8 = sb("iota8", [128, 8], F32, cur); cur += 32
    route_ind = sb("route_ind", [128, 8, 16], F32, cur); cur += 512
    route_sc = sb("route_sc", [128, 8, 8], F32, cur); cur += 256
    w_g = [sb(f"w_g{i}", [128, 3], F32, cur + 32 * i) for i in range(2)]
    cur += 64
    sel_all = sb("sel_all", [128, 64], F32, cur); cur += 256
    idx1_i = sb("idx1_i", [128, 8], I32, cur); cur += 32
    idx2_i = sb("idx2_i", [128, 8], I32, cur); cur += 32
    s_in = sb("s_in", [128, 128], F32, cur); cur += 512
    s_in_b = sb("s_in_b", [128, 128], BF16, cur); cur += 256
    qd_b = sb("qd_b", [128, 128], BF16, cur); cur += 256
    stat = sb("stat", [128, 8], F32, cur); cur += 32
    bq_c = sb("bq_c", [128, 48], F32, cur); cur += 192
    wg_sb = sb("wg_sb", [128, 16, 8], BF16, cur); cur += 256
    smb = [sb(f"smb{i}", [128, 128], BF16, cur + 256 * i) for i in range(3)]
    cur += 768
    cur = (cur + 63) & ~63

    A0 = cur
    q_fmB = sb("q_fmB", [128, 1024], BF16, A0)
    q2_fm = sb("q2_fm", [128, 1024], BF16, A0 + 2048)
    k_fmB = sb("k_fmB", [128, 1024], BF16, A0 + 4096)
    v_fmB = sb("v_fmB", [128, 1024], BF16, A0 + 6144)
    k2_tm = sb("k2_tm", [128, 8, 128], BF16, A0 + 8192)
    v_tm = sb("v_tm", [128, 8, 128], BF16, A0 + 10240)
    wqb = [sb(f"wqb{i}", [128, 16, 128], BF16, A0 + 12288 + 4096 * i)
           for i in range(2)]
    w12b = [sb(f"w12b{i}", [128, 2, 16, 128], BF16, A0 + 8192 * i)
            for i in range(2)]
    cur += 20480
    B0 = cur
    xbuf = [sb(f"xbuf{i}", [128, 2048], F32, B0 + 8192 * i) for i in range(2)]
    woutb = [sb("woutb0", [128, 16, 512], BF16, B0)]
    cur += 16384
    C0 = cur
    woutb.append(sb("woutb1", [128, 16, 512], BF16, C0))
    w3b = [sb(f"w3b{i}", [128, 11, 512], BF16, C0 + 11264 * i) for i in range(2)]
    stage = [sb(f"stage{i}", [128, 2176], BF16, C0 + 4352 * i)
             for i in range(2)]
    cur += 22528
    D0 = cur
    g_sb = sb("g_sb", [8, 1024], F32, D0)
    qre = [sb(f"qre{i}", [128, 1024], BF16, D0 + 2048 * i) for i in range(2)]
    qre.append(sb("qre2", [128, 1024], BF16, OUT0 + 22528))
    cur += 4096
    a4 = [sb(f"a4{i}", [128, 4, 128], F32, OUT0 + 16384 + 2048 * i)
          for i in range(3)]
    assert BASE + cur <= LIMIT, cur

    pf = [Bank(nc, f"pf{i}", nc.alloc_psum_tensor(f"pf{i}", [128, 512], F32).ap())
          for i in range(6)]
    pbf = [Bank(nc, f"pbf{i}",
                nc.alloc_psum_tensor(f"pbf{i}", [128, 1024], BF16).ap())
           for i in range(2)]

    c_ld = Sem(nc, "c_ld")
    dv = Sem(nc, "dv")
    ac = Sem(nc, "ac")
    a_st = Sem(nc, "a_st")
    q_st = Sem(nc, "q_st")
    o_st = Sem(nc, "o_st")
    cc_s = Sem(nc, "cc_s")
    xpz = Sem(nc, "xpz")
    pcast = Sem(nc, "pcast")
    xnsp = Sem(nc, "xnsp")
    rsc = Sem(nc, "rsc")
    sgl = Sem(nc, "sgl")
    swc = Sem(nc, "swc")
    yz = Sem(nc, "yz")
    yst = Sem(nc, "yst")
    ygl = Sem(nc, "ygl")

    st_x = Stream(nc, p, "stx", "sync", xbuf)
    st_wq = Stream(nc, p, "stwq", "gpsimd", wqb)
    w12b.append(sb("w12b2", [128, 2, 16, 128], BF16, BOUT0))
    st_w12 = Stream(nc, p, "stw12", "gpsimd", w12b)
    st_wo = Stream(nc, p, "stwo", "gpsimd", woutb)
    st_w3 = Stream(nc, p, "stw3", "gpsimd", w3b)
    st_a4 = Stream(nc, p, "sta4", "sync", a4)
    st_qr = Stream(nc, p, "stqr", "sync", qre)

    def T(fn, *incs): p.op("tensor", fn, *incs)
    def V(fn, *incs): p.op("vector", fn, *incs)
    def A(fn, *incs): p.op("scalar", fn, *incs)
    def G(fn, *incs): p.op("gpsimd", fn, *incs)
    def SY(fn, *incs): p.op("sync", fn, *incs)
    def Tw(s, v=None): p.wait("tensor", s, v)
    def Vw(s, v=None): p.wait("vector", s, v)
    def Aw(s, v=None): p.wait("scalar", s, v)
    def Gw(s, v=None): p.wait("gpsimd", s, v)
    def SYw(s, v=None): p.wait("sync", s, v)

    for dst, src in ((ident_f, d_ident), (ident_b, d_ident), (diffx, d_diff),
                     (iota_bc, d_iota), (slop, d_slops), (cols, d_cols),
                     (tmet, d_tmet), (bout_bc, d_boutb), (bq_c, d_bqkv),
                     (blkbc, d_blkbc), (iota8, d_iota8)):
        G(lambda E, dst=dst, src=src: E.dma_start(dst[:], src), (c_ld, 16))
    G(lambda E: E.dma_start(wg_sb[:], d_wgt.rearrange("(c p) e -> p c e", p=128)),
      (c_ld, 16))
    n_const = 12
    _pc = []
    V(lambda E: E.memset(stage[0][:], 0.0), (dv, 1))
    V(lambda E: E.memset(stage[1][:], 0.0), (dv, 1))
    dv_stz = dv.v
    Aw(dv, dv_stz)
    for k in range(25):
        r0 = k * 128
        n = 128 if k < 24 else 1
        A(lambda E, r0=r0, n=n: E.dma_start(
            d_xperm[r0:r0 + n, :], stage[0][0:n, :]), (xpz, 16))
    Vw(c_ld, 16 * n_const)
    V(lambda E: E.tensor_scalar(negs[:], slop[:], -1.0, None, ALU.mult), (dv, 1))
    Aw(dv)
    for r in range(4):
        A(lambda E, r=r: E.activation(cfr[:, r, :], slop[:], AF.Exp,
                                      scale=tmet[:, r:r + 1]), (ac, 1))
    for h in range(NH):
        A(lambda E, h=h: E.activation(d_all[:, h:h + 1], cols[:, 1:2], AF.Exp,
                                      scale=negs[:, h:h + 1]), (ac, 1))

    for blk in range(NB):
        slot, wv = st_x.load([lambda E, b, blk=blk: E.dma_start(
            b[:], d_x[blk * 128:(blk + 1) * 128, :])])
        xb = st_x.buf(slot)
        xtm = xn_tm2[blk % 2]
        Vw(st_x.slots[slot], wv)
        V(lambda E, xb=xb: E.reduce_sum(stat[:, 0:1], xb[:], axis=AX.X), (dv, 1))
        Aw(st_x.slots[slot], wv)
        A(lambda E, xb=xb: E.activation(
            sq_scr[:], xb[:], AF.Square, accum_out=stat[:, 1:2]), (ac, 1))
        Vw(dv)
        V(lambda E: E.tensor_scalar(stat[:, 2:3], stat[:, 0:1], 1.0 / 2048,
                                    None, ALU.mult), (dv, 1))
        Vw(ac)
        V(lambda E: E.tensor_scalar(stat[:, 3:4], stat[:, 1:2], 1.0 / 2048,
                                    None, ALU.mult), (dv, 1))
        Vw(dv)
        V(lambda E: E.scalar_tensor_tensor(stat[:, 4:5], stat[:, 2:3],
                                           stat[:, 2:3], stat[:, 3:4],
                                           ALU.mult, ALU.subtract), (dv, 1))
        Vw(dv)
        V(lambda E: E.tensor_scalar(stat[:, 5:6], stat[:, 4:5], -1.0, 1.0e-5,
                                    ALU.mult, ALU.add), (dv, 1))
        Aw(dv)
        A(lambda E: E.activation(stat[:, 5:6], stat[:, 5:6], AF.Ln), (ac, 1))
        Aw(ac)
        A(lambda E: E.activation(stat[:, 6:7], stat[:, 5:6], AF.Exp,
                                 scale=-0.5), (ac, 1))
        Vw(ac)
        Vw(xnsp, 16 * max(0, blk - 1))
        V(lambda E, xb=xb, xtm=xtm: E.tensor_scalar(
            xtm[:], xb[:], stat[:, 2:3], stat[:, 6:7],
            ALU.subtract, ALU.mult), (dv, 1), (st_x.free, 1))
        SYw(dv)
        SY(lambda E, blk=blk, xtm=xtm: E.dma_start(
            d_xnsp[blk * 128:(blk + 1) * 128, :], xtm[:]), (xnsp, 16))
        if blk == 0:
            Tw(c_ld, 16 * n_const)
        for half in range(2):
            bank = pbf[half]
            Tw(dv)
            Tw(bank.rd)
            for c8 in range(8):
                c = half * 8 + c8
                T(lambda E, c=c, c8=c8, bank=bank, xtm=xtm: E.transpose(
                    bank.ap[:, c8 * 128:(c8 + 1) * 128],
                    xtm[:, c * 128:(c + 1) * 128], ident_b[:]),
                  *(((bank.pe, 1),) if c8 == 7 else ()))
            Vw(bank.pe)
            V(lambda E, half=half, blk=blk, bank=bank: E.tensor_copy(
                xn_fm[:, half * 8:half * 8 + 8, blk * 128:(blk + 1) * 128],
                bank.ap.rearrange("p (c m) -> p c m", c=8)),
              (bank.rd, 1), (dv, 1))
        if blk == 3:
            dv_ln_half = dv.v
    dv_ln_done = dv.v

    for ic in range(2):
        bank = pf[2 + ic]
        Tw(dv, dv_ln_half if ic == 0 else dv_ln_done)
        Tw(bank.rd)
        for dd in range(16):
            T(lambda E, dd=dd, ic=ic, bank=bank: E.matmul(
                bank.ap[0:8, :], wg_sb[:, dd, :],
                xn_fm[:, dd, ic * 512:(ic + 1) * 512],
                start=(dd == 0), stop=(dd == 15)),
              *(((bank.pe, 1),) if dd == 15 else ()))
        Vw(bank.pe)
        V(lambda E, ic=ic, bank=bank: E.tensor_copy(
            g_sb[:, ic * 512:(ic + 1) * 512], bank.ap[0:8, :]),
          (bank.rd, 1), (dv, 1))
    bank = pf[4]
    Tw(dv)
    Tw(bank.rd)
    for blk in range(NB):
        T(lambda E, blk=blk, bank=bank: E.transpose(
            bank.ap[:, blk * 8:(blk + 1) * 8],
            g_sb[:, blk * 128:(blk + 1) * 128],
            ident_f[0:8, 0:8]), *(((bank.pe, 1),) if blk == 7 else ()))
    Vw(bank.pe)
    V(lambda E, bank=bank: E.tensor_copy(gt_all[:], bank.ap[:, 0:64]),
      (bank.rd, 1), (dv, 1))
    for blk in range(NB):
        gt = gt_all[:, blk * 8:(blk + 1) * 8]
        i1 = route_ind[:, blk, 0:8]
        i2 = route_ind[:, blk, 8:16]
        Vw(dv)
        V(lambda E, gt=gt: E.reduce_max(dpow[:], gt, axis=AX.X), (dv, 1))
        Vw(dv)
        V(lambda E, gt=gt, i1=i1: E.tensor_scalar(i1, gt, dpow[:],
                                                  None, ALU.is_ge), (dv, 1))
        Vw(dv)
        V(lambda E, gt=gt, i1=i1: E.scalar_tensor_tensor(
            s_in[:, 8:16], i1, -1.0e30, gt, ALU.mult, ALU.add), (dv, 1))
        Vw(dv)
        V(lambda E: E.reduce_max(s_in[:, 16:17], s_in[:, 8:16], axis=AX.X),
          (dv, 1))
        Vw(dv)
        V(lambda E, i2=i2: E.tensor_scalar(i2, s_in[:, 8:16],
                                           s_in[:, 16:17], None, ALU.is_ge),
          (dv, 1))
        V(lambda E: E.tensor_tensor(s_in[:, 17:18], dpow[:], s_in[:, 16:17],
                                    ALU.subtract), (dv, 1))
        Aw(dv)
        A(lambda E, blk=blk: E.activation(route_sc[:, blk, 2:3],
                                          s_in[:, 17:18], AF.Sigmoid),
          (ac, 1))
        Vw(ac)
        V(lambda E, blk=blk: E.tensor_scalar(
            route_sc[:, blk, 3:4], route_sc[:, blk, 2:3], -1.0, 1.0,
            ALU.mult, ALU.add), (dv, 1))
        V(lambda E, blk=blk, i1=i1: E.tensor_scalar(
            s_in[:, 32:40], i1, route_sc[:, blk, 2:3], None, ALU.mult),
          (dv, 1))
        Vw(dv)
        V(lambda E, blk=blk, i2=i2: E.scalar_tensor_tensor(
            w_tm[:, blk, :], i2, route_sc[:, blk, 3:4], s_in[:, 32:40],
            ALU.mult, ALU.add), (dv, 1))
        V(lambda E, i1=i1: E.tensor_tensor(s_in[:, 40:48], i1, iota8[:],
                                           ALU.mult), (dv, 1))
        Vw(dv)
        V(lambda E, blk=blk: E.reduce_sum(route_sc[:, blk, 0:1],
                                          s_in[:, 40:48], axis=AX.X), (dv, 1))
        V(lambda E, i2=i2: E.tensor_tensor(s_in[:, 40:48], i2, iota8[:],
                                           ALU.mult), (dv, 1))
        Vw(dv)
        V(lambda E, blk=blk: E.reduce_sum(route_sc[:, blk, 1:2],
                                          s_in[:, 40:48], axis=AX.X), (dv, 1))
        V(lambda E, blk=blk, i1=i1, i2=i2: E.tensor_tensor(
            sel_all[:, blk * 8:(blk + 1) * 8], i1, i2, ALU.add), (dv, 1))


    for grp in range(2):
        bank = pf[2 + grp]
        Tw(dv)
        Tw(bank.rd)
        for b4 in range(4):
            blk = grp * 4 + b4
            T(lambda E, blk=blk, b4=b4, bank=bank: E.transpose(
                bank.ap[0:8, b4 * 128:(b4 + 1) * 128],
                sel_all[:, blk * 8:(blk + 1) * 8], ident_f[0:128, 0:128]),
              *(((bank.pe, 1),) if b4 == 3 else ()))
        Vw(bank.pe)
        V(lambda E, grp=grp, bank=bank: E.tensor_copy(
            g_sb[:, grp * 512:(grp + 1) * 512], bank.ap[0:8, 0:512]),
          (bank.rd, 1), (dv, 1))
    zv = sq_scr.ap()[0:8, 0:1024]
    pv = sq_scr.ap()[0:8, 1024:2048]
    V(lambda E, zv=zv: E.memset(zv, 0.0), (dv, 1))
    Vw(dv)
    V(lambda E, zv=zv, pv=pv: E.tensor_tensor_scan(
        pv, g_sb[:], zv, 0.0, ALU.add, ALU.add), (dv, 1))
    bank = pf[4]
    Tw(dv)
    Tw(bank.rd)
    for blk in range(NB):
        T(lambda E, blk=blk, bank=bank, pv=pv: E.transpose(
            bank.ap[:, blk * 8:(blk + 1) * 8],
            pv[0:8, blk * 128:(blk + 1) * 128], ident_f[0:8, 0:8]),
          *(((bank.pe, 1),) if blk == 7 else ()))
    Vw(bank.pe)
    V(lambda E, bank=bank: E.tensor_copy(gt_all[:], bank.ap[:, 0:64]),
      (bank.rd, 1), (dv, 1))
    for blk in range(NB):
        i1 = route_ind[:, blk, 0:8]
        i2 = route_ind[:, blk, 8:16]
        pos = gt_all[:, blk * 8:(blk + 1) * 8]
        Vw(dv)
        V(lambda E, i1=i1, pos=pos: E.tensor_tensor(s_in[:, 40:48], i1, pos,
                                                    ALU.mult), (dv, 1))
        Vw(dv)
        V(lambda E, blk=blk: E.reduce_sum(route_sc[:, blk, 4:5],
                                          s_in[:, 40:48], axis=AX.X), (dv, 1))
        V(lambda E, i2=i2, pos=pos: E.tensor_tensor(s_in[:, 40:48], i2, pos,
                                                    ALU.mult), (dv, 1))
        Vw(dv)
        V(lambda E, blk=blk: E.reduce_sum(route_sc[:, blk, 5:6],
                                          s_in[:, 40:48], axis=AX.X), (dv, 1))
        for k, (pcol, icol, itgt) in enumerate(
                ((4, 0, idx1_i), (5, 1, idx2_i))):
            Vw(dv)
            V(lambda E, blk=blk, icol=icol: E.tensor_scalar(
                s_in[:, 49:50], route_sc[:, blk, icol:icol + 1], 384.0, -1.0,
                ALU.mult, ALU.add), (dv, 1))
            Vw(dv)
            V(lambda E, blk=blk, pcol=pcol: E.tensor_tensor(
                s_in[:, 50:51], s_in[:, 49:50],
                route_sc[:, blk, pcol:pcol + 1], ALU.add), (dv, 1))
            V(lambda E, blk=blk, pcol=pcol: E.tensor_scalar(
                s_in[:, 51:52], route_sc[:, blk, pcol:pcol + 1], 384.5, None,
                ALU.is_ge), (dv, 1))
            Vw(dv)
            V(lambda E: E.scalar_tensor_tensor(
                s_in[:, 50:51], s_in[:, 51:52], 1.0e9, s_in[:, 50:51],
                ALU.mult, ALU.add), (dv, 1))
            Vw(dv)
            V(lambda E: E.tensor_scalar(s_in[:, 50:51], s_in[:, 50:51],
                                        3072.0, None, ALU.min), (dv, 1))
            Vw(dv)
            V(lambda E, blk=blk, itgt=itgt: E.tensor_copy(
                itgt[:, blk:blk + 1], s_in[:, 50:51]), (dv, 1))
    dv_route = dv.v
    Vw(dv)
    V(lambda E: E.memset(sq_scr[:], 0.0), (dv, 1))
    dv_zero = dv.v
    SYw(dv, dv_zero)
    SY(lambda E: E.dma_start(d_yperm[3072:3073, :], sq_scr[0:1, :]), (yz, 16))
    _groups = []
    _og, _ov, _os = p.ops["gpsimd"], p.ops["vector"], p.ops["sync"]
    k = 0
    for blk in range(NB):
        for itgt, pcol in ((idx1_i, 2), (idx2_i, 3)):
            sl = k % 2
            _ds, _dv, _dg = [], [], []
            p.ops["sync"] = _ds
            if k == 0:
                SYw(xpz, 16 * 25)
                SYw(xnsp, 16 * 8)
            if k >= 2:
                SYw(rsc, 16 * (k - 1))
            SY(lambda E, blk=blk, sl=sl: E.dma_start(
                stage[sl][:, 0:2048],
                d_xnsp[blk * 128:(blk + 1) * 128, :]), (sgl, 16))
            p.ops["sync"] = _os
            p.ops["vector"] = _dv
            p.wait("vector", sgl, 16 * (k + 1))
            p.op("vector", (lambda E, blk=blk, sl=sl, pcol=pcol:
                            E.tensor_copy(stage[sl][:, 2048:2049],
                                          route_sc[:, blk, pcol:pcol + 1])),
                 (swc, 1))
            p.ops["vector"] = _ov
            p.ops["gpsimd"] = _dg
            Gw(swc, k + 1)
            if k == 0:
                Gw(dv, dv_route)
                Gw(xpz, 16 * 25)
            G(lambda E, blk=blk, sl=sl, itgt=itgt: E.indirect_dma_start(
                d_xperm, IndirectOffsetOnAxis(itgt[:, blk:blk + 1], 0),
                stage[sl][:], None,
                bounds_check=3072, oob_is_err=False), (rsc, 16))
            p.ops["gpsimd"] = _og
            _groups.append((_ds, _dv, _dg))
            k += 1
    rsc_done = rsc.v

    _glag = []
    for h in range(NH):
        if h == 11:
            Gw(a_st, 16 * 8)
            G(lambda E: E.collective_compute(
                "AllGather", ALU.bypass,
                replica_groups=[[0, 1, 2, 3], [4, 5, 6, 7]],
                ins=[d_astage.ap()[0:8 * 128, :].opt()],
                outs=[d_aall1.ap().opt()]), (cc_s, 1))
        if h >= 1:
            while _glag:
                p.ops["gpsimd"].append(_glag.pop(0))
            for _ in range(2):
                if _groups:
                    gs, gv, gg = _groups.pop(0)
                    p.ops["sync"].extend(gs)
                    p.ops["vector"].extend(gv)
                    _glag.extend(gg)
        Aw(dv)
        A(lambda E, h=h: E.activation(qd_b[:], iota_bc[:], AF.Exp,
                                      scale=negs[:, h:h + 1]), (ac, 1))
        A(lambda E, h=h: E.activation(stat[:, 7:8], cols[:, 0:1], AF.Exp,
                                      scale=negs[:, h:h + 1]), (ac, 1))
        A(lambda E, h=h: E.activation(s_in[:], diffx[:], AF.Exp,
                                      scale=negs[:, h:h + 1]), (ac, 1))
        ac_tab = ac.v

        for j, slabi in ((0, 16 + h), (1, 32 + h)):
            slot, wv = st_wq.load([lambda E, b, s=slabi: E.dma_start(
                b[:], d_wqkv[s])])
            wslab = st_wq.buf(slot)
            dst = k_fmB if j == 0 else v_fmB
            for ic in range(2):
                bank = pf[ic]
                Tw(st_wq.slots[slot], wv)
                if h == 0 and j == 0:
                    Tw(dv, dv_ln_half if ic == 0 else dv_ln_done)
                Tw(bank.rd)
                for dd in range(16):
                    T(lambda E, dd=dd, ic=ic, bank=bank, wslab=wslab: E.matmul(
                        bank.ap[:], wslab[:, dd, :],
                        xn_fm[:, dd, ic * 512:(ic + 1) * 512],
                        start=(dd == 0), stop=(dd == 15)),
                      *(((bank.pe, 1),) if dd == 15 else ()))
                Vw(bank.pe)
                incs = [(bank.rd, 1), (dv, 1)]
                if ic == 1:
                    incs.append((st_wq.free, 1))
                V(lambda E, dst=dst, ic=ic, bank=bank, si=slabi: E.tensor_scalar(
                    dst[:, ic * 512:(ic + 1) * 512], bank.ap[:],
                    bq_c[:, si:si + 1], None, ALU.add), *incs)

        Tw(dv)
        for j in range(2):
            bank = pbf[j]
            src = k_fmB if j == 0 else v_fmB
            Tw(bank.rd)
            for blk in range(NB):
                T(lambda E, blk=blk, src=src, bank=bank: E.transpose(
                    bank.ap[:, blk * 128:(blk + 1) * 128],
                    src[:, blk * 128:(blk + 1) * 128], ident_b[:]),
                  *(((bank.pe, 1),) if blk == 7 else ()))
        Vw(ac, ac_tab)
        Vw(pbf[0].pe)
        V(lambda E: E.tensor_scalar(
            k2_tm.ap().rearrange("p b m -> p (b m)"), pbf[0].ap[:],
            stat[:, 7:8], None, ALU.mult), (pbf[0].rd, 1), (dv, 1))
        Vw(pbf[1].pe)
        V(lambda E: E.tensor_copy(v_tm.ap().rearrange("p b m -> p (b m)"),
                                  pbf[1].ap[:]), (pbf[1].rd, 1), (dv, 1))
        dv_kv = dv.v

        def emit_scan(h=h, dv_kv=dv_kv):
            Vw(a_st, a_st.v)
            V(lambda E: E.memset(s_f32[:], 0.0), (dv, 1))
            V(lambda E: E.memset(s_bf_all.ap().rearrange("p b m -> p (b m)"),
                                 0.0), (dv, 1))
            Tw(dv)
            for i in range(NB):
                bank = pf[4 + (i // 4)]
                sl = i % 4
                if sl == 0:
                    Tw(bank.rd)
                T(lambda E, i=i, sl=sl, bank=bank: E.matmul(
                    bank.ap[:, sl * 128:(sl + 1) * 128], k2_tm[:, i, :],
                    v_tm[:, i, :], start=True, stop=True),
                  *(((bank.pe, 1),) if sl == 3 else ()))
            for i in range(NB):
                bank = pf[4 + (i // 4)]
                sl = i % 4
                if sl == 0:
                    Vw(bank.pe)
                if i > 0:
                    V(lambda E, i=i: E.tensor_copy(s_bf_all[:, i, :],
                                                   s_f32[:]), (dv, 1))
                incs = [(dv, 1)]
                if sl == 3:
                    incs.append((bank.rd, 1))
                V(lambda E, h=h, sl=sl, bank=bank: E.scalar_tensor_tensor(
                    s_f32[:], s_f32[:], d_all[:, h:h + 1],
                    bank.ap[:, sl * 128:(sl + 1) * 128],
                    ALU.mult, ALU.add), *incs)
            SYw(dv)
            SY(lambda E, h=h: E.dma_start(
                d_astage.ap()[h * 128:(h + 1) * 128, :], s_f32[:]),
               (a_st, 16))

        def emit_qproj(h=h):
            slot, wv = st_wq.load([lambda E, b, s=h: E.dma_start(
                b[:], d_wqkv[s])])
            wslab = st_wq.buf(slot)
            for ic in range(2):
                bank = pf[ic]
                Tw(st_wq.slots[slot], wv)
                Tw(bank.rd)
                for dd in range(16):
                    T(lambda E, dd=dd, ic=ic, bank=bank, wslab=wslab:
                      E.matmul(
                          bank.ap[:], wslab[:, dd, :],
                          xn_fm[:, dd, ic * 512:(ic + 1) * 512],
                          start=(dd == 0), stop=(dd == 15)),
                      *(((bank.pe, 1),) if dd == 15 else ()))
                Vw(bank.pe)
                incs = [(bank.rd, 1), (dv, 1)]
                if ic == 1:
                    incs.append((st_wq.free, 1))
                V(lambda E, ic=ic, bank=bank, h=h: E.tensor_scalar(
                    q_fmB[:, ic * 512:(ic + 1) * 512], bank.ap[:],
                    bq_c[:, h:h + 1], None, ALU.add), *incs)
            return dv.v

        def emit_q2(h=h):
            Vw(q_st, q_st.v)
            Vw(ac, ac_tab)
            Vw(dv)
            for blk in range(NB):
                V(lambda E, blk=blk: E.tensor_tensor(
                    q2_fm[:, blk * 128:(blk + 1) * 128],
                    q_fmB[:, blk * 128:(blk + 1) * 128], qd_b[:], ALU.mult),
                  (dv, 1))
            SYw(dv)
            SY(lambda E, h=h: E.dma_start(d_qspill[h], q2_fm[:]), (q_st, 16))

        if h == NH - 1:
            emit_scan()
            dv_qevac = emit_qproj()
            emit_q2()
        else:
            dv_qevac = emit_qproj()
            emit_scan()
            emit_q2()

        def emit_scoresT(i):
            sc_bank = pf[2 + (i % 2)]
            Tw(sc_bank.rd)
            T(lambda E, i=i, bank=sc_bank: E.matmul(
                bank.ap[:, 0:128], k_fmB[:, i * 128:(i + 1) * 128],
                q_fmB[:, i * 128:(i + 1) * 128], start=True, stop=True),
              (sc_bank.pe, 1))

        Tw(dv, dv_qevac)
        emit_scoresT(0)
        emit_scoresT(1)
        for i in range(NB):
            sc_bank = pf[2 + (i % 2)]
            o_bank = pf[4 + (i % 2)]
            Vw(sc_bank.pe)
            V(lambda E, i=i, bank=sc_bank: E.tensor_tensor(
                smb[i % 3][:], bank.ap[:, 0:128], s_in[:], ALU.mult),
              (sc_bank.rd, 1), (dv, 1))
            Tw(dv)
            Tw(o_bank.rd)
            T(lambda E, i=i, bank=o_bank: E.matmul(
                bank.ap[:, 0:128], s_bf_all[:, i, :],
                q2_fm[:, i * 128:(i + 1) * 128], start=True, stop=False))
            T(lambda E, i=i, bank=o_bank: E.matmul(
                bank.ap[:, 0:128], v_tm[:, i, :], smb[i % 3][:],
                start=False, stop=True), (o_bank.pe, 1))
            if i + 2 < NB:
                emit_scoresT(i + 2)
            Vw(o_bank.pe)
            V(lambda E, i=i, h=h, bank=o_bank: E.tensor_copy(
                o_fm[:, h, i * 128:(i + 1) * 128], bank.ap[:, 0:128]),
              (o_bank.rd, 1), (dv, 1))
    while _groups:
        gs, gv, gg = _groups.pop(0)
        p.ops["sync"].extend(gs)
        p.ops["vector"].extend(gv)
        _glag.extend(gg)
    while _glag:
        p.ops["gpsimd"].append(_glag.pop(0))
    dv_pass1_done = dv.v

    if "o1" in taps:
        d_do1 = nc.dram_tensor("dbg_o1", [128, 16 * 1024], BF16,
                               kind="ExternalOutput").ap()
        tsem = Sem(nc, "tap_o1")
        SYw(dv, dv_pass1_done)
        SY(lambda E: E.dma_start(
            d_do1, o_fm.ap().rearrange("p c m -> p (c m)")), (tsem, 16))
        p.wait("vector", tsem, 16)

    Gw(a_st, 16 * 16)
    G(lambda E: E.collective_compute(
        "AllGather", ALU.bypass,
        replica_groups=[[0, 1, 2, 3], [4, 5, 6, 7]],
        ins=[d_astage.ap()[8 * 128:16 * 128, :].opt()],
        outs=[d_aall2.ap().opt()]), (cc_s, 1))

    aall_v1 = d_aall1.ap().rearrange("(r h p) m -> r h p m", r=4, h=8)
    aall_v2 = d_aall2.ap().rearrange("(r h p) m -> r h p m", r=4, h=8)
    for h in range(NH):
        av = aall_v1 if h < 8 else aall_v2
        hh_i = h % 8
        gates = ()
        if h == 0:
            gates = ((cc_s, 1),)
        elif h == 8:
            gates = ((cc_s, 2),)
        slot, wv = st_a4.load(
            [lambda E, b, av=av, hh_i=hh_i: E.dma_start(
                b[:], av[:, hh_i].rearrange("r p m -> p r m"))], gates=gates)
        slotq, wvq = st_qr.load(
            [lambda E, b, h=h: E.dma_start(b[:], d_qspill[h])],
            gates=((q_st, q_st.v),) if h == 0 else ())
        a4b = st_a4.buf(slot)
        Vw(st_a4.slots[slot], wv)
        V(lambda E, a4b=a4b, h=h: E.tensor_scalar(
            s_in[:], a4b[:, 0, :], cfr[:, 0, h:h + 1], None, ALU.mult), (dv, 1))
        for r in range(1, 4):
            incs = [(dv, 1)]
            if r == 3:
                incs.append((st_a4.free, 1))
            V(lambda E, a4b=a4b, h=h, r=r: E.scalar_tensor_tensor(
                s_in[:], a4b[:, r, :], cfr[:, r, h:h + 1], s_in[:],
                ALU.mult, ALU.add), *incs)
        Aw(dv)
        A(lambda E: E.copy(s_in_b[:], s_in[:]), (ac, 1))
        A(lambda E, h=h: E.activation(dpa[:], blkbc[:], AF.Exp,
                                      scale=negs[:, h:h + 1]), (ac, 1))
        qreb = st_qr.buf(slotq)
        for ic in range(2):
            bank = pf[ic]
            Tw(ac)
            Tw(st_qr.slots[slotq], wvq)
            Tw(bank.rd)
            T(lambda E, ic=ic, bank=bank, qreb=qreb: E.matmul(
                bank.ap[:], s_in_b[:], qreb[:, ic * 512:(ic + 1) * 512],
                start=True, stop=True),
              (bank.pe, 1), *(((st_qr.free, 1),) if ic == 1 else ()))
            Vw(bank.pe)
            Vw(ac)
            for bi in range(4):
                i = ic * 4 + bi
                incs = [(dv, 1)]
                if bi == 3:
                    incs.append((bank.rd, 1))
                V(lambda E, i=i, bi=bi, h=h, bank=bank: E.scalar_tensor_tensor(
                    o_fm[:, h, i * 128:(i + 1) * 128],
                    bank.ap[:, bi * 128:(bi + 1) * 128], dpa[:, i:i + 1],
                    o_fm[:, h, i * 128:(i + 1) * 128], ALU.mult, ALU.add),
                  *incs)
    dv_pass2_done = dv.v

    if "o2" in taps:
        d_do2 = nc.dram_tensor("dbg_o2", [128, 16 * 1024], BF16,
                               kind="ExternalOutput").ap()
        tsem = Sem(nc, "tap_o2")
        SYw(dv, dv_pass2_done)
        SY(lambda E: E.dma_start(
            d_do2, o_fm.ap().rearrange("p c m -> p (c m)")), (tsem, 16))
        p.wait("vector", tsem, 16)

    for fc in range(4):
        dmas = [
            (lambda E, b, fc=fc, cg=cg: E.dma_start(
                b[:, cg * 4:(cg + 1) * 4, :],
                d_wout[cg * 512:(cg + 1) * 512, fc * 512:(fc + 1) * 512]
                .rearrange("(g p) f -> p g f", p=128)))
            for cg in range(4)]
        gates = ((dv, dv_ln_done),) if fc == 0 else ()
        slot, wv = st_wo.load(dmas, gates=gates)
        wob = st_wo.buf(slot)
        Tw(st_wo.slots[slot], wv)
        if fc == 0:
            Tw(dv, dv_pass2_done)
        for blkset in ((0, 1, 2, 3, 4, 5), (6, 7)):
            for bi in range(len(blkset)):
                Tw(pf[bi].rd)
            for c in range(16):
                for bi, blk in enumerate(blkset):
                    bank = pf[bi]
                    T(lambda E, c=c, blk=blk, bank=bank, wob=wob: E.matmul(
                        bank.ap[:], o_fm[:, c, blk * 128:(blk + 1) * 128],
                        wob[:, c, :], start=(c == 0), stop=(c == 15)),
                      *(((bank.pe, 1),) if c == 15 else ()))
            for bi, blk in enumerate(blkset):
                bank = pf[bi]
                Vw(bank.pe)
                incs = [(bank.rd, 1), (dv, 1)]
                if blk == 7:
                    incs.append((st_wo.free, 1))
                V(lambda E, blk=blk, fc=fc, bank=bank: E.tensor_tensor(
                    out_tm[:, blk, fc * 512:(fc + 1) * 512], bank.ap[:],
                    bout_bc[:, fc * 512:(fc + 1) * 512], ALU.add), *incs)
    dv_att_done = dv.v

    if "att" in taps:
        d_datt = nc.dram_tensor("dbg_att", [1024, 2048], F32,
                                kind="ExternalOutput").ap()
        tsem = Sem(nc, "tap_att")
        SYw(dv, dv_att_done)
        SY(lambda E: E.dma_start(
            d_datt.rearrange("(b p) f -> p b f", p=128),
            out_tm[:]), (tsem, 16))
        p.wait("vector", tsem, 16)

    st_xg = Stream(nc, p, "stxg", "sync", xn_g)
    ybu = [0] * 4
    dv_xg_parity = [0, 0]

    def emit_gather_loads(e):
        par = e % 2
        toks = []
        for blk in range(3):
            gates = ()
            if e == 0 and blk == 0:
                gates = ((rsc, rsc_done), (dv, dv_pass1_done))
            r0 = e * 384 + blk * 128
            sx, wvx = st_xg.load(
                [lambda E, b, r0=r0: E.dma_start(
                    b[:], d_xperm[r0:r0 + 128, :])], gates=gates)
            toks.append((sx, wvx))
        return toks

    def emit_gather_transp(e, toks):
        par = e % 2
        xg = xn_ge2[par]
        for blk in range(3):
            sx, wvx = toks[blk]
            xgb = st_xg.buf(sx)
            Vw(st_xg.slots[sx], wvx)
            V(lambda E, xgb=xgb, par=par, blk=blk: E.tensor_copy(
                w_g[par][:, blk:blk + 1], xgb[:, 2048:2049]), (dv, 1))
            for half in range(2):
                bank = pbf[half]
                Tw(st_xg.slots[sx], wvx)
                if e == 0 and blk == 0 and half == 0:
                    Tw(dv, dv_att_done)
                Tw(bank.rd)
                for c8 in range(8):
                    ch = half * 8 + c8
                    T(lambda E, ch=ch, c8=c8, bank=bank, xgb=xgb: E.transpose(
                        bank.ap[:, c8 * 128:(c8 + 1) * 128],
                        xgb[:, ch * 128:(ch + 1) * 128], ident_b[:]),
                      *(((bank.pe, 1),) if c8 == 7 else ()))
                Vw(bank.pe)
                incs = [(bank.rd, 1), (dv, 1)]
                if half == 1:
                    incs.append((st_xg.free, 1))
                V(lambda E, half=half, blk=blk, bank=bank, xg=xg: E.tensor_copy(
                    xg[:, half * 8:half * 8 + 8, blk * 128:(blk + 1) * 128],
                    bank.ap.rearrange("p (c m) -> p c m", c=8)), *incs)
        dv_xg_parity[par] = dv.v

    toks_cur = emit_gather_loads(0)
    emit_gather_transp(0, toks_cur)
    for e in range(NE):
        if e + 1 < NE:
            toks_nxt = emit_gather_loads(e + 1)
        par = e % 2
        xg = xn_ge2[par]
        dv_xg = dv_xg_parity[par]

        for t in range(NT):
            g0 = ((q_st, q_st.v),) if (e == 0 and t == 0) else ()
            if e == 0 and t == 2:
                g0 = ((dv, dv_att_done),)
            s1, wv1 = st_w12.load(
                [lambda E, b, e=e, t=t: E.dma_start(
                    b[:], d_w12[e, t])],
                gates=g0)
            b1 = pf[0] if t % 2 == 0 else pf[2]
            b2 = pf[1] if t % 2 == 0 else pf[3]
            wb = st_w12.buf(s1)
            for mi, bank in ((0, b1), (1, b2)):
                Tw(st_w12.slots[s1], wv1)
                if t == 0:
                    Tw(dv, dv_xg)
                Tw(bank.rd)
                for dd in range(16):
                    T(lambda E, dd=dd, mi=mi, bank=bank, wb=wb, xg=xg:
                      E.matmul(
                          bank.ap[:, 0:384], wb[:, mi, dd, :],
                          xg[:, dd, 0:384],
                          start=(dd == 0), stop=(dd == 15)),
                      *(((bank.pe, 1),) if dd == 15 else ()))
            Aw(b1.pe)
            A(lambda E, b1=b1: E.activation(
                sil[:], b1.ap[:, 0:384], AF.Silu), (ac, 1), (b1.rd, 1))
            Vw(ac)
            Vw(b2.pe)
            V(lambda E, t=t, b2=b2: E.tensor_tensor(
                hh[:, t, :], sil[:], b2.ap[:, 0:384], ALU.mult),
              (b2.rd, 1), (dv, 1), (st_w12.free, 1))
        dv_h_done = dv.v

        if e + 1 < NE:
            emit_gather_transp(e + 1, toks_nxt)

        for fc in range(4):
            dmas = [lambda E, b, e=e, fc=fc: E.dma_start(
                b[:], d_w3[e, fc])]
            g3 = ((dv, dv_att_done), (rsc, rsc_done)) \
                if (e == 0 and fc == 0) else ()
            sl3, wv3 = st_w3.load(dmas, gates=g3)
            w3buf = st_w3.buf(sl3)
            Tw(st_w3.slots[sl3], wv3)
            if fc == 0:
                Tw(dv, dv_h_done)
            for blk in range(3):
                Tw(pf[blk].rd)
            for t in range(NT):
                for blk in range(3):
                    bank = pf[blk]
                    T(lambda E, t=t, blk=blk, bank=bank, w3buf=w3buf:
                      E.matmul(
                          bank.ap[:], hh[:, t, blk * 128:(blk + 1) * 128],
                          w3buf[:, t, :], start=(t == 0), stop=(t == 10)),
                      *(((bank.pe, 1),) if t == 10 else ()))
            for blk in range(3):
                bank = pf[blk]
                yb = ybounce[ybu[0] % 3]
                Vw(bank.pe)
                if ybu[0] >= 3:
                    Vw(yst, 16 * (ybu[0] - 2))
                incs = [(bank.rd, 1), (dv, 1)]
                if blk == 2:
                    incs.append((st_w3.free, 1))
                V(lambda E, blk=blk, par=par, bank=bank, yb=yb:
                  E.tensor_scalar(
                      yb[:], bank.ap[:],
                      w_g[par][:, blk:blk + 1], None, ALU.mult), *incs)
                r0 = e * 384 + blk * 128
                SYw(dv)
                SY(lambda E, r0=r0, fc=fc, yb=yb: E.dma_start(
                    d_yperm[r0:r0 + 128, fc * 512:(fc + 1) * 512], yb[:]),
                   (yst, 16))
                ybu[0] += 1
    dv_ffn_done = dv.v
    yst_done = yst.v

    for blk in range(NB):
        if blk == 0:
            Gw(yst, yst_done)
            Gw(yz, 16)
        else:
            Gw(dv, dv.v)
        for j, itgt in ((0, idx1_i), (1, idx2_i)):
            G(lambda E, blk=blk, j=j, itgt=itgt: E.indirect_dma_start(
                yg[j][:], None, d_yperm,
                IndirectOffsetOnAxis(itgt[:, blk:blk + 1], 0),
                bounds_check=3072, oob_is_err=False), (ygl, 16))
        gates = ((dv, dv_ffn_done),) if blk == 0 else ()
        slot, wv = st_x.load([lambda E, b, blk=blk: E.dma_start(
            b[:], d_x[blk * 128:(blk + 1) * 128, :])], gates=gates)
        xb = st_x.buf(slot)
        Vw(ygl, ygl.v)
        V(lambda E, blk=blk: E.tensor_tensor(
            out_tm[:, blk, :], out_tm[:, blk, :], yg[0][:], ALU.add), (dv, 1))
        Vw(dv)
        V(lambda E, blk=blk: E.tensor_tensor(
            out_tm[:, blk, :], out_tm[:, blk, :], yg[1][:], ALU.add), (dv, 1))
        Vw(st_x.slots[slot], wv)
        Vw(dv)
        V(lambda E, blk=blk, xb=xb: E.tensor_tensor(
            out_tm[:, blk, :], out_tm[:, blk, :], xb[:], ALU.add),
          (st_x.free, 1), (dv, 1))
        SYw(dv)
        SY(lambda E, blk=blk: E.dma_start(
            d_out[blk * 128:(blk + 1) * 128, :], out_tm[:, blk, :]),
           (o_st, 16))
    if "route" in taps:
        d_r1 = nc.dram_tensor("dbg_idx1", [128, 8], I32, kind="ExternalOutput").ap()
        d_r2 = nc.dram_tensor("dbg_idx2", [128, 8], I32, kind="ExternalOutput").ap()
        d_rs = nc.dram_tensor("dbg_rsc", [128, 64], F32, kind="ExternalOutput").ap()
        d_xp = nc.dram_tensor("dbg_xperm", [768, 2049], BF16, kind="ExternalOutput").ap()
        d_yp = nc.dram_tensor("dbg_yperm", [768, 2048], F32, kind="ExternalOutput").ap()
        SY(lambda E: E.dma_start(d_r1, idx1_i[:]), (o_st, 16))
        SY(lambda E: E.dma_start(d_r2, idx2_i[:]), (o_st, 16))
        SY(lambda E: E.dma_start(
            d_rs, route_sc.ap().rearrange("p a b -> p (a b)")), (o_st, 16))
        SY(lambda E: E.dma_start(d_xp, d_xperm[0:768, 0:2049]), (o_st, 16))
        SY(lambda E: E.dma_start(d_yp, d_yperm[0:768, :]), (o_st, 16))
    if "xn" in taps:
        d_dxn = nc.dram_tensor("dbg_xn", [128, 16 * 1024], BF16,
                               kind="ExternalOutput").ap()
        SY(lambda E: E.dma_start(
            d_dxn, xn_fm.ap().rearrange("p c m -> p (c m)")), (o_st, 16))
    if "wtm" in taps:
        d_dwt = nc.dram_tensor("dbg_wtm", [128, 64], F32,
                               kind="ExternalOutput").ap()
        SY(lambda E: E.dma_start(
            d_dwt, w_tm.ap().rearrange("p b e -> p (b e)")), (o_st, 16))
    SYw(o_st, o_st.v)

    with nc.Block() as block:
        @block.tensor
        def _(E):
            for f in p.ops["tensor"]:
                f(E)

        @block.vector
        def _(E):
            for f in p.ops["vector"]:
                f(E)

        @block.scalar
        def _(E):
            for f in p.ops["scalar"]:
                f(E)

        @block.gpsimd
        def _(E):
            for f in p.ops["gpsimd"]:
                f(E)

        @block.sync
        def _(E):
            for f in p.ops["sync"]:
                f(E)
    return nc


def _host_inputs(ins):
    f32 = np.float32
    x = np.ascontiguousarray(np.asarray(ins["L0_attn_x"], f32).reshape(8192, 2048))
    Wqkv = np.asarray(ins["L0_attn_Wqkv"], f32)
    wqkv_s = np.ascontiguousarray(Wqkv.reshape(16, 128, 48, 128)
                                  .transpose(2, 1, 0, 3))
    bqkv_c = np.ascontiguousarray(
        np.asarray(ins["L0_attn_bqkv"], f32).reshape(48, 128).T)
    wout = np.ascontiguousarray(np.asarray(ins["L0_attn_Wout"], f32))
    bout_bc = np.ascontiguousarray(
        np.broadcast_to(np.asarray(ins["L0_attn_bout"], f32), (128, 2048)))
    slopes_bc = np.ascontiguousarray(
        np.broadcast_to(np.asarray(ins["L0_attn_slopes"], f32), (128, 16)))
    wgt = np.ascontiguousarray(np.asarray(ins["L0_ffn_Wg"], f32).T)
    W1 = np.asarray(ins["L0_ffn_W1"], f32)
    W2 = np.asarray(ins["L0_ffn_W2"], f32)
    W3 = np.asarray(ins["L0_ffn_W3"], f32)
    w1s = W1.transpose(0, 2, 1).reshape(8, 2048, 11, 128).transpose(0, 2, 1, 3)
    w2s = W2.transpose(0, 2, 1).reshape(8, 2048, 11, 128).transpose(0, 2, 1, 3)
    w12s = np.stack([w1s, w2s], axis=2)
    w12s = np.ascontiguousarray(w12s.reshape(8, 11, 2, 16, 128, 128)
                                .transpose(0, 1, 4, 2, 3, 5))
    w3t = np.ascontiguousarray(W3.transpose(0, 2, 1)
                               .reshape(8, 11, 128, 4, 512)
                               .transpose(0, 3, 2, 1, 4))

    ii = np.arange(128, dtype=f32)
    diffm = ii[None, :] - ii[:, None]
    diffx = np.where(diffm >= 0, diffm, BIG).astype(f32)
    iota_bc = np.ascontiguousarray(np.broadcast_to(ii, (128, 128)))
    cols = np.ascontiguousarray(
        np.stack([128.0 - ii, np.full(128, 128.0, f32)], axis=1).astype(f32))
    ident = np.eye(128, dtype=f32)
    blk_bc = np.ascontiguousarray(
        np.broadcast_to(128.0 * np.arange(8, dtype=f32), (128, 8)))
    iota8_bc = np.ascontiguousarray(
        np.broadcast_to(np.arange(8, dtype=f32), (128, 8)))

    shared = dict(wqkv_s=wqkv_s, bqkv_c=bqkv_c, wout=wout, bout_bc=bout_bc,
                  slopes_bc=slopes_bc, wgt=wgt, w12s=w12s, w3t=w3t,
                  diffx=diffx, iota_bc=iota_bc, cols=cols, ident=ident,
                  blk_bc=blk_bc, iota8_bc=iota8_bc)
    in_maps = []
    for core in range(8):
        q = core % 4
        tm = np.full((128, 4), -BIG, f32)
        for j in range(q):
            tm[:, j] = -1024.0 * (q - 1 - j)
        m = dict(shared)
        m["xs"] = np.ascontiguousarray(x[core * 1024:(core + 1) * 1024])
        m["neg_t_meta"] = tm
        in_maps.append(m)
    return in_maps


def kernel(**ins):
    in_maps = _host_inputs(ins)
    nc = build_nc()
    res = run_bass_kernel_spmd(nc, in_maps, core_ids=list(range(8)))
    out = np.concatenate([np.asarray(res.results[c]["out"]) for c in range(8)],
                         axis=0)
    return out.reshape(2, 4096, 2048).astype(np.float32)


# version 25
# speedup vs baseline: 1.0258x; 1.0167x over previous
import os as _os
import numpy as np
import concourse.bass as bass
import concourse.mybir as mybir
from concourse.bass_utils import run_bass_kernel_spmd
from concourse.bass_core_rust import IndirectOffsetOnAxis

F32 = mybir.dt.float32
I32 = mybir.dt.int32
BF16 = mybir.dt.bfloat16
AF = mybir.ActivationFunctionType
ALU = mybir.AluOpType
AX = mybir.AxisListType

ENGINES = ("tensor", "vector", "scalar", "gpsimd", "sync")
NB = 8
NH = 16
NE = 8
NT = 11
BIG = 1.0e7


class Sem:
    __slots__ = ("h", "v")

    def __init__(self, nc, name):
        self.h = nc.alloc_semaphore(name)
        self.v = 0


class Bank:
    def __init__(self, nc, name, ap):
        self.ap = ap
        self.pe = Sem(nc, name + "p")
        self.rd = Sem(nc, name + "r")


class Prog:
    def __init__(self, nc):
        self.nc = nc
        self.ops = {e: [] for e in ENGINES}

    def op(self, eng, fn, *incs):
        for s, n in incs:
            s.v += n
        pairs = tuple((s.h, n) for s, n in incs)

        def g(E, fn=fn, pairs=pairs):
            ins = fn(E)
            if pairs:
                ins.then_inc(*pairs[0])
                for h, n in pairs[1:]:
                    E.nop().then_inc(h, n)
        self.ops[eng].append(g)

    def wait(self, eng, sem, val=None):
        v = sem.v if val is None else val
        self.ops[eng].append(lambda E, s=sem.h, v=v: E.wait_ge(s, v))


class Stream:

    def __init__(self, nc, p, name, queue, bufs):
        self.p = p
        self.queue = queue
        self.bufs = bufs
        self.slots = [Sem(nc, f"{name}_s{i}") for i in range(len(bufs))]
        self.free = Sem(nc, f"{name}_f")
        self.rounds = 0

    def load(self, dma_fns, gates=()):
        d = len(self.bufs)
        slot = self.rounds % d
        for g in gates:
            self.p.wait(self.queue, g[0], g[1])
        if self.rounds >= d:
            self.p.wait(self.queue, self.free, self.rounds - d + 1)
        s = self.slots[slot]
        for fn in dma_fns:
            self.p.op(self.queue, (lambda E, f=fn, b=self.bufs[slot]: f(E, b)),
                      (s, 16))
        self.rounds += 1
        return slot, s.v

    def buf(self, slot):
        return self.bufs[slot]


def build_nc(taps=()):
    nc = bass.Bass("TRN2", target_bir_lowering=False,
                   detect_race_conditions=False)
    p = Prog(nc)

    d_x = nc.dram_tensor("xs", [1024, 2048], F32, kind="ExternalInput").ap()
    d_wqkv = nc.dram_tensor("wqkv_s", [48, 128, 16, 128], F32, kind="ExternalInput").ap()
    d_bqkv = nc.dram_tensor("bqkv_c", [128, 48], F32, kind="ExternalInput").ap()
    d_wout = nc.dram_tensor("wout", [4, 128, 16, 512], F32, kind="ExternalInput").ap()
    d_boutb = nc.dram_tensor("bout_bc", [128, 2048], F32, kind="ExternalInput").ap()
    d_slops = nc.dram_tensor("slopes_bc", [128, 16], F32, kind="ExternalInput").ap()
    d_wgt = nc.dram_tensor("wgt", [2048, 8], F32, kind="ExternalInput").ap()
    d_w12 = nc.dram_tensor("w12s", [8, 11, 128, 2, 16, 128], F32,
                           kind="ExternalInput").ap()
    d_w3 = nc.dram_tensor("w3t", [8, 4, 128, 11, 512], F32, kind="ExternalInput").ap()
    d_diff = nc.dram_tensor("diffx", [128, 128], F32, kind="ExternalInput").ap()
    d_iota = nc.dram_tensor("iota_bc", [128, 128], F32, kind="ExternalInput").ap()
    d_cols = nc.dram_tensor("cols", [128, 2], F32, kind="ExternalInput").ap()
    d_tmet = nc.dram_tensor("neg_t_meta", [128, 4], F32, kind="ExternalInput").ap()
    d_ident = nc.dram_tensor("ident", [128, 128], F32, kind="ExternalInput").ap()
    d_blkbc = nc.dram_tensor("blk_bc", [128, 8], F32, kind="ExternalInput").ap()
    d_iota8 = nc.dram_tensor("iota8_bc", [128, 8], F32, kind="ExternalInput").ap()

    d_out = nc.dram_tensor("out", [1024, 2048], F32, kind="ExternalOutput").ap()

    d_astage = nc.dram_tensor("a_stage", [NH * 128, 128], F32, kind="Internal")
    d_aall1 = nc.dram_tensor("a_all1", [4 * 8 * 128, 128], F32, kind="Internal")
    d_aall2 = nc.dram_tensor("a_all2", [4 * 8 * 128, 128], F32, kind="Internal")
    d_qspill = nc.dram_tensor("q_spill", [NH, 128, 1024], BF16, kind="Internal").ap()
    d_xnsp = nc.dram_tensor("xn_sp", [1024, 2048], BF16, kind="Internal").ap()
    d_xperm = nc.dram_tensor("xn_perm", [3073, 2176], BF16, kind="Internal").ap()
    d_yperm = nc.dram_tensor("y_perm", [3073, 2048], F32, kind="Internal").ap()

    BASE = 16512
    LIMIT = 229344

    def sb(name, shape, dtype, rel):
        nbytes = int(np.prod(shape[1:])) * mybir.dt.size(dtype)
        assert BASE + rel + nbytes <= LIMIT, (name, rel, nbytes)
        return nc.alloc_sbuf_tensor_at(name, shape, dtype, offset=BASE + rel)

    cur = 0
    xn_fm = sb("xn_fm", [128, 16, 1024], BF16, cur); cur += 32768
    OUT0 = cur
    out_tm = sb("out_tm", [128, 8, 2048], F32, OUT0); cur += 65536
    sq_scr = sb("sq_scr", [128, 2048], F32, OUT0)
    xn_tm2 = [sb("xn_tm", [128, 2048], BF16, OUT0 + 8192),
              sb("xn_tmB", [128, 2048], BF16, OUT0 + 16384)]
    s_bf_all = sb("s_bf_all", [128, 8, 128], BF16, OUT0 + 12288)
    s_f32 = sb("s_f32", [128, 128], F32, OUT0 + 14336)
    OFM0 = cur
    o_fm = sb("o_fm", [128, 16, 1024], BF16, OFM0); cur += 32768
    hh = sb("hh", [128, 11, 384], BF16, OFM0)
    sil = sb("sil", [128, 384], F32, OFM0 + 8448)
    xn_ge2 = [sb("xn_ge", [128, 16, 384], BF16, OFM0 + 10496),
              sb("xn_geB", [128, 16, 384], BF16, 14848)]
    xn_g = [sb(f"xn_g{i}", [128, 2176], BF16, 4352 * i) for i in range(2)]
    ybounce = [sb(f"ybounce{i}", [128, 512], F32, 8704 + 2048 * i)
               for i in range(3)]
    yg = [sb(f"yg{i}", [128, 2048], F32, 14848 + 8192 * i) for i in range(2)]
    BOUT0 = cur
    bout_bc = sb("bout_bc", [128, 2048], F32, cur); cur += 8192
    ident_f = sb("ident_f", [128, 128], F32, cur); cur += 512
    ident_b = sb("ident_b", [128, 128], BF16, cur); cur += 256
    diffx = sb("diffx", [128, 128], F32, cur); cur += 512
    iota_bc = sb("iota_bc", [128, 128], F32, cur); cur += 512
    slop = sb("slop", [128, 16], F32, cur); cur += 64
    negs = sb("negs", [128, 16], F32, cur); cur += 64
    cols = sb("cols", [128, 2], F32, cur); cur += 32
    tmet = sb("tmet", [128, 4], F32, cur); cur += 32
    cfr = sb("cfr", [128, 4, 16], F32, cur); cur += 256
    d_all = sb("d_all", [128, 16], F32, cur); cur += 64
    w_tm = sb("w_tm", [128, 8, 8], F32, cur); cur += 256
    dpow = sb("dpow", [128, 1], F32, cur); cur += 32
    blkbc = sb("blkbc", [128, 8], F32, cur); cur += 32
    dpa = sb("dpa", [128, 8], F32, cur); cur += 32
    gt_all = sb("gt_all", [128, 64], F32, cur); cur += 256
    iota8 = sb("iota8", [128, 8], F32, cur); cur += 32
    route_ind = sb("route_ind", [128, 8, 16], F32, cur); cur += 512
    route_sc = sb("route_sc", [128, 8, 8], F32, cur); cur += 256
    w_g = [sb(f"w_g{i}", [128, 3], F32, cur + 32 * i) for i in range(2)]
    cur += 64
    sel_all = sb("sel_all", [128, 64], F32, cur); cur += 256
    idx1_i = sb("idx1_i", [128, 8], I32, cur); cur += 32
    idx2_i = sb("idx2_i", [128, 8], I32, cur); cur += 32
    s_in = sb("s_in", [128, 128], F32, cur); cur += 512
    s_in_b = sb("s_in_b", [128, 128], BF16, cur); cur += 256
    qd_b = sb("qd_b", [128, 128], BF16, cur); cur += 256
    stat = sb("stat", [128, 8], F32, cur); cur += 32
    bq_c = sb("bq_c", [128, 48], F32, cur); cur += 192
    wg_sb = sb("wg_sb", [128, 16, 8], BF16, cur); cur += 256
    smb = [sb(f"smb{i}", [128, 128], BF16, cur + 256 * i) for i in range(3)]
    cur += 768
    cur = (cur + 63) & ~63

    A0 = cur
    q_fmB = sb("q_fmB", [128, 1024], BF16, A0)
    q2_fm = sb("q2_fm", [128, 1024], BF16, A0 + 2048)
    k_fmB = sb("k_fmB", [128, 1024], BF16, A0 + 4096)
    v_fmB = sb("v_fmB", [128, 1024], BF16, A0 + 6144)
    k2_tm = sb("k2_tm", [128, 8, 128], BF16, A0 + 8192)
    v_tm = sb("v_tm", [128, 8, 128], BF16, A0 + 10240)
    wqb = [sb(f"wqb{i}", [128, 16, 128], BF16, A0 + 12288 + 4096 * i)
           for i in range(2)]
    w12b = [sb(f"w12b{i}", [128, 2, 16, 128], BF16, A0 + 8192 * i)
            for i in range(2)]
    cur += 20480
    B0 = cur
    xbuf = [sb(f"xbuf{i}", [128, 2048], F32, B0 + 8192 * i) for i in range(2)]
    woutb = [sb("woutb0", [128, 16, 512], BF16, B0)]
    cur += 16384
    C0 = cur
    woutb.append(sb("woutb1", [128, 16, 512], BF16, C0))
    w3b = [sb(f"w3b{i}", [128, 11, 512], BF16, C0 + 11264 * i) for i in range(2)]
    stage = [sb(f"stage{i}", [128, 2176], BF16, C0 + 4352 * i)
             for i in range(2)]
    cur += 22528
    D0 = cur
    g_sb = sb("g_sb", [8, 1024], F32, D0)
    qre = [sb(f"qre{i}", [128, 1024], BF16, D0 + 2048 * i) for i in range(2)]
    qre.append(sb("qre2", [128, 1024], BF16, OUT0 + 22528))
    cur += 4096
    a4 = [sb(f"a4{i}", [128, 4, 128], F32, OUT0 + 16384 + 2048 * i)
          for i in range(3)]
    assert BASE + cur <= LIMIT, cur

    pf = [Bank(nc, f"pf{i}", nc.alloc_psum_tensor(f"pf{i}", [128, 512], F32).ap())
          for i in range(6)]
    pbf = [Bank(nc, f"pbf{i}",
                nc.alloc_psum_tensor(f"pbf{i}", [128, 1024], BF16).ap())
           for i in range(2)]

    c_ld = Sem(nc, "c_ld")
    dv = Sem(nc, "dv")
    ac = Sem(nc, "ac")
    a_st = Sem(nc, "a_st")
    q_st = Sem(nc, "q_st")
    o_st = Sem(nc, "o_st")
    cc_s = Sem(nc, "cc_s")
    xpz = Sem(nc, "xpz")
    pcast = Sem(nc, "pcast")
    xnsp = Sem(nc, "xnsp")
    rsc = Sem(nc, "rsc")
    sgl = Sem(nc, "sgl")
    swc = Sem(nc, "swc")
    yz = Sem(nc, "yz")
    yst = Sem(nc, "yst")
    ygl = Sem(nc, "ygl")

    st_x = Stream(nc, p, "stx", "sync", xbuf)
    st_wq = Stream(nc, p, "stwq", "gpsimd", wqb)
    w12b.append(sb("w12b2", [128, 2, 16, 128], BF16, BOUT0))
    st_w12 = Stream(nc, p, "stw12", "gpsimd", w12b)
    st_wo = Stream(nc, p, "stwo", "gpsimd", woutb)
    st_w3 = Stream(nc, p, "stw3", "gpsimd", w3b)
    st_a4 = Stream(nc, p, "sta4", "sync", a4)
    st_qr = Stream(nc, p, "stqr", "sync", qre)

    def T(fn, *incs): p.op("tensor", fn, *incs)
    def V(fn, *incs): p.op("vector", fn, *incs)
    def A(fn, *incs): p.op("scalar", fn, *incs)
    def G(fn, *incs): p.op("gpsimd", fn, *incs)
    def SY(fn, *incs): p.op("sync", fn, *incs)
    def Tw(s, v=None): p.wait("tensor", s, v)
    def Vw(s, v=None): p.wait("vector", s, v)
    def Aw(s, v=None): p.wait("scalar", s, v)
    def Gw(s, v=None): p.wait("gpsimd", s, v)
    def SYw(s, v=None): p.wait("sync", s, v)

    for dst, src in ((ident_f, d_ident), (ident_b, d_ident), (diffx, d_diff),
                     (iota_bc, d_iota), (slop, d_slops), (cols, d_cols),
                     (tmet, d_tmet), (bout_bc, d_boutb), (bq_c, d_bqkv),
                     (blkbc, d_blkbc), (iota8, d_iota8)):
        G(lambda E, dst=dst, src=src: E.dma_start(dst[:], src), (c_ld, 16))
    G(lambda E: E.dma_start(wg_sb[:], d_wgt.rearrange("(c p) e -> p c e", p=128)),
      (c_ld, 16))
    n_const = 12
    _pc = []
    V(lambda E: E.memset(stage[0][:], 0.0), (dv, 1))
    V(lambda E: E.memset(stage[1][:], 0.0), (dv, 1))
    dv_stz = dv.v
    Aw(dv, dv_stz)
    for k in range(25):
        r0 = k * 128
        n = 128 if k < 24 else 1
        A(lambda E, r0=r0, n=n: E.dma_start(
            d_xperm[r0:r0 + n, :], stage[0][0:n, :]), (xpz, 16))
    Vw(c_ld, 16 * n_const)
    V(lambda E: E.tensor_scalar(negs[:], slop[:], -1.0, None, ALU.mult), (dv, 1))
    Aw(dv)
    for r in range(4):
        A(lambda E, r=r: E.activation(cfr[:, r, :], slop[:], AF.Exp,
                                      scale=tmet[:, r:r + 1]), (ac, 1))
    for h in range(NH):
        A(lambda E, h=h: E.activation(d_all[:, h:h + 1], cols[:, 1:2], AF.Exp,
                                      scale=negs[:, h:h + 1]), (ac, 1))

    for blk in range(NB):
        slot, wv = st_x.load([lambda E, b, blk=blk: E.dma_start(
            b[:], d_x[blk * 128:(blk + 1) * 128, :])])
        xb = st_x.buf(slot)
        xtm = xn_tm2[blk % 2]
        Vw(st_x.slots[slot], wv)
        V(lambda E, xb=xb: E.reduce_sum(stat[:, 0:1], xb[:], axis=AX.X), (dv, 1))
        Aw(st_x.slots[slot], wv)
        A(lambda E, xb=xb: E.activation(
            sq_scr[:], xb[:], AF.Square, accum_out=stat[:, 1:2]), (ac, 1))
        Vw(dv)
        V(lambda E: E.tensor_scalar(stat[:, 2:3], stat[:, 0:1], 1.0 / 2048,
                                    None, ALU.mult), (dv, 1))
        Vw(ac)
        V(lambda E: E.tensor_scalar(stat[:, 3:4], stat[:, 1:2], 1.0 / 2048,
                                    None, ALU.mult), (dv, 1))
        Vw(dv)
        V(lambda E: E.scalar_tensor_tensor(stat[:, 4:5], stat[:, 2:3],
                                           stat[:, 2:3], stat[:, 3:4],
                                           ALU.mult, ALU.subtract), (dv, 1))
        Vw(dv)
        V(lambda E: E.tensor_scalar(stat[:, 5:6], stat[:, 4:5], -1.0, 1.0e-5,
                                    ALU.mult, ALU.add), (dv, 1))
        Aw(dv)
        A(lambda E: E.activation(stat[:, 5:6], stat[:, 5:6], AF.Ln), (ac, 1))
        Aw(ac)
        A(lambda E: E.activation(stat[:, 6:7], stat[:, 5:6], AF.Exp,
                                 scale=-0.5), (ac, 1))
        Vw(ac)
        Vw(xnsp, 16 * max(0, blk - 1))
        V(lambda E, xb=xb, xtm=xtm: E.tensor_scalar(
            xtm[:], xb[:], stat[:, 2:3], stat[:, 6:7],
            ALU.subtract, ALU.mult), (dv, 1), (st_x.free, 1))
        SYw(dv)
        SY(lambda E, blk=blk, xtm=xtm: E.dma_start(
            d_xnsp[blk * 128:(blk + 1) * 128, :], xtm[:]), (xnsp, 16))
        if blk == 0:
            Tw(c_ld, 16 * n_const)
        for half in range(2):
            bank = pbf[half]
            Tw(dv)
            Tw(bank.rd)
            for c8 in range(8):
                c = half * 8 + c8
                T(lambda E, c=c, c8=c8, bank=bank, xtm=xtm: E.transpose(
                    bank.ap[:, c8 * 128:(c8 + 1) * 128],
                    xtm[:, c * 128:(c + 1) * 128], ident_b[:]),
                  *(((bank.pe, 1),) if c8 == 7 else ()))
            Vw(bank.pe)
            V(lambda E, half=half, blk=blk, bank=bank: E.tensor_copy(
                xn_fm[:, half * 8:half * 8 + 8, blk * 128:(blk + 1) * 128],
                bank.ap.rearrange("p (c m) -> p c m", c=8)),
              (bank.rd, 1), (dv, 1))
        if blk == 3:
            dv_ln_half = dv.v
    dv_ln_done = dv.v

    for ic in range(2):
        bank = pf[2 + ic]
        Tw(dv, dv_ln_half if ic == 0 else dv_ln_done)
        Tw(bank.rd)
        for dd in range(16):
            T(lambda E, dd=dd, ic=ic, bank=bank: E.matmul(
                bank.ap[0:8, :], wg_sb[:, dd, :],
                xn_fm[:, dd, ic * 512:(ic + 1) * 512],
                start=(dd == 0), stop=(dd == 15)),
              *(((bank.pe, 1),) if dd == 15 else ()))
        Vw(bank.pe)
        V(lambda E, ic=ic, bank=bank: E.tensor_copy(
            g_sb[:, ic * 512:(ic + 1) * 512], bank.ap[0:8, :]),
          (bank.rd, 1), (dv, 1))
    bank = pf[4]
    Tw(dv)
    Tw(bank.rd)
    for blk in range(NB):
        T(lambda E, blk=blk, bank=bank: E.transpose(
            bank.ap[:, blk * 8:(blk + 1) * 8],
            g_sb[:, blk * 128:(blk + 1) * 128],
            ident_f[0:8, 0:8]), *(((bank.pe, 1),) if blk == 7 else ()))
    Vw(bank.pe)
    V(lambda E, bank=bank: E.tensor_copy(gt_all[:], bank.ap[:, 0:64]),
      (bank.rd, 1), (dv, 1))
    for blk in range(NB):
        gt = gt_all[:, blk * 8:(blk + 1) * 8]
        i1 = route_ind[:, blk, 0:8]
        i2 = route_ind[:, blk, 8:16]
        Vw(dv)
        V(lambda E, gt=gt: E.reduce_max(dpow[:], gt, axis=AX.X), (dv, 1))
        Vw(dv)
        V(lambda E, gt=gt, i1=i1: E.tensor_scalar(i1, gt, dpow[:],
                                                  None, ALU.is_ge), (dv, 1))
        Vw(dv)
        V(lambda E, gt=gt, i1=i1: E.scalar_tensor_tensor(
            s_in[:, 8:16], i1, -1.0e30, gt, ALU.mult, ALU.add), (dv, 1))
        Vw(dv)
        V(lambda E: E.reduce_max(s_in[:, 16:17], s_in[:, 8:16], axis=AX.X),
          (dv, 1))
        Vw(dv)
        V(lambda E, i2=i2: E.tensor_scalar(i2, s_in[:, 8:16],
                                           s_in[:, 16:17], None, ALU.is_ge),
          (dv, 1))
        V(lambda E: E.tensor_tensor(s_in[:, 17:18], dpow[:], s_in[:, 16:17],
                                    ALU.subtract), (dv, 1))
        Aw(dv)
        A(lambda E, blk=blk: E.activation(route_sc[:, blk, 2:3],
                                          s_in[:, 17:18], AF.Sigmoid),
          (ac, 1))
        Vw(ac)
        V(lambda E, blk=blk: E.tensor_scalar(
            route_sc[:, blk, 3:4], route_sc[:, blk, 2:3], -1.0, 1.0,
            ALU.mult, ALU.add), (dv, 1))
        V(lambda E, blk=blk, i1=i1: E.tensor_scalar(
            s_in[:, 32:40], i1, route_sc[:, blk, 2:3], None, ALU.mult),
          (dv, 1))
        Vw(dv)
        V(lambda E, blk=blk, i2=i2: E.scalar_tensor_tensor(
            w_tm[:, blk, :], i2, route_sc[:, blk, 3:4], s_in[:, 32:40],
            ALU.mult, ALU.add), (dv, 1))
        V(lambda E, i1=i1: E.tensor_tensor(s_in[:, 40:48], i1, iota8[:],
                                           ALU.mult), (dv, 1))
        Vw(dv)
        V(lambda E, blk=blk: E.reduce_sum(route_sc[:, blk, 0:1],
                                          s_in[:, 40:48], axis=AX.X), (dv, 1))
        V(lambda E, i2=i2: E.tensor_tensor(s_in[:, 40:48], i2, iota8[:],
                                           ALU.mult), (dv, 1))
        Vw(dv)
        V(lambda E, blk=blk: E.reduce_sum(route_sc[:, blk, 1:2],
                                          s_in[:, 40:48], axis=AX.X), (dv, 1))
        V(lambda E, blk=blk, i1=i1, i2=i2: E.tensor_tensor(
            sel_all[:, blk * 8:(blk + 1) * 8], i1, i2, ALU.add), (dv, 1))


    for grp in range(2):
        bank = pf[2 + grp]
        Tw(dv)
        Tw(bank.rd)
        for b4 in range(4):
            blk = grp * 4 + b4
            T(lambda E, blk=blk, b4=b4, bank=bank: E.transpose(
                bank.ap[0:8, b4 * 128:(b4 + 1) * 128],
                sel_all[:, blk * 8:(blk + 1) * 8], ident_f[0:128, 0:128]),
              *(((bank.pe, 1),) if b4 == 3 else ()))
        Vw(bank.pe)
        V(lambda E, grp=grp, bank=bank: E.tensor_copy(
            g_sb[:, grp * 512:(grp + 1) * 512], bank.ap[0:8, 0:512]),
          (bank.rd, 1), (dv, 1))
    zv = sq_scr.ap()[0:8, 0:1024]
    pv = sq_scr.ap()[0:8, 1024:2048]
    V(lambda E, zv=zv: E.memset(zv, 0.0), (dv, 1))
    Vw(dv)
    V(lambda E, zv=zv, pv=pv: E.tensor_tensor_scan(
        pv, g_sb[:], zv, 0.0, ALU.add, ALU.add), (dv, 1))
    bank = pf[4]
    Tw(dv)
    Tw(bank.rd)
    for blk in range(NB):
        T(lambda E, blk=blk, bank=bank, pv=pv: E.transpose(
            bank.ap[:, blk * 8:(blk + 1) * 8],
            pv[0:8, blk * 128:(blk + 1) * 128], ident_f[0:8, 0:8]),
          *(((bank.pe, 1),) if blk == 7 else ()))
    Vw(bank.pe)
    V(lambda E, bank=bank: E.tensor_copy(gt_all[:], bank.ap[:, 0:64]),
      (bank.rd, 1), (dv, 1))
    for blk in range(NB):
        i1 = route_ind[:, blk, 0:8]
        i2 = route_ind[:, blk, 8:16]
        pos = gt_all[:, blk * 8:(blk + 1) * 8]
        Vw(dv)
        V(lambda E, i1=i1, pos=pos: E.tensor_tensor(s_in[:, 40:48], i1, pos,
                                                    ALU.mult), (dv, 1))
        Vw(dv)
        V(lambda E, blk=blk: E.reduce_sum(route_sc[:, blk, 4:5],
                                          s_in[:, 40:48], axis=AX.X), (dv, 1))
        V(lambda E, i2=i2, pos=pos: E.tensor_tensor(s_in[:, 40:48], i2, pos,
                                                    ALU.mult), (dv, 1))
        Vw(dv)
        V(lambda E, blk=blk: E.reduce_sum(route_sc[:, blk, 5:6],
                                          s_in[:, 40:48], axis=AX.X), (dv, 1))
        for k, (pcol, icol, itgt) in enumerate(
                ((4, 0, idx1_i), (5, 1, idx2_i))):
            Vw(dv)
            V(lambda E, blk=blk, icol=icol: E.tensor_scalar(
                s_in[:, 49:50], route_sc[:, blk, icol:icol + 1], 384.0, -1.0,
                ALU.mult, ALU.add), (dv, 1))
            Vw(dv)
            V(lambda E, blk=blk, pcol=pcol: E.tensor_tensor(
                s_in[:, 50:51], s_in[:, 49:50],
                route_sc[:, blk, pcol:pcol + 1], ALU.add), (dv, 1))
            V(lambda E, blk=blk, pcol=pcol: E.tensor_scalar(
                s_in[:, 51:52], route_sc[:, blk, pcol:pcol + 1], 384.5, None,
                ALU.is_ge), (dv, 1))
            Vw(dv)
            V(lambda E: E.scalar_tensor_tensor(
                s_in[:, 50:51], s_in[:, 51:52], 1.0e9, s_in[:, 50:51],
                ALU.mult, ALU.add), (dv, 1))
            Vw(dv)
            V(lambda E: E.tensor_scalar(s_in[:, 50:51], s_in[:, 50:51],
                                        3072.0, None, ALU.min), (dv, 1))
            Vw(dv)
            V(lambda E, blk=blk, itgt=itgt: E.tensor_copy(
                itgt[:, blk:blk + 1], s_in[:, 50:51]), (dv, 1))
    dv_route = dv.v
    Vw(dv)
    V(lambda E: E.memset(sq_scr[:], 0.0), (dv, 1))
    dv_zero = dv.v
    SYw(dv, dv_zero)
    SY(lambda E: E.dma_start(d_yperm[3072:3073, :], sq_scr[0:1, :]), (yz, 16))
    _groups = []
    _og, _ov, _os = p.ops["gpsimd"], p.ops["vector"], p.ops["sync"]
    k = 0
    for blk in range(NB):
        for itgt, pcol in ((idx1_i, 2), (idx2_i, 3)):
            sl = k % 2
            _ds, _dv, _dg = [], [], []
            p.ops["sync"] = _ds
            if k == 0:
                SYw(xpz, 16 * 25)
                SYw(xnsp, 16 * 8)
            if k >= 2:
                SYw(rsc, 16 * (k - 1))
            SY(lambda E, blk=blk, sl=sl: E.dma_start(
                stage[sl][:, 0:2048],
                d_xnsp[blk * 128:(blk + 1) * 128, :]), (sgl, 16))
            p.ops["sync"] = _os
            p.ops["vector"] = _dv
            p.wait("vector", sgl, 16 * (k + 1))
            p.op("vector", (lambda E, blk=blk, sl=sl, pcol=pcol:
                            E.tensor_copy(stage[sl][:, 2048:2049],
                                          route_sc[:, blk, pcol:pcol + 1])),
                 (swc, 1))
            p.ops["vector"] = _ov
            p.ops["gpsimd"] = _dg
            Gw(swc, k + 1)
            if k == 0:
                Gw(dv, dv_route)
                Gw(xpz, 16 * 25)
            G(lambda E, blk=blk, sl=sl, itgt=itgt: E.indirect_dma_start(
                d_xperm, IndirectOffsetOnAxis(itgt[:, blk:blk + 1], 0),
                stage[sl][:], None,
                bounds_check=3072, oob_is_err=False), (rsc, 16))
            p.ops["gpsimd"] = _og
            _groups.append((_ds, _dv, _dg))
            k += 1
    rsc_done = rsc.v

    _glag = []
    for h in range(NH):
        if h == 11:
            Gw(a_st, 16 * 8)
            G(lambda E: E.collective_compute(
                "AllGather", ALU.bypass,
                replica_groups=[[0, 1, 2, 3], [4, 5, 6, 7]],
                ins=[d_astage.ap()[0:8 * 128, :].opt()],
                outs=[d_aall1.ap().opt()]), (cc_s, 1))
        if h >= 1:
            while _glag:
                p.ops["gpsimd"].append(_glag.pop(0))
            for _ in range(2):
                if _groups:
                    gs, gv, gg = _groups.pop(0)
                    p.ops["sync"].extend(gs)
                    p.ops["vector"].extend(gv)
                    _glag.extend(gg)
        Aw(dv)
        A(lambda E, h=h: E.activation(qd_b[:], iota_bc[:], AF.Exp,
                                      scale=negs[:, h:h + 1]), (ac, 1))
        A(lambda E, h=h: E.activation(stat[:, 7:8], cols[:, 0:1], AF.Exp,
                                      scale=negs[:, h:h + 1]), (ac, 1))
        A(lambda E, h=h: E.activation(s_in[:], diffx[:], AF.Exp,
                                      scale=negs[:, h:h + 1]), (ac, 1))
        ac_tab = ac.v

        for j, slabi in ((0, 16 + h), (1, 32 + h)):
            slot, wv = st_wq.load([lambda E, b, s=slabi: E.dma_start(
                b[:], d_wqkv[s])])
            wslab = st_wq.buf(slot)
            dst = k_fmB if j == 0 else v_fmB
            for ic in range(2):
                bank = pf[ic]
                Tw(st_wq.slots[slot], wv)
                if h == 0 and j == 0:
                    Tw(dv, dv_ln_half if ic == 0 else dv_ln_done)
                Tw(bank.rd)
                for dd in range(16):
                    T(lambda E, dd=dd, ic=ic, bank=bank, wslab=wslab: E.matmul(
                        bank.ap[:], wslab[:, dd, :],
                        xn_fm[:, dd, ic * 512:(ic + 1) * 512],
                        start=(dd == 0), stop=(dd == 15)),
                      *(((bank.pe, 1),) if dd == 15 else ()))
                Vw(bank.pe)
                incs = [(bank.rd, 1), (dv, 1)]
                if ic == 1:
                    incs.append((st_wq.free, 1))
                V(lambda E, dst=dst, ic=ic, bank=bank, si=slabi: E.tensor_scalar(
                    dst[:, ic * 512:(ic + 1) * 512], bank.ap[:],
                    bq_c[:, si:si + 1], None, ALU.add), *incs)

        Tw(dv)
        for j in range(2):
            bank = pbf[j]
            src = k_fmB if j == 0 else v_fmB
            Tw(bank.rd)
            for blk in range(NB):
                T(lambda E, blk=blk, src=src, bank=bank: E.transpose(
                    bank.ap[:, blk * 128:(blk + 1) * 128],
                    src[:, blk * 128:(blk + 1) * 128], ident_b[:]),
                  *(((bank.pe, 1),) if blk == 7 else ()))
        Vw(ac, ac_tab)
        Vw(pbf[0].pe)
        V(lambda E: E.tensor_scalar(
            k2_tm.ap().rearrange("p b m -> p (b m)"), pbf[0].ap[:],
            stat[:, 7:8], None, ALU.mult), (pbf[0].rd, 1), (dv, 1))
        Vw(pbf[1].pe)
        V(lambda E: E.tensor_copy(v_tm.ap().rearrange("p b m -> p (b m)"),
                                  pbf[1].ap[:]), (pbf[1].rd, 1), (dv, 1))
        dv_kv = dv.v

        def emit_scan(h=h, dv_kv=dv_kv):
            Vw(a_st, a_st.v)
            V(lambda E: E.memset(s_f32[:], 0.0), (dv, 1))
            V(lambda E: E.memset(s_bf_all.ap().rearrange("p b m -> p (b m)"),
                                 0.0), (dv, 1))
            Tw(dv)
            for i in range(NB):
                bank = pf[4 + (i % 2)]
                Tw(bank.rd)
                T(lambda E, i=i, bank=bank: E.matmul(
                    bank.ap[:, 0:128], k2_tm[:, i, :], v_tm[:, i, :],
                    start=True, stop=True), (bank.pe, 1))
                if i > 0:
                    V(lambda E, i=i: E.tensor_copy(s_bf_all[:, i, :],
                                                   s_f32[:]), (dv, 1))
                Vw(bank.pe)
                V(lambda E, h=h, bank=bank: E.scalar_tensor_tensor(
                    s_f32[:], s_f32[:], d_all[:, h:h + 1], bank.ap[:, 0:128],
                    ALU.mult, ALU.add), (bank.rd, 1), (dv, 1))
            SYw(dv)
            SY(lambda E, h=h: E.dma_start(
                d_astage.ap()[h * 128:(h + 1) * 128, :], s_f32[:]),
               (a_st, 16))

        def emit_qproj(h=h):
            slot, wv = st_wq.load([lambda E, b, s=h: E.dma_start(
                b[:], d_wqkv[s])])
            wslab = st_wq.buf(slot)
            for ic in range(2):
                bank = pf[ic]
                Tw(st_wq.slots[slot], wv)
                Tw(bank.rd)
                for dd in range(16):
                    T(lambda E, dd=dd, ic=ic, bank=bank, wslab=wslab:
                      E.matmul(
                          bank.ap[:], wslab[:, dd, :],
                          xn_fm[:, dd, ic * 512:(ic + 1) * 512],
                          start=(dd == 0), stop=(dd == 15)),
                      *(((bank.pe, 1),) if dd == 15 else ()))
                Vw(bank.pe)
                incs = [(bank.rd, 1), (dv, 1)]
                if ic == 1:
                    incs.append((st_wq.free, 1))
                V(lambda E, ic=ic, bank=bank, h=h: E.tensor_scalar(
                    q_fmB[:, ic * 512:(ic + 1) * 512], bank.ap[:],
                    bq_c[:, h:h + 1], None, ALU.add), *incs)
            return dv.v

        def emit_q2(h=h):
            Vw(q_st, q_st.v)
            Vw(ac, ac_tab)
            Vw(dv)
            for blk in range(NB):
                V(lambda E, blk=blk: E.tensor_tensor(
                    q2_fm[:, blk * 128:(blk + 1) * 128],
                    q_fmB[:, blk * 128:(blk + 1) * 128], qd_b[:], ALU.mult),
                  (dv, 1))
            SYw(dv)
            SY(lambda E, h=h: E.dma_start(d_qspill[h], q2_fm[:]), (q_st, 16))

        if h == NH - 1:
            emit_scan()
            dv_qevac = emit_qproj()
            emit_q2()
        else:
            dv_qevac = emit_qproj()
            emit_scan()
            emit_q2()

        def emit_scoresT(i):
            sc_bank = pf[2 + (i % 2)]
            Tw(sc_bank.rd)
            T(lambda E, i=i, bank=sc_bank: E.matmul(
                bank.ap[:, 0:128], k_fmB[:, i * 128:(i + 1) * 128],
                q_fmB[:, i * 128:(i + 1) * 128], start=True, stop=True),
              (sc_bank.pe, 1))

        Tw(dv, dv_qevac)
        emit_scoresT(0)
        emit_scoresT(1)
        for i in range(NB):
            sc_bank = pf[2 + (i % 2)]
            o_bank = pf[4 + (i % 2)]
            Vw(sc_bank.pe)
            V(lambda E, i=i, bank=sc_bank: E.tensor_tensor(
                smb[i % 3][:], bank.ap[:, 0:128], s_in[:], ALU.mult),
              (sc_bank.rd, 1), (dv, 1))
            Tw(dv)
            Tw(o_bank.rd)
            T(lambda E, i=i, bank=o_bank: E.matmul(
                bank.ap[:, 0:128], s_bf_all[:, i, :],
                q2_fm[:, i * 128:(i + 1) * 128], start=True, stop=False))
            T(lambda E, i=i, bank=o_bank: E.matmul(
                bank.ap[:, 0:128], v_tm[:, i, :], smb[i % 3][:],
                start=False, stop=True), (o_bank.pe, 1))
            if i + 2 < NB:
                emit_scoresT(i + 2)
            Vw(o_bank.pe)
            V(lambda E, i=i, h=h, bank=o_bank: E.tensor_copy(
                o_fm[:, h, i * 128:(i + 1) * 128], bank.ap[:, 0:128]),
              (o_bank.rd, 1), (dv, 1))
    while _groups:
        gs, gv, gg = _groups.pop(0)
        p.ops["sync"].extend(gs)
        p.ops["vector"].extend(gv)
        _glag.extend(gg)
    while _glag:
        p.ops["gpsimd"].append(_glag.pop(0))
    dv_pass1_done = dv.v

    if "o1" in taps:
        d_do1 = nc.dram_tensor("dbg_o1", [128, 16 * 1024], BF16,
                               kind="ExternalOutput").ap()
        tsem = Sem(nc, "tap_o1")
        SYw(dv, dv_pass1_done)
        SY(lambda E: E.dma_start(
            d_do1, o_fm.ap().rearrange("p c m -> p (c m)")), (tsem, 16))
        p.wait("vector", tsem, 16)

    Gw(a_st, 16 * 16)
    G(lambda E: E.collective_compute(
        "AllGather", ALU.bypass,
        replica_groups=[[0, 1, 2, 3], [4, 5, 6, 7]],
        ins=[d_astage.ap()[8 * 128:16 * 128, :].opt()],
        outs=[d_aall2.ap().opt()]), (cc_s, 1))

    aall_v1 = d_aall1.ap().rearrange("(r h p) m -> r h p m", r=4, h=8)
    aall_v2 = d_aall2.ap().rearrange("(r h p) m -> r h p m", r=4, h=8)
    for h in range(NH):
        av = aall_v1 if h < 8 else aall_v2
        hh_i = h % 8
        gates = ()
        if h == 0:
            gates = ((cc_s, 1),)
        elif h == 8:
            gates = ((cc_s, 2),)
        slot, wv = st_a4.load(
            [lambda E, b, av=av, hh_i=hh_i: E.dma_start(
                b[:], av[:, hh_i].rearrange("r p m -> p r m"))], gates=gates)
        slotq, wvq = st_qr.load(
            [lambda E, b, h=h: E.dma_start(b[:], d_qspill[h])],
            gates=((q_st, q_st.v),) if h == 0 else ())
        a4b = st_a4.buf(slot)
        Vw(st_a4.slots[slot], wv)
        V(lambda E, a4b=a4b, h=h: E.tensor_scalar(
            s_in[:], a4b[:, 0, :], cfr[:, 0, h:h + 1], None, ALU.mult), (dv, 1))
        for r in range(1, 4):
            incs = [(dv, 1)]
            if r == 3:
                incs.append((st_a4.free, 1))
            V(lambda E, a4b=a4b, h=h, r=r: E.scalar_tensor_tensor(
                s_in[:], a4b[:, r, :], cfr[:, r, h:h + 1], s_in[:],
                ALU.mult, ALU.add), *incs)
        Aw(dv)
        A(lambda E: E.copy(s_in_b[:], s_in[:]), (ac, 1))
        A(lambda E, h=h: E.activation(dpa[:], blkbc[:], AF.Exp,
                                      scale=negs[:, h:h + 1]), (ac, 1))
        qreb = st_qr.buf(slotq)
        for ic in range(2):
            bank = pf[ic]
            Tw(ac)
            Tw(st_qr.slots[slotq], wvq)
            Tw(bank.rd)
            T(lambda E, ic=ic, bank=bank, qreb=qreb: E.matmul(
                bank.ap[:], s_in_b[:], qreb[:, ic * 512:(ic + 1) * 512],
                start=True, stop=True),
              (bank.pe, 1), *(((st_qr.free, 1),) if ic == 1 else ()))
            Vw(bank.pe)
            Vw(ac)
            for bi in range(4):
                i = ic * 4 + bi
                incs = [(dv, 1)]
                if bi == 3:
                    incs.append((bank.rd, 1))
                V(lambda E, i=i, bi=bi, h=h, bank=bank: E.scalar_tensor_tensor(
                    o_fm[:, h, i * 128:(i + 1) * 128],
                    bank.ap[:, bi * 128:(bi + 1) * 128], dpa[:, i:i + 1],
                    o_fm[:, h, i * 128:(i + 1) * 128], ALU.mult, ALU.add),
                  *incs)
    dv_pass2_done = dv.v

    if "o2" in taps:
        d_do2 = nc.dram_tensor("dbg_o2", [128, 16 * 1024], BF16,
                               kind="ExternalOutput").ap()
        tsem = Sem(nc, "tap_o2")
        SYw(dv, dv_pass2_done)
        SY(lambda E: E.dma_start(
            d_do2, o_fm.ap().rearrange("p c m -> p (c m)")), (tsem, 16))
        p.wait("vector", tsem, 16)

    for fc in range(4):
        dmas = [lambda E, b, fc=fc: E.dma_start(b[:], d_wout[fc])]
        gates = ((dv, dv_ln_done),) if fc == 0 else ()
        slot, wv = st_wo.load(dmas, gates=gates)
        wob = st_wo.buf(slot)
        Tw(st_wo.slots[slot], wv)
        if fc == 0:
            Tw(dv, dv_pass2_done)
        for blkset in ((0, 1, 2, 3, 4, 5), (6, 7)):
            for bi in range(len(blkset)):
                Tw(pf[bi].rd)
            for c in range(16):
                for bi, blk in enumerate(blkset):
                    bank = pf[bi]
                    T(lambda E, c=c, blk=blk, bank=bank, wob=wob: E.matmul(
                        bank.ap[:], o_fm[:, c, blk * 128:(blk + 1) * 128],
                        wob[:, c, :], start=(c == 0), stop=(c == 15)),
                      *(((bank.pe, 1),) if c == 15 else ()))
            for bi, blk in enumerate(blkset):
                bank = pf[bi]
                Vw(bank.pe)
                incs = [(bank.rd, 1), (dv, 1)]
                if blk == 7:
                    incs.append((st_wo.free, 1))
                V(lambda E, blk=blk, fc=fc, bank=bank: E.tensor_tensor(
                    out_tm[:, blk, fc * 512:(fc + 1) * 512], bank.ap[:],
                    bout_bc[:, fc * 512:(fc + 1) * 512], ALU.add), *incs)
    dv_att_done = dv.v

    if "att" in taps:
        d_datt = nc.dram_tensor("dbg_att", [1024, 2048], F32,
                                kind="ExternalOutput").ap()
        tsem = Sem(nc, "tap_att")
        SYw(dv, dv_att_done)
        SY(lambda E: E.dma_start(
            d_datt.rearrange("(b p) f -> p b f", p=128),
            out_tm[:]), (tsem, 16))
        p.wait("vector", tsem, 16)

    st_xg = Stream(nc, p, "stxg", "sync", xn_g)
    ybu = [0] * 4
    dv_xg_parity = [0, 0]

    def emit_gather_loads(e):
        par = e % 2
        toks = []
        for blk in range(3):
            gates = ()
            if e == 0 and blk == 0:
                gates = ((rsc, rsc_done), (dv, dv_pass1_done))
            r0 = e * 384 + blk * 128
            sx, wvx = st_xg.load(
                [lambda E, b, r0=r0: E.dma_start(
                    b[:], d_xperm[r0:r0 + 128, :])], gates=gates)
            toks.append((sx, wvx))
        return toks

    def emit_gather_transp(e, toks):
        par = e % 2
        xg = xn_ge2[par]
        for blk in range(3):
            sx, wvx = toks[blk]
            xgb = st_xg.buf(sx)
            Vw(st_xg.slots[sx], wvx)
            V(lambda E, xgb=xgb, par=par, blk=blk: E.tensor_copy(
                w_g[par][:, blk:blk + 1], xgb[:, 2048:2049]), (dv, 1))
            for half in range(2):
                bank = pbf[half]
                Tw(st_xg.slots[sx], wvx)
                if e == 0 and blk == 0 and half == 0:
                    Tw(dv, dv_att_done)
                Tw(bank.rd)
                for c8 in range(8):
                    ch = half * 8 + c8
                    T(lambda E, ch=ch, c8=c8, bank=bank, xgb=xgb: E.transpose(
                        bank.ap[:, c8 * 128:(c8 + 1) * 128],
                        xgb[:, ch * 128:(ch + 1) * 128], ident_b[:]),
                      *(((bank.pe, 1),) if c8 == 7 else ()))
                Vw(bank.pe)
                incs = [(bank.rd, 1), (dv, 1)]
                if half == 1:
                    incs.append((st_xg.free, 1))
                V(lambda E, half=half, blk=blk, bank=bank, xg=xg: E.tensor_copy(
                    xg[:, half * 8:half * 8 + 8, blk * 128:(blk + 1) * 128],
                    bank.ap.rearrange("p (c m) -> p c m", c=8)), *incs)
        dv_xg_parity[par] = dv.v

    toks_cur = emit_gather_loads(0)
    emit_gather_transp(0, toks_cur)
    for e in range(NE):
        if e + 1 < NE:
            toks_nxt = emit_gather_loads(e + 1)
        par = e % 2
        xg = xn_ge2[par]
        dv_xg = dv_xg_parity[par]

        for t in range(NT):
            g0 = ((q_st, q_st.v),) if (e == 0 and t == 0) else ()
            if e == 0 and t == 2:
                g0 = ((dv, dv_att_done),)
            s1, wv1 = st_w12.load(
                [lambda E, b, e=e, t=t: E.dma_start(
                    b[:], d_w12[e, t])],
                gates=g0)
            b1 = pf[0] if t % 2 == 0 else pf[2]
            b2 = pf[1] if t % 2 == 0 else pf[3]
            wb = st_w12.buf(s1)
            for mi, bank in ((0, b1), (1, b2)):
                Tw(st_w12.slots[s1], wv1)
                if t == 0:
                    Tw(dv, dv_xg)
                Tw(bank.rd)
                for dd in range(16):
                    T(lambda E, dd=dd, mi=mi, bank=bank, wb=wb, xg=xg:
                      E.matmul(
                          bank.ap[:, 0:384], wb[:, mi, dd, :],
                          xg[:, dd, 0:384],
                          start=(dd == 0), stop=(dd == 15)),
                      *(((bank.pe, 1),) if dd == 15 else ()))
            Aw(b1.pe)
            A(lambda E, b1=b1: E.activation(
                sil[:], b1.ap[:, 0:384], AF.Silu), (ac, 1), (b1.rd, 1))
            Vw(ac)
            Vw(b2.pe)
            V(lambda E, t=t, b2=b2: E.tensor_tensor(
                hh[:, t, :], sil[:], b2.ap[:, 0:384], ALU.mult),
              (b2.rd, 1), (dv, 1), (st_w12.free, 1))
        dv_h_done = dv.v

        if e + 1 < NE:
            emit_gather_transp(e + 1, toks_nxt)

        for fc in range(4):
            dmas = [lambda E, b, e=e, fc=fc: E.dma_start(
                b[:], d_w3[e, fc])]
            g3 = ((dv, dv_att_done), (rsc, rsc_done)) \
                if (e == 0 and fc == 0) else ()
            sl3, wv3 = st_w3.load(dmas, gates=g3)
            w3buf = st_w3.buf(sl3)
            Tw(st_w3.slots[sl3], wv3)
            if fc == 0:
                Tw(dv, dv_h_done)
            for blk in range(3):
                Tw(pf[blk].rd)
            for t in range(NT):
                for blk in range(3):
                    bank = pf[blk]
                    T(lambda E, t=t, blk=blk, bank=bank, w3buf=w3buf:
                      E.matmul(
                          bank.ap[:], hh[:, t, blk * 128:(blk + 1) * 128],
                          w3buf[:, t, :], start=(t == 0), stop=(t == 10)),
                      *(((bank.pe, 1),) if t == 10 else ()))
            for blk in range(3):
                bank = pf[blk]
                yb = ybounce[ybu[0] % 3]
                Vw(bank.pe)
                if ybu[0] >= 3:
                    Vw(yst, 16 * (ybu[0] - 2))
                incs = [(bank.rd, 1), (dv, 1)]
                if blk == 2:
                    incs.append((st_w3.free, 1))
                V(lambda E, blk=blk, par=par, bank=bank, yb=yb:
                  E.tensor_scalar(
                      yb[:], bank.ap[:],
                      w_g[par][:, blk:blk + 1], None, ALU.mult), *incs)
                r0 = e * 384 + blk * 128
                SYw(dv)
                SY(lambda E, r0=r0, fc=fc, yb=yb: E.dma_start(
                    d_yperm[r0:r0 + 128, fc * 512:(fc + 1) * 512], yb[:]),
                   (yst, 16))
                ybu[0] += 1
    dv_ffn_done = dv.v
    yst_done = yst.v

    for blk in range(NB):
        if blk == 0:
            Gw(yst, yst_done)
            Gw(yz, 16)
        else:
            Gw(dv, dv.v)
        for j, itgt in ((0, idx1_i), (1, idx2_i)):
            G(lambda E, blk=blk, j=j, itgt=itgt: E.indirect_dma_start(
                yg[j][:], None, d_yperm,
                IndirectOffsetOnAxis(itgt[:, blk:blk + 1], 0),
                bounds_check=3072, oob_is_err=False), (ygl, 16))
        gates = ((dv, dv_ffn_done),) if blk == 0 else ()
        slot, wv = st_x.load([lambda E, b, blk=blk: E.dma_start(
            b[:], d_x[blk * 128:(blk + 1) * 128, :])], gates=gates)
        xb = st_x.buf(slot)
        Vw(ygl, ygl.v)
        V(lambda E, blk=blk: E.tensor_tensor(
            out_tm[:, blk, :], out_tm[:, blk, :], yg[0][:], ALU.add), (dv, 1))
        Vw(dv)
        V(lambda E, blk=blk: E.tensor_tensor(
            out_tm[:, blk, :], out_tm[:, blk, :], yg[1][:], ALU.add), (dv, 1))
        Vw(st_x.slots[slot], wv)
        Vw(dv)
        V(lambda E, blk=blk, xb=xb: E.tensor_tensor(
            out_tm[:, blk, :], out_tm[:, blk, :], xb[:], ALU.add),
          (st_x.free, 1), (dv, 1))
        SYw(dv)
        SY(lambda E, blk=blk: E.dma_start(
            d_out[blk * 128:(blk + 1) * 128, :], out_tm[:, blk, :]),
           (o_st, 16))
    if "route" in taps:
        d_r1 = nc.dram_tensor("dbg_idx1", [128, 8], I32, kind="ExternalOutput").ap()
        d_r2 = nc.dram_tensor("dbg_idx2", [128, 8], I32, kind="ExternalOutput").ap()
        d_rs = nc.dram_tensor("dbg_rsc", [128, 64], F32, kind="ExternalOutput").ap()
        d_xp = nc.dram_tensor("dbg_xperm", [768, 2049], BF16, kind="ExternalOutput").ap()
        d_yp = nc.dram_tensor("dbg_yperm", [768, 2048], F32, kind="ExternalOutput").ap()
        SY(lambda E: E.dma_start(d_r1, idx1_i[:]), (o_st, 16))
        SY(lambda E: E.dma_start(d_r2, idx2_i[:]), (o_st, 16))
        SY(lambda E: E.dma_start(
            d_rs, route_sc.ap().rearrange("p a b -> p (a b)")), (o_st, 16))
        SY(lambda E: E.dma_start(d_xp, d_xperm[0:768, 0:2049]), (o_st, 16))
        SY(lambda E: E.dma_start(d_yp, d_yperm[0:768, :]), (o_st, 16))
    if "xn" in taps:
        d_dxn = nc.dram_tensor("dbg_xn", [128, 16 * 1024], BF16,
                               kind="ExternalOutput").ap()
        SY(lambda E: E.dma_start(
            d_dxn, xn_fm.ap().rearrange("p c m -> p (c m)")), (o_st, 16))
    if "wtm" in taps:
        d_dwt = nc.dram_tensor("dbg_wtm", [128, 64], F32,
                               kind="ExternalOutput").ap()
        SY(lambda E: E.dma_start(
            d_dwt, w_tm.ap().rearrange("p b e -> p (b e)")), (o_st, 16))
    SYw(o_st, o_st.v)

    with nc.Block() as block:
        @block.tensor
        def _(E):
            for f in p.ops["tensor"]:
                f(E)

        @block.vector
        def _(E):
            for f in p.ops["vector"]:
                f(E)

        @block.scalar
        def _(E):
            for f in p.ops["scalar"]:
                f(E)

        @block.gpsimd
        def _(E):
            for f in p.ops["gpsimd"]:
                f(E)

        @block.sync
        def _(E):
            for f in p.ops["sync"]:
                f(E)
    return nc


def _host_inputs(ins):
    f32 = np.float32
    x = np.ascontiguousarray(np.asarray(ins["L0_attn_x"], f32).reshape(8192, 2048))
    Wqkv = np.asarray(ins["L0_attn_Wqkv"], f32)
    wqkv_s = np.ascontiguousarray(Wqkv.reshape(16, 128, 48, 128)
                                  .transpose(2, 1, 0, 3))
    bqkv_c = np.ascontiguousarray(
        np.asarray(ins["L0_attn_bqkv"], f32).reshape(48, 128).T)
    wout = np.ascontiguousarray(np.asarray(ins["L0_attn_Wout"], f32)
                                .reshape(16, 128, 4, 512)
                                .transpose(2, 1, 0, 3))
    bout_bc = np.ascontiguousarray(
        np.broadcast_to(np.asarray(ins["L0_attn_bout"], f32), (128, 2048)))
    slopes_bc = np.ascontiguousarray(
        np.broadcast_to(np.asarray(ins["L0_attn_slopes"], f32), (128, 16)))
    wgt = np.ascontiguousarray(np.asarray(ins["L0_ffn_Wg"], f32).T)
    W1 = np.asarray(ins["L0_ffn_W1"], f32)
    W2 = np.asarray(ins["L0_ffn_W2"], f32)
    W3 = np.asarray(ins["L0_ffn_W3"], f32)
    w1s = W1.transpose(0, 2, 1).reshape(8, 2048, 11, 128).transpose(0, 2, 1, 3)
    w2s = W2.transpose(0, 2, 1).reshape(8, 2048, 11, 128).transpose(0, 2, 1, 3)
    w12s = np.stack([w1s, w2s], axis=2)
    w12s = np.ascontiguousarray(w12s.reshape(8, 11, 2, 16, 128, 128)
                                .transpose(0, 1, 4, 2, 3, 5))
    w3t = np.ascontiguousarray(W3.transpose(0, 2, 1)
                               .reshape(8, 11, 128, 4, 512)
                               .transpose(0, 3, 2, 1, 4))

    ii = np.arange(128, dtype=f32)
    diffm = ii[None, :] - ii[:, None]
    diffx = np.where(diffm >= 0, diffm, BIG).astype(f32)
    iota_bc = np.ascontiguousarray(np.broadcast_to(ii, (128, 128)))
    cols = np.ascontiguousarray(
        np.stack([128.0 - ii, np.full(128, 128.0, f32)], axis=1).astype(f32))
    ident = np.eye(128, dtype=f32)
    blk_bc = np.ascontiguousarray(
        np.broadcast_to(128.0 * np.arange(8, dtype=f32), (128, 8)))
    iota8_bc = np.ascontiguousarray(
        np.broadcast_to(np.arange(8, dtype=f32), (128, 8)))

    shared = dict(wqkv_s=wqkv_s, bqkv_c=bqkv_c, wout=wout, bout_bc=bout_bc,
                  slopes_bc=slopes_bc, wgt=wgt, w12s=w12s, w3t=w3t,
                  diffx=diffx, iota_bc=iota_bc, cols=cols, ident=ident,
                  blk_bc=blk_bc, iota8_bc=iota8_bc)
    in_maps = []
    for core in range(8):
        q = core % 4
        tm = np.full((128, 4), -BIG, f32)
        for j in range(q):
            tm[:, j] = -1024.0 * (q - 1 - j)
        m = dict(shared)
        m["xs"] = np.ascontiguousarray(x[core * 1024:(core + 1) * 1024])
        m["neg_t_meta"] = tm
        in_maps.append(m)
    return in_maps


def kernel(**ins):
    in_maps = _host_inputs(ins)
    nc = build_nc()
    res = run_bass_kernel_spmd(nc, in_maps, core_ids=list(range(8)))
    out = np.concatenate([np.asarray(res.results[c]["out"]) for c in range(8)],
                         axis=0)
    return out.reshape(2, 4096, 2048).astype(np.float32)
